# version 11
# baseline (speedup 1.0000x reference)
import contextlib
from collections import defaultdict

import numpy as np
import ml_dtypes

import concourse.bass as bass
import concourse.mybir as mybir
from concourse.bass_utils import run_bass_kernel_spmd

F32 = mybir.dt.float32
BF16 = mybir.dt.bfloat16
I32 = mybir.dt.int32
AX = mybir.AxisListType.X
OP = mybir.AluOpType
AF = mybir.ActivationFunctionType

f32 = np.float32

B, T, C = 16, 2048, 1024
NCORE = 8
SEQ_PER_CORE = B // NCORE
H, D = 16, 64
INNER = 4096
NT = T // 128
NC = C // 128
NI = INNER // 128
EXP_SHIFT = -4.0


class Prog:

    def __init__(self, nc):
        self.nc = nc
        self.ops = []
        self.cnt = defaultdict(int)
        self._sem_handles = {}

    def sem(self, name):
        if name not in self._sem_handles:
            self._sem_handles[name] = self.nc.alloc_semaphore(name=name)
        return self._sem_handles[name]

    def now(self, sem):
        return self.cnt[sem]

    def emit(self, eng, fn, waits=(), inc=None, dma=False):
        incs = []
        if inc is not None:
            n = 16 if dma else 1
            self.cnt[inc] += n
            incs.append((inc, n))
        self.ops.append((eng, fn, tuple((s, v) for s, v in waits if v > 0),
                         tuple(incs)))

    def build(self, block):
        def make_runner(tag):
            def run(eng):
                watermark = defaultdict(int)
                for e, fn, waits, incs in self.ops:
                    if e != tag:
                        continue
                    for s, v in waits:
                        if v > watermark[s]:
                            eng.wait_ge(self.sem(s), v)
                            watermark[s] = v
                    ins = fn(eng)
                    for s, n in incs:
                        ins = ins.then_inc(self.sem(s), n)
            return run

        block.tensor(make_runner("T"))
        block.scalar(make_runner("S"))
        block.vector(make_runner("V"))
        block.gpsimd(make_runner("G"))
        block.sync(make_runner("Y"))


def ceil_div(a, b):
    return (a + b - 1) // b


def build_program(taps=(), debug_wq=False, preproc_only=False, drc=True, pad_nops=0, pad_dmas=0, nseq=SEQ_PER_CORE, nheads=H, do_ffn=True, skip_store=False, dlim=4):
    taps = set(taps)
    nc = bass.Bass("TRN2", target_bir_lowering=False, detect_race_conditions=drc)

    x_d = nc.dram_tensor("x", [SEQ_PER_CORE, T, C], F32, kind="ExternalInput").ap()
    y_d = nc.dram_tensor("y", [SEQ_PER_CORE, T, C], F32, kind="ExternalOutput").ap()
    wqkv_d = nc.dram_tensor("w_qkv", [C, 3 * C], F32, kind="ExternalInput").ap()
    bqkv_d = nc.dram_tensor("b_qkv", [3 * C], F32, kind="ExternalInput").ap()
    wproj_d = nc.dram_tensor("w_proj", [C, C], F32, kind="ExternalInput").ap()
    bproj_d = nc.dram_tensor("b_proj", [C], F32, kind="ExternalInput").ap()
    wg_d = nc.dram_tensor("w_gate", [INNER, C], F32, kind="ExternalInput").ap()
    wu_d = nc.dram_tensor("w_up", [INNER, C], F32, kind="ExternalInput").ap()
    wd_d = nc.dram_tensor("w_down", [C, INNER], F32, kind="ExternalInput").ap()

    wk = "ExternalInput" if debug_wq else (
        "ExternalOutput" if "wq" in taps else "Internal")
    wgT_d = nc.dram_tensor("wgT", [8, NC, 128, 512], BF16, kind=wk).ap()
    wuT_d = nc.dram_tensor("wuT", [8, NC, 128, 512], BF16, kind=wk).ap()
    wdT_d = nc.dram_tensor("wdT", [2, NI, 128, 512], BF16, kind=wk).ap()

    tap_d = {}
    def decl_tap(name, shape, dt):
        if name in taps:
            tap_d[name] = nc.dram_tensor("tap_" + name, shape, dt,
                                         kind="ExternalOutput").ap()
    decl_tap("x0", [128, NT, C], F32)
    decl_tap("xn", [128, NC, T], BF16)
    decl_tap("qf", [128, NC, T], BF16)
    decl_tap("kf", [128, NC, T], BF16)
    decl_tap("vtm", [128, NT, 16 * 65], BF16)
    decl_tap("of", [128, NC, T], BF16)
    decl_tap("res1", [128, NT, C], F32)
    decl_tap("xqf", [2, 128, NC, 1024], BF16)
    decl_tap("gu", [2, 128, 8, INNER], BF16)
    decl_tap("gud3", [2, 128, 8, INNER], BF16)
    decl_tap("wsc", [128, 8], F32)
    decl_tap("wgb0", [128, NC, 512], BF16)
    decl_tap("wqrb", [NC, 128, 512], BF16)
    decl_tap("wqrb2", [NC, 128, 512], BF16)
    decl_tap("mark", [128, 128], BF16)
    decl_tap("stg", [128, 1024], BF16)
    decl_tap("sca", [128, 1024], F32)
    decl_tap("scv", [128, 1024], F32)
    decl_tap("colsum", [128, 40], F32)

    ident_np = np.eye(128, dtype=ml_dtypes.bfloat16)
    mask_np = np.tril(np.full((128, 128), -1e9, dtype=np.float32), -1)
    ident_d = nc.inline_tensor(ident_np, name="identc").ap()
    mask_d = nc.inline_tensor(mask_np, name="maskc").ap()

    ctx = contextlib.ExitStack()
    E = ctx.enter_context

    with ctx:
        x_sb = E(nc.sbuf_tensor("x_sb", [128, NT, C], F32))
        ident = E(nc.sbuf_tensor("ident", [128, 128], BF16))
        mask = E(nc.sbuf_tensor("mask", [128, 128], F32))
        ones_row = E(nc.sbuf_tensor("ones_row", [1, 128], F32))
        ones_col = E(nc.sbuf_tensor("ones_col", [128, 1], F32))
        bqk = E(nc.sbuf_tensor("bqk", [128, 24], F32))
        diag = E(nc.sbuf_tensor("diag", [128, 128], BF16))
        small = E(nc.sbuf_tensor("small", [128, 16], F32))
        invsx = E(nc.sbuf_tensor("invsx", [128, 8], F32))
        rsumc = E(nc.sbuf_tensor("rsumc", [128, 4], F32))
        wsc = E(nc.sbuf_tensor("wsc", [128, 8], F32))
        sc11 = E(nc.sbuf_tensor("sc11", [1, 8], F32))

        psA = E(nc.psum_tensor("psA", [128, 2048], F32))
        psB = E(nc.psum_tensor("psB", [128, 2048], F32))
        psA_bf = psA[:].bitcast(BF16)
        psB_bf = psB[:].bitcast(BF16)

        p = Prog(nc)
        ST, SS, SV, SG = "sT", "sS", "sV", "sG"
        DY, DG, DO = "dY", "dG", "dO"
        DC = "dC"

        guards = {"rcclip": 0, "wstr": [0, 0], "wstrb": [0, 0], "wgu": [0, 0],
                  "wdb": [0, 0], "wps": [0, 0]}

        def t_mm(out_ap, lhsT_ap, rhs_ap, start, stop, waits=(), inc=None,
                 tile_position=None):
            p.emit("T", lambda e, o=out_ap, l=lhsT_ap, r=rhs_ap, st=start,
                   sp=stop, tp=tile_position:
                   e.matmul(o, l, r, start=st, stop=sp, tile_position=tp),
                   waits=waits, inc=inc)

        rcg = {0: 0, 1: 0}

        def round_chain(scr, src_ap, scale_ap, dst_ap, n, waits_first=(),
                        slot=None, off=0):
            scrV, scrI, scrF, scrR = scr
            guard = p.now(SV) if slot is None else rcg[slot]
            wl = list(waits_first) + [("sV", guard)]
            o1 = off + n
            p.emit("S", lambda e, off=off, o1=o1: e.activation(
                scrV[:, off:o1], src_ap, AF.Identity,
                scale=scale_ap, bias=f32(0.0)), waits=wl)
            p.emit("S", lambda e, off=off, o1=o1: e.copy(scrI[:, off:o1],
                                                         scrV[:, off:o1]))
            p.emit("S", lambda e, off=off, o1=o1: e.copy(scrR[:, off:o1],
                                                         scrI[:, off:o1]),
                   inc=SS)
            sv = p.now(SS)
            p.emit("V", lambda e, off=off, o1=o1: e.tensor_tensor(
                scrV[:, off:o1], scrV[:, off:o1], scrR[:, off:o1],
                OP.subtract), waits=[(SS, sv)])
            p.emit("V", lambda e, off=off, o1=o1: e.tensor_scalar(
                scrF[:, off:o1], scrV[:, off:o1], -1.0, 0.5,
                OP.mult, OP.is_ge))
            p.emit("V", lambda e, off=off, o1=o1: e.scalar_tensor_tensor(
                scrR[:, off:o1], scrV[:, off:o1], 0.5, scrR[:, off:o1],
                OP.is_ge, OP.add))
            p.emit("V", lambda e, d=dst_ap, off=off, o1=o1: e.tensor_tensor(
                d, scrR[:, off:o1], scrF[:, off:o1], OP.subtract), inc=SV)
            if slot is not None:
                rcg[slot] = p.now(SV)

        p.emit("Y", lambda e: e.dma_start(out=ident[:], in_=ident_d), inc=DC, dma=True)
        p.emit("Y", lambda e: e.dma_start(out=mask[:], in_=mask_d), inc=DC, dma=True)
        p.emit("Y", lambda e: e.dma_start(
            out=bqk[:], in_=bqkv_d.rearrange("(f p) -> p f", p=128),
            allow_slow_non_contiguous=True), inc=DC, dma=True)
        CONST_DY = p.now(DC)
        if "mark" in taps:
            p.emit("V", lambda e: e.memset(diag[:], 7.0), inc=SV)
            mv = p.now(SV)
            p.emit("Y", lambda e: e.dma_start(out=wgT_d[7, 7, :, 0:128],
                                              in_=diag[:]),
                   waits=[(SV, mv)], inc=DC, dma=True)
        p.emit("V", lambda e: e.memset(ones_row[:], 1.0))
        p.emit("V", lambda e: e.memset(ones_col[:], 1.0), inc=SV)
        ONES_V = p.now(SV)

        with nc.sbuf_tensor("wbuf", [128, 2, 4096], F32) as wbuf, \
             nc.sbuf_tensor("colsum", [128, 40], F32) as colsum, \
             nc.sbuf_tensor("scrV0", [128, 4096], F32) as scrV, \
             nc.sbuf_tensor("scrI0", [128, 4096], I32) as scrI, \
             nc.sbuf_tensor("scrR0", [128, 4096], F32) as scrR, \
             nc.sbuf_tensor("scrA0", [128, 4096], F32) as scrA, \
             nc.sbuf_tensor("stag", [128, 4096], BF16) as stag, \
             nc.sbuf_tensor("stag2", [128, 4096], BF16) as stag2:
            scrF = scrI[:].bitcast(F32)
            scr = (scrV, scrI, scrF, scrR)

            mats = [(wg_d, INNER, C, 0, wgT_d),
                    (wu_d, INNER, C, 1, wuT_d),
                    (wd_d, C, INNER, 2, wdT_d)]
            last_use = {0: 0, 1: 0}
            last_use_sem = {0: SV, 1: SV}
            WSC_S = 0

            def load_chunk(md, R, Ccols, it, par):
                if Ccols == C:
                    srcap = md.rearrange("(k p) c -> p k c", p=128)[
                        :, it * 4:(it + 1) * 4, :]
                    halves = [srcap[:, 0:2], srcap[:, 2:4]]
                else:
                    srcap = md[it * 128:(it + 1) * 128, :]
                    halves = [srcap[:, 0:2048], srcap[:, 2048:4096]]
                dv = []
                for q, eng in enumerate(("Y", "S")):
                    sem = f"wb{q}"
                    p.emit(eng, lambda e, ha=halves[q], q=q, par=par,
                           wbuf=wbuf: e.dma_start(
                        out=wbuf[:, par, q * 2048:(q + 1) * 2048].rearrange(
                            "p (k c) -> p k c", c=1024)
                        if ha.shape != (128, 2048) else
                        wbuf[:, par, q * 2048:(q + 1) * 2048], in_=ha),
                        waits=[(last_use_sem[par], last_use[par])],
                        inc=sem, dma=True)
                    dv.append((sem, p.now(sem)))
                return dv

            for mi, (md, R, Ccols, sci, dstT) in enumerate(mats):
                nit = (R * Ccols) // (128 * 4096)
                for it in range(nit):
                    par = it % 2
                    dv = load_chunk(md, R, Ccols, it, par)
                    p.emit("V", lambda e, it=it, par=par, mi=mi, wbuf=wbuf,
                           colsum=colsum:
                           e.tensor_reduce(colsum[:, mi * 8 + it:
                                                   mi * 8 + it + 1],
                                           wbuf[:, par, :], AX, OP.add,
                                           apply_absolute_value=True),
                           waits=dv, inc=SV)
                    last_use[par] = p.now(SV)
                    last_use_sem[par] = SV
            wsc_v = [0, 0, 0]
            for mi, (md, R, Ccols, sci, dstT) in enumerate(mats):
                nit = (R * Ccols) // (128 * 4096)
                p.emit("V", lambda e, nit=nit, mi=mi: e.tensor_reduce(
                    colsum[:, 36 + mi:37 + mi],
                    colsum[:, mi * 8:mi * 8 + nit], AX, OP.add),
                    waits=[(SV, p.now(SV))], inc=SV)
                gv = p.now(SV)
                t_mm(psB[0:1, 8 + mi:9 + mi], ones_col[:],
                     colsum[:, 36 + mi:37 + mi], True, True,
                     waits=[(SV, gv), (SV, ONES_V)], inc=ST)
                tv = p.now(ST)
                p.emit("S", lambda e, mi=mi: e.copy(sc11[0:1, mi:mi + 1],
                                                    psB[0:1, 8 + mi:9 + mi]),
                       waits=[(ST, tv)], inc=SS)
                sv = p.now(SS)
                t_mm(psB[:, 12 + mi:13 + mi], ones_row[:],
                     sc11[0:1, mi:mi + 1], True, True,
                     waits=[(SS, sv)], inc=ST)
                tv = p.now(ST)
                p.emit("V", lambda e, sci=sci: e.tensor_scalar(
                    wsc[:, 3 + sci:4 + sci], psB[:, 12 + sci:13 + sci],
                    1.0 / (INNER * C), 1e-5, OP.mult, OP.max),
                    waits=[(ST, tv)], inc=SV)
                p.emit("V", lambda e, sci=sci: e.reciprocal(
                    wsc[:, sci:sci + 1], wsc[:, 3 + sci:4 + sci]),
                    waits=[(SV, p.now(SV))], inc=SV)
                wsc_v[mi] = p.now(SV)

            for mi, (md, R, Ccols, sci, dstT) in enumerate(mats):
                nit = (R * Ccols) // (128 * 4096)
                qsc = wsc[:, sci:sci + 1]
                if debug_wq:
                    continue
                for it in range(nit):
                    par = it % 2
                    dv = load_chunk(md, R, Ccols, it, par)
                    for half in range(2):
                        round_chain(scr, wbuf[:, par, half * 2048:
                                                (half + 1) * 2048],
                                    qsc, scrA[:, half * 2048:(half + 1) * 2048],
                                    2048,
                                    waits_first=dv + [(SV, wsc_v[mi])],
                                    slot=half, off=half * 2048)
                    p.emit("V", lambda e, scrA=scrA, stag=stag:
                           e.tensor_scalar(stag[:], scrA[:], 1.0, -1.0,
                                           OP.min, OP.max),
                           waits=[(ST, p.now(ST))], inc=SV)
                    guards["rcclip"] = p.now(SV)
                    last_use[par] = p.now(SS)
                    last_use_sem[par] = SS
                    vq = p.now(SV)
                    for t32 in range(32):
                        ps_t = psA if t32 < 16 else psB
                        jo = t32 % 16
                        if Ccols == C:
                            ic, rr = divmod(t32, 4)
                            src_ap = stag[:, rr * 1024 + ic * 128:
                                          rr * 1024 + ic * 128 + 128]
                        else:
                            src_ap = stag[:, t32 * 128:(t32 + 1) * 128]
                        t_mm(ps_t[:, jo * 128:(jo + 1) * 128],
                             src_ap, ident[:],
                             True, True,
                             waits=[(SV, vq), (DC, CONST_DY),
                                    (SS, p.now(SS))] if t32 == 0 else (),
                             inc=ST if t32 == 31 else None)
                    tv = p.now(ST)
                    for bkt in range(8):
                        ps_t = psA if bkt < 4 else psB
                        bo = bkt % 4
                        p.emit("S", lambda e, bkt=bkt, ps_t=ps_t, bo=bo,
                               stag2=stag2: e.copy(
                            stag2[:, bkt * 512:(bkt + 1) * 512],
                            ps_t[:, bo * 512:(bo + 1) * 512]),
                            waits=[(ST, tv), (DG, p.now(DG))] if bkt == 0
                            else (), inc=SS if bkt == 7 else None)
                    sv = p.now(SS)
                    if Ccols == C:
                        dst = dstT[it].rearrange("i p c -> p i c")
                        srcv = stag2[:].rearrange("p (i c) -> p i c", c=512)
                        for q, eng in enumerate(("Y", "S")):
                            p.emit(eng, lambda e, dst=dst, srcv=srcv, q=q:
                                   e.dma_start(out=dst[:, q * 4:q * 4 + 4],
                                               in_=srcv[:, q * 4:q * 4 + 4]),
                                   waits=[(SS, sv)], inc=DG, dma=True)
                    else:
                        dst = dstT[it // 4, :, :,
                                   (it % 4) * 128:(it % 4) * 128 + 128]
                        dstv = dst.rearrange("i p c -> p i c")
                        srcv = stag2[:].rearrange("p (i q) -> p i q", q=128)
                        for q, eng in enumerate(("Y", "S")):
                            p.emit(eng, lambda e, dstv=dstv, srcv=srcv, q=q:
                                   e.dma_start(
                                out=dstv[:, q * 16:q * 16 + 16],
                                in_=srcv[:, q * 16:q * 16 + 16]),
                                waits=[(SS, sv)], inc=DG, dma=True)
        PRE_G = p.now(DG)
        pre_counts = [(ST, p.now(ST)), (SS, p.now(SS)), (SV, p.now(SV)),
                      (SG, p.now(SG)), (DG, p.now(DG)), (DC, p.now(DC)),
                      ("wb0", p.now("wb0")), ("wb1", p.now("wb1"))]
        for _eng in ("T", "S", "V", "G"):
            p.emit(_eng, lambda e: e.nop(), waits=pre_counts)
        if "wqrb" in taps:
            p.emit("Y", lambda e: e.dma_start(out=tap_d["wqrb"], in_=wgT_d[0]),
                   waits=[(DG, PRE_G)], inc=DO, dma=True)

        x_reader_ss = [0] * NT
        x_issue = [0] * NT
        for s in range(0 if preproc_only else nseq):
            sqAC = contextlib.ExitStack()
            with sqAC:
                xn_f = sqAC.enter_context(
                    nc.sbuf_tensor(f"xn_f{s}", [128, NC, T], BF16))
                sqAB = contextlib.ExitStack()
                with sqAB:
                    EAB = sqAB.enter_context
                    q_f = EAB(nc.sbuf_tensor(f"q_f{s}", [128, NC, T], BF16))
                    k_f = EAB(nc.sbuf_tensor(f"k_f{s}", [128, NC, T], BF16))
                    v_tm = EAB(nc.sbuf_tensor(f"v_tm{s}",
                                               [128, NT, 16 * 65], BF16))

                    sqA = contextlib.ExitStack()
                    with sqA:
                        EA = sqA.enter_context
                        xn_tm = EA(nc.sbuf_tensor(f"xn_tm{s}", [128, 1024], BF16))
                        v_fb = EA(nc.sbuf_tensor(f"v_fb{s}", [128, 1024], BF16))
                        wstr = EA(nc.sbuf_tensor(f"wstr{s}", [128, 2, 4, 128], F32))
                        wstrb = EA(nc.sbuf_tensor(f"wstrb{s}",
                                                  [128, 2, NC, 128], BF16))

                        prev_out = p.now(DO)
                        for t in range(NT):
                            xs = f"xd{t % 4}"
                            wl = [(DO, prev_out)]
                            if s == 0 and t >= 4:
                                wl.append((SS, x_reader_ss[t - 4]))
                            elif s == 1:
                                wl.append((SS, x_reader_ss[(t - 4) % NT]))
                            p.emit("Y", lambda e, t=t, s=s: e.dma_start(
                                out=x_sb[:, t, :],
                                in_=x_d.rearrange("s (n p) c -> s n p c",
                                                  p=128)[s, t]),
                                waits=wl, inc=xs, dma=True)
                            x_issue[t] = p.now(xs)
                        if "x0" in taps and s == 0:
                            p.emit("Y", lambda e: e.dma_start(
                                out=tap_d["x0"], in_=x_sb[:]),
                                waits=[(f"xd{tt2}", x_issue[tt2])
                                       for tt2 in range(NT)],
                                inc=DO, dma=True)

                        def emit_ln(t, out_ap, waits0):
                            p.emit("V", lambda e, t=t: e.reduce_sum(
                                small[:, 0:1], x_sb[:, t, :], AX), waits=waits0,
                                inc=SV)
                            p.emit("V", lambda e: e.tensor_scalar(
                                small[:, 1:2], small[:, 0:1], -1.0 / C, None,
                                OP.mult), waits=[(SV, p.now(SV))], inc=SV)
                            vv = p.now(SV)
                            p.emit("S", lambda e, t=t: e.activation(
                                psA[:, 0:1024], x_sb[:, t, :], AF.Square,
                                bias=small[:, 1:2], scale=f32(1.0)),
                                waits=[(SV, vv)], inc=SS)
                            sv = p.now(SS)
                            p.emit("V", lambda e: e.reduce_sum(
                                small[:, 2:3], psA[:, 0:1024], AX),
                                waits=[(SS, sv)], inc=SV)
                            p.emit("V", lambda e: e.tensor_scalar(
                                small[:, 3:4], small[:, 2:3], 1.0 / C, 1e-5,
                                OP.mult, OP.add), waits=[(SV, p.now(SV))],
                                inc=SV)
                            vv = p.now(SV)
                            p.emit("S", lambda e: e.activation(
                                small[:, 4:5], small[:, 3:4], AF.Sqrt),
                                waits=[(SV, vv)], inc=SS)
                            sv = p.now(SS)
                            p.emit("V", lambda e: e.reciprocal(
                                small[:, 5:6], small[:, 4:5]), waits=[(SS, sv)],
                                inc=SV)
                            p.emit("V", lambda e: e.tensor_tensor(
                                small[:, 6:7], small[:, 1:2], small[:, 5:6],
                                OP.mult), waits=[(SV, p.now(SV))], inc=SV)
                            vv = p.now(SV)
                            p.emit("S", lambda e, t=t, o=out_ap: e.activation(
                                o, x_sb[:, t, :], AF.Identity,
                                bias=small[:, 6:7], scale=small[:, 5:6]),
                                waits=[(SV, vv)], inc=SS)
                            return p.now(SS)

                        for t in range(NT):
                            sv = emit_ln(t, xn_tm[:],
                                         [(f"xd{t % 4}", x_issue[t])])
                            x_reader_ss[t] = sv
                            for j in range(8):
                                bank = j // 4
                                t_mm(psB[:, bank * 512 + (j % 4) * 128:
                                         bank * 512 + (j % 4) * 128 + 128],
                                     xn_tm[:, j * 128:(j + 1) * 128], ident[:],
                                     True, True,
                                     waits=[(SS, sv), (SV, p.now(SV)),
                                            (DC, CONST_DY)] if j == 0 else (),
                                     inc=ST if j in (3, 7) else None)
                            tv = p.now(ST)
                            p.emit("V", lambda e, t=t: e.tensor_copy(
                                xn_f[:, 0:4, t * 128:(t + 1) * 128],
                                psB[:, 0:512].rearrange("p (j c) -> p j c",
                                                        c=128)),
                                waits=[(ST, tv - 1)])
                            p.emit("V", lambda e, t=t: e.tensor_copy(
                                xn_f[:, 4:8, t * 128:(t + 1) * 128],
                                psB[:, 512:1024].rearrange("p (j c) -> p j c",
                                                           c=128)),
                                waits=[(ST, tv)], inc=SV)
                        XN_V = p.now(SV)
                        if "xn" in taps and s == 0:
                            p.emit("Y", lambda e: e.dma_start(
                                out=tap_d["xn"], in_=xn_f[:]),
                                waits=[(SV, XN_V)], inc=DO, dma=True)

                        for fo in range(24):
                            par = fo % 2
                            guard_t = guards["wstrb"][par]
                            for hf in range(2):
                                guard_g = guards["wstr"][hf]
                                wssem = f"ws{hf}"
                                p.emit("Y", lambda e, fo=fo, hf=hf, wstr=wstr:
                                       e.dma_start(
                                    out=wstr[:, hf],
                                    in_=wqkv_d.rearrange(
                                        "(c p) (f j) -> f p c j",
                                        p=128, j=128)[fo][:, hf * 4:hf * 4 + 4]),
                                    waits=[(SG, guard_g)], inc=wssem, dma=True)
                                dv = p.now(wssem)
                                p.emit("G", lambda e, par=par, hf=hf,
                                       wstrb=wstrb, wstr=wstr: e.tensor_copy(
                                    wstrb[:, par, hf * 4:hf * 4 + 4, :],
                                    wstr[:, hf]),
                                    waits=[(wssem, dv), (DG, PRE_G),
                                           (ST, guard_t)],
                                    inc=SG)
                                guards["wstr"][hf] = p.now(SG)
                            gv = p.now(SG)
                            ps = psA[:, par * 1024:par * 1024 + 1024]
                            for th in range(2):
                                sv_guard = p.now(SS)
                                for c in range(NC):
                                    for tq in range(2):
                                        wl = []
                                        if c == 0 and tq == 0:
                                            wl = [(SG, gv), (SV, XN_V),
                                                  (SS, sv_guard)]
                                        t_mm(ps[:, tq * 512:(tq + 1) * 512],
                                             wstrb[:, par, c, :],
                                             xn_f[:, c, th * 1024 + tq * 512:
                                                  th * 1024 + (tq + 1) * 512],
                                             start=(c == 0), stop=(c == NC - 1),
                                             waits=wl,
                                             inc=ST if (c == NC - 1 and tq == 1)
                                             else None)
                                tv = p.now(ST)
                                if fo < 8:
                                    dst = q_f[:, fo, th * 1024:(th + 1) * 1024]
                                elif fo < 16:
                                    dst = k_f[:, fo - 8,
                                              th * 1024:(th + 1) * 1024]
                                else:
                                    dst = v_fb[:, 0:1024]
                                p.emit("S", lambda e, dst=dst, ps=ps, fo=fo:
                                       e.activation(
                                    dst, ps[:, 0:1024], AF.Identity,
                                    bias=bqk[:, fo:fo + 1], scale=f32(1.0)),
                                    waits=[(ST, tv)] + (
                                        [(ST, p.now(ST))] if fo >= 16 else []),
                                    inc=SS)
                                if fo >= 16:
                                    sv = p.now(SS)
                                    fv = fo - 16
                                    for tb in range(2):
                                        bank = tb % 2
                                        for j in range(4):
                                            wl = []
                                            if j == 0:
                                                wl = [(SS, sv), (SV, p.now(SV))]
                                            t_mm(psB[:, bank * 512 + j * 128:
                                                     bank * 512 + j * 128 + 128],
                                                 v_fb[:, (tb * 4 + j) * 128:
                                                      (tb * 4 + j + 1) * 128],
                                                 ident[:], True, True, waits=wl,
                                                 inc=ST if j == 3 else None)
                                        tv = p.now(ST)
                                        psv = psB[:, bank * 512:
                                                   bank * 512 + 512].rearrange(
                                            "p (j c) -> p j c", c=128)
                                        for hh in range(2):
                                            p.emit("V", lambda e, th=th, tb=tb,
                                                   fv=fv, hh=hh, psv=psv:
                                                   e.tensor_copy(
                                                v_tm[:, th * 8 + tb * 4:
                                                     th * 8 + tb * 4 + 4,
                                                     (2 * fv + hh) * 65:
                                                     (2 * fv + hh) * 65 + 64],
                                                psv[:, :, hh * 64:
                                                    hh * 64 + 64]),
                                                waits=[(ST, tv)]
                                                if hh == 0 else (),
                                                inc=SV if hh == 1 else None)
                            guards["wstrb"][par] = p.now(ST)
                        QKV_S, QKV_V = p.now(SS), p.now(SV)
                        if "qf" in taps and s == 0:
                            for nm, buf in [("qf", q_f), ("kf", k_f),
                                            ("vtm", v_tm)]:
                                if nm in tap_d:
                                    p.emit("Y", lambda e, nm=nm, buf=buf:
                                           e.dma_start(out=tap_d[nm],
                                                       in_=buf[:]),
                                           waits=[(SS, QKV_S), (SV, QKV_V)],
                                           inc=DO, dma=True)

                    O_f = xn_f
                    with nc.sbuf_tensor(f"eSt{s}", [128, 2, 2, 512], BF16) \
                            as expSt, \
                         nc.sbuf_tensor(f"rsr{s}", [1, 1024], F32) as rs_row, \
                         nc.sbuf_tensor(f"rsT{s}", [64, 1024], F32) as rsT:
                        ones_dst = v_tm[:].rearrange("p k (h c) -> p k h c",
                                                     c=65)
                        for half in range(2):
                            p.emit("S", lambda e, half=half,
                                   ones_dst=ones_dst: e.activation(
                                ones_dst[:, half * 8:half * 8 + 8, :, 64:65],
                                ident[:].rearrange("p (a b c) -> p a b c",
                                                   a=8, b=16, c=1),
                                AF.Identity, bias=f32(1.0), scale=f32(0.0)),
                                waits=[(DC, CONST_DY), (SV, p.now(SV)),
                                       (ST, p.now(ST)), (SG, p.now(SG))]
                                if half == 0 else (),
                                inc=SS if half == 1 else None)

                        def stb(b):
                            t = psA if b < 2 else psB
                            return t[:, (b % 2) * 512:(b % 2) * 512 + 512]

                        rsTile = [psB[0:64, 512:1024],
                                  psB[64:128, 512:1024]]

                        def ob_ap(b, p0, p1):
                            t = psA if b < 2 else psB
                            c0 = 1024 + (b % 2) * 512
                            return t[p0:p1, c0:c0 + 512]

                        st_ss = [0, 0, 0]
                        exp_free = [[0, 0], [0, 0]]
                        o_free = [(0, 0)] * 4
                        rep_st = [0, 0]
                        recip_sv = [0, 0]
                        kbctr = 0
                        items = [(qt, pr) for qt in range(4) for pr in range(8)]
                        epi = None
                        for idx in range(len(items) + 1):
                            cur = None
                            if idx < len(items):
                                qt, pr = items[idx]
                                kbmax = 4 * qt + 3
                                obs = (2 * (idx % 2), 2 * (idx % 2) + 1)
                                heads = (2 * pr, 2 * pr + 1)
                                def emit_attv(t2, h, pkb, ess):
                                    n0 = max(0, pkb - 4 * qt) * 128
                                    awl = [(SS, ess)]
                                    if pkb == 0:
                                        awl += [(SV, o_free[obs[t2]][0]),
                                                (SS, o_free[obs[t2]][1])]
                                    t_mm(ob_ap(obs[t2], 0, 65)[:, n0:512],
                                         v_tm[:, pkb, 65 * h:65 * h + 65],
                                         expSt[:, t2, pkb % 2, n0:512],
                                         start=(pkb == 0),
                                         stop=(pkb == kbmax),
                                         waits=awl, inc=ST)
                                    exp_free[t2][pkb % 2] = p.now(ST)

                                pend = None
                                for kb in range(kbmax + 1):
                                    j = kb - 4 * qt
                                    n0 = max(0, j) * 128
                                    this_kb = []
                                    for t2, h in enumerate(heads):
                                        hc, hb = h // 2, (h % 2) * 64
                                        sb_ = kbctr % 3
                                        kbctr += 1
                                        wl = [(SS, st_ss[sb_])]
                                        if idx == 0 and kb == 0 and t2 == 0:
                                            wl += [(SS, QKV_S), (SV, QKV_V)]
                                        t_mm(stb(sb_)[:, n0:512],
                                             k_f[hb:hb + 64, hc,
                                                 kb * 128:(kb + 1) * 128],
                                             q_f[hb:hb + 64, hc,
                                                 qt * 512 + n0:(qt + 1) * 512],
                                             True, True, waits=wl, inc=ST)
                                        stv = p.now(ST)
                                        ewl = [(ST, stv),
                                               (ST, exp_free[t2][kb % 2])]
                                        if j >= 0:
                                            p.emit("V", lambda e, sb_=sb_,
                                                   n0=n0: e.tensor_tensor(
                                                stb(sb_)[:, n0:n0 + 128],
                                                stb(sb_)[:, n0:n0 + 128],
                                                mask[:], OP.add),
                                                waits=[(ST, stv),
                                                       (DC, CONST_DY)],
                                                inc=SV)
                                            ewl.append((SV, p.now(SV)))
                                        p.emit("S", lambda e, sb_=sb_, n0=n0,
                                               kb=kb, t2=t2,
                                               expSt=expSt: e.activation(
                                            expSt[:, t2, kb % 2, n0:512],
                                            stb(sb_)[:, n0:512],
                                            AF.Exp, bias=f32(EXP_SHIFT),
                                            scale=f32(0.125)),
                                            waits=ewl, inc=SS)
                                        st_ss[sb_] = p.now(SS)
                                        this_kb.append((t2, h, kb,
                                                        p.now(SS)))
                                    if pend:
                                        for args in pend:
                                            emit_attv(*args)
                                    pend = this_kb
                                for args in pend:
                                    emit_attv(*args)
                                cur = (qt, heads, obs, p.now(ST))
                            if epi is not None:
                                eqt, eheads, eobs, etv = epi
                                rs_ss2 = [0, 0]
                                for t2 in range(2):
                                    p.emit("S", lambda e, eb=eobs[t2], t2=t2,
                                           rs_row=rs_row: e.copy(
                                        rs_row[0:1, t2 * 512:t2 * 512 + 512],
                                        ob_ap(eb, 64, 65)),
                                        waits=[(ST, etv), (ST, rep_st[t2])],
                                        inc=SS)
                                    rs_ss2[t2] = p.now(SS)
                                for t2 in range(2):
                                    t_mm(rsTile[t2], ones_row[0:1, 0:64],
                                         rs_row[0:1, t2 * 512:t2 * 512 + 512],
                                         True, True,
                                         waits=[(SS, rs_ss2[t2]),
                                                (SV, recip_sv[t2])],
                                         inc=ST,
                                         tile_position=(0, 64 * t2))
                                    rep_st[t2] = p.now(ST)
                                for t2 in range(2):
                                    p.emit("V", lambda e, t2=t2, rsT=rsT:
                                           e.reciprocal(
                                        rsT[0:64, t2 * 512:t2 * 512 + 512],
                                        rsTile[t2]),
                                        waits=[(ST, rep_st[t2])], inc=SV)
                                    recip_sv[t2] = p.now(SV)
                                for t2, eh in enumerate(eheads):
                                    ehc, ehb = eh // 2, (eh % 2) * 64
                                    eb = eobs[t2]
                                    p.emit("V", lambda e, ehb=ehb, ehc=ehc,
                                           eqt=eqt, eb=eb, t2=t2, rsT=rsT:
                                           e.tensor_tensor(
                                        O_f[ehb:ehb + 64, ehc,
                                            eqt * 512:(eqt + 1) * 512],
                                        ob_ap(eb, 0, 64),
                                        rsT[0:64, t2 * 512:t2 * 512 + 512],
                                        OP.mult),
                                        waits=[(SV, recip_sv[t2])], inc=SV)
                                    o_free[eb] = (p.now(SV), rs_ss2[t2])
                            epi = cur
                    ATT_S = p.now(SS)
                    if "of" in taps and s == 0:
                        p.emit("Y", lambda e: e.dma_start(
                            out=tap_d["of"], in_=O_f[:]),
                            waits=[(SS, ATT_S), (SV, p.now(SV))],
                            inc=DO, dma=True)

                with nc.sbuf_tensor(f"wpj{s}", [128, NC, C], BF16) as wpj, \
                     nc.sbuf_tensor(f"wps{s}", [128, 2, C], F32) as wps, \
                     nc.sbuf_tensor(f"bpr{s}", [1, C], F32) as bproj_row:
                    p.emit("Y", lambda e, bproj_row=bproj_row: e.dma_start(
                        out=bproj_row[:],
                        in_=bproj_d.unsqueeze(0)),
                        waits=[(SS, ATT_S)], inc="bp", dma=True)
                    BPR_DY = p.now("bp")
                    for c in range(NC):
                        par = c % 2
                        guard = guards["wps"][par]
                        wpsem = f"wp{par}"
                        p.emit("Y", lambda e, c=c, par=par, wps=wps: e.dma_start(
                            out=wps[:, par, :],
                            in_=wproj_d[c * 128:(c + 1) * 128, :]),
                            waits=[(SG, guard), (SS, ATT_S)], inc=wpsem,
                            dma=True)
                        dv = p.now(wpsem)
                        p.emit("G", lambda e, c=c, par=par, wpj=wpj, wps=wps:
                               e.tensor_copy(
                            wpj[:, c, :], wps[:, par, :]),
                            waits=[(wpsem, dv)], inc=SG)
                        guards["wps"][par] = p.now(SG)
                    WPJ_G = p.now(SG)
                    for t in range(NT):
                        par = t % 2
                        ps = psA[:, par * 1024:par * 1024 + 1024]
                        vres = p.now(SV)
                        for c in range(NC):
                            for half in range(2):
                                wl = []
                                if c == 0 and half == 0:
                                    wl = [(SG, WPJ_G), (SS, ATT_S), (SV, vres),
                                          ("bp", BPR_DY)]
                                t_mm(ps[:, half * 512:(half + 1) * 512],
                                     O_f[:, c, t * 128:(t + 1) * 128],
                                     wpj[:, c, half * 512:(half + 1) * 512],
                                     start=(c == 0), stop=False, waits=wl)
                        for half in range(2):
                            t_mm(ps[:, half * 512:(half + 1) * 512],
                                 ones_row[:, 0:128],
                                 bproj_row[0:1, half * 512:(half + 1) * 512],
                                 start=False, stop=True,
                                 inc=ST if half == 1 else None)
                        tv = p.now(ST)
                        p.emit("V", lambda e, t=t, ps=ps: e.tensor_tensor(
                            x_sb[:, t, :], x_sb[:, t, :], ps[:, 0:1024], OP.add),
                            waits=[(ST, tv)], inc=SV)
                PROJ_V = p.now(SV)
                if "res1" in taps and s == 0:
                    p.emit("Y", lambda e: e.dma_start(
                        out=tap_d["res1"], in_=x_sb[:]),
                        waits=[(SV, PROJ_V)], inc=DO, dma=True)
            TAP_RES1_DO = p.now(DO)

            if not do_ffn:
                continue
            sqD = contextlib.ExitStack()
            with sqD:
                ED = sqD.enter_context
                scrV = ED(nc.sbuf_tensor(f"scrVd{s}", [128, 1024], F32))
                scrI = ED(nc.sbuf_tensor(f"scrId{s}", [128, 1024], I32))
                scrR = ED(nc.sbuf_tensor(f"scrRd{s}", [128, 1024], F32))
                scrA = ED(nc.sbuf_tensor(f"scrAd{s}", [128, 1024], F32))
                sil = ED(nc.sbuf_tensor(f"sild{s}", [128, 512], F32))
                xn_tm = ED(nc.sbuf_tensor(f"xntmd{s}", [128, 2, 1024], BF16))
                st1 = ED(nc.sbuf_tensor(f"st1d{s}", [128, 8, 8], F32))
                st2 = ED(nc.sbuf_tensor(f"st2d{s}", [128, 8, 8], F32))
                diag2 = ED(nc.sbuf_tensor(f"dg2d{s}", [128, 2, 128], BF16))
                xqf = ED(nc.sbuf_tensor(f"xqf{s}", [128, NC, 1024], BF16))
                gu = ED(nc.sbuf_tensor(f"gu{s}", [128, 8, INNER], BF16))
                wgb = ED(nc.sbuf_tensor(f"wgb{s}", [128, 2, NC, 512], BF16))
                wub = ED(nc.sbuf_tensor(f"wub{s}", [128, 2, NC, 512], BF16))
                wdb = ED(nc.sbuf_tensor(f"wdb{s}", [128, 2, 512], BF16))
                scrF = scrI[:].bitcast(F32)
                scr = (scrV, scrI, scrF, scrR)

                TAP_G3_DO = p.now(DO)

                def emit_ln2(t, out_ap, waits0):
                    p.emit("V", lambda e, t=t: e.reduce_sum(
                        small[:, 0:1], x_sb[:, t, :], AX), waits=waits0,
                        inc=SV)
                    p.emit("V", lambda e: e.tensor_scalar(
                        small[:, 1:2], small[:, 0:1], -1.0 / C, None, OP.mult),
                        waits=[(SV, p.now(SV))], inc=SV)
                    vv = p.now(SV)
                    p.emit("S", lambda e, t=t: e.activation(
                        psA[:, 0:1024], x_sb[:, t, :], AF.Square,
                        bias=small[:, 1:2], scale=f32(1.0)),
                        waits=[(SV, vv)], inc=SS)
                    sv = p.now(SS)
                    p.emit("V", lambda e: e.reduce_sum(
                        small[:, 2:3], psA[:, 0:1024], AX), waits=[(SS, sv)],
                        inc=SV)
                    p.emit("V", lambda e: e.tensor_scalar(
                        small[:, 3:4], small[:, 2:3], 1.0 / C, 1e-5, OP.mult,
                        OP.add), waits=[(SV, p.now(SV))], inc=SV)
                    vv = p.now(SV)
                    p.emit("S", lambda e: e.activation(
                        small[:, 4:5], small[:, 3:4], AF.Sqrt),
                        waits=[(SV, vv)], inc=SS)
                    sv = p.now(SS)
                    p.emit("V", lambda e: e.reciprocal(small[:, 5:6],
                                                       small[:, 4:5]),
                           waits=[(SS, sv)], inc=SV)
                    p.emit("V", lambda e: e.tensor_tensor(
                        small[:, 6:7], small[:, 1:2], small[:, 5:6], OP.mult),
                        waits=[(SV, p.now(SV))], inc=SV)
                    vv = p.now(SV)
                    p.emit("S", lambda e, t=t, o=out_ap: e.activation(
                        o, x_sb[:, t, :], AF.Identity,
                        bias=small[:, 6:7], scale=small[:, 5:6]),
                        waits=[(SV, vv)], inc=SS)
                    return p.now(SS)

                xq_ready = [0] * 8
                for hh in range(2):
                    if dlim < 1:
                        continue
                    wv = [[0] * 8 for _ in range(14)]
                    for tl in range(8):
                        t = hh * 8 + tl
                        w0 = [(SV, PROJ_V)] if (tl == 0 and hh == 0) else ()
                        p.emit("V", lambda e, t=t, tl=tl: e.reduce_sum(
                            st1[:, 0, tl:tl + 1], x_sb[:, t, :], AX),
                            waits=w0, inc=SV)
                        p.emit("V", lambda e, t=t, tl=tl: e.tensor_reduce(
                            st1[:, 7, tl:tl + 1], x_sb[:, t, :], AX, OP.max),
                            inc=SV)
                        p.emit("V", lambda e, t=t, tl=tl: e.tensor_reduce(
                            st2[:, 0, tl:tl + 1], x_sb[:, t, :], AX, OP.min),
                            inc=SV)
                        wv[0][tl] = p.now(SV)
                    for tl in range(8):
                        p.emit("V", lambda e, tl=tl: e.tensor_scalar(
                            st1[:, 1, tl:tl + 1], st1[:, 0, tl:tl + 1],
                            -1.0 / C, None, OP.mult),
                            waits=[(SV, wv[0][tl])], inc=SV)
                        wv[1][tl] = p.now(SV)
                    for tl in range(8):
                        t = hh * 8 + tl
                        p.emit("S", lambda e, t=t, tl=tl: e.activation(
                            scrV[:], x_sb[:, t, :], AF.Square,
                            bias=st1[:, 1, tl:tl + 1], scale=f32(1.0),
                            accum_out=st1[:, 2, tl:tl + 1]),
                            waits=[(SV, wv[1][tl])], inc=SS)
                        wv[2][tl] = p.now(SS)
                    for tl in range(8):
                        p.emit("V", lambda e, tl=tl: e.tensor_scalar(
                            st1[:, 3, tl:tl + 1], st1[:, 2, tl:tl + 1],
                            1.0 / C, 1e-5, OP.mult, OP.add),
                            waits=[(SS, wv[2][tl])], inc=SV)
                        wv[3][tl] = p.now(SV)
                    for tl in range(8):
                        p.emit("S", lambda e, tl=tl: e.activation(
                            st1[:, 4, tl:tl + 1], st1[:, 3, tl:tl + 1],
                            AF.Sqrt), waits=[(SV, wv[3][tl])], inc=SS)
                        wv[4][tl] = p.now(SS)
                    for tl in range(8):
                        p.emit("V", lambda e, tl=tl: e.reciprocal(
                            st1[:, 5, tl:tl + 1], st1[:, 4, tl:tl + 1]),
                            waits=[(SS, wv[4][tl])], inc=SV)
                        wv[5][tl] = p.now(SV)
                    for tl in range(8):
                        p.emit("V", lambda e, tl=tl: e.tensor_tensor(
                            st1[:, 6, tl:tl + 1], st1[:, 1, tl:tl + 1],
                            st1[:, 5, tl:tl + 1], OP.mult),
                            waits=[(SV, wv[5][tl])], inc=SV)
                        wv[6][tl] = p.now(SV)
                    for tl in range(8):
                        p.emit("V", lambda e, tl=tl: e.tensor_tensor(
                            st2[:, 1, tl:tl + 1], st1[:, 7, tl:tl + 1],
                            st1[:, 1, tl:tl + 1], OP.add),
                            waits=[(SV, wv[1][tl])])
                        p.emit("V", lambda e, tl=tl: e.tensor_tensor(
                            st2[:, 2, tl:tl + 1], st2[:, 0, tl:tl + 1],
                            st1[:, 1, tl:tl + 1], OP.add), inc=SV)
                        wv[7][tl] = p.now(SV)
                    for tl in range(8):
                        p.emit("V", lambda e, tl=tl: e.tensor_scalar(
                            st2[:, 3, tl:tl + 1], st2[:, 2, tl:tl + 1],
                            -1.0, None, OP.mult),
                            waits=[(SV, wv[7][tl])], inc=SV)
                        wv[8][tl] = p.now(SV)
                    for tl in range(8):
                        p.emit("V", lambda e, tl=tl: e.tensor_tensor(
                            st2[:, 4, tl:tl + 1], st2[:, 1, tl:tl + 1],
                            st2[:, 3, tl:tl + 1], OP.max),
                            waits=[(SV, wv[8][tl])], inc=SV)
                        wv[9][tl] = p.now(SV)
                    for tl in range(8):
                        p.emit("V", lambda e, tl=tl: e.tensor_scalar(
                            st2[:, 5, tl:tl + 1], st2[:, 4, tl:tl + 1],
                            st1[:, 5, tl:tl + 1], 1e-5, OP.mult, OP.max),
                            waits=[(SV, wv[9][tl]), (SV, wv[5][tl])], inc=SV)
                        wv[10][tl] = p.now(SV)
                    for tl in range(8):
                        p.emit("V", lambda e, tl=tl: e.reciprocal(
                            st2[:, 6, tl:tl + 1], st2[:, 5, tl:tl + 1]),
                            waits=[(SV, wv[10][tl])], inc=SV)
                        wv[11][tl] = p.now(SV)
                    for tl in range(8):
                        p.emit("V", lambda e, tl=tl: e.tensor_scalar(
                            st2[:, 7, tl:tl + 1], st2[:, 6, tl:tl + 1],
                            127.0, None, OP.mult),
                            waits=[(SV, wv[11][tl])], inc=SV)
                        wv[12][tl] = p.now(SV)
                    for tl in range(8):
                        p.emit("V", lambda e, tl=tl: e.tensor_scalar(
                            invsx[:, tl:tl + 1], st2[:, 5, tl:tl + 1],
                            1.0 / 127.0, None, OP.mult),
                            waits=[(SV, wv[10][tl])], inc=SV)
                        wv[13][tl] = p.now(SV)
                    tvp = [0, 0]
                    for tl in range(8):
                        t = hh * 8 + tl
                        par = tl % 2
                        p.emit("S", lambda e, t=t, tl=tl: e.activation(
                            scrA[:], x_sb[:, t, :], AF.Identity,
                            bias=st1[:, 6, tl:tl + 1],
                            scale=st1[:, 5, tl:tl + 1]),
                            waits=[(SV, wv[6][tl])], inc=SS)
                        for half in range(2):
                            round_chain(scr, scrA[:, half * 512:
                                                 (half + 1) * 512],
                                        st2[:, 7, tl:tl + 1],
                                        xn_tm[:, par, half * 512:
                                              (half + 1) * 512],
                                        512,
                                        waits_first=[(SV, wv[12][tl]),
                                                     (ST, tvp[par])],
                                        slot=half, off=half * 512)
                        p.emit("V", lambda e, tl=tl, par=par: e.tensor_scalar(
                            diag2[:, par], ident[:], invsx[:, tl:tl + 1],
                            None, OP.mult),
                            waits=[(SV, wv[13][tl]), (ST, tvp[par])], inc=SV)
                        vq = p.now(SV)
                        for j in range(8):
                            bank = j // 4
                            t_mm(psB[:, bank * 512 + (j % 4) * 128:
                                     bank * 512 + (j % 4) * 128 + 128],
                                 xn_tm[:, par, j * 128:(j + 1) * 128],
                                 diag2[:, par],
                                 True, True,
                                 waits=[(SV, vq)] if j == 0 else (),
                                 inc=ST if j in (3, 7) else None)
                        tv = p.now(ST)
                        tvp[par] = tv
                        p.emit("V", lambda e, tl=tl: e.tensor_copy(
                            xqf[:, 0:4, tl * 128:(tl + 1) * 128],
                            psB[:, 0:512].rearrange("p (j c) -> p j c",
                                                    c=128)),
                            waits=[(ST, tv - 1)])
                        p.emit("V", lambda e, tl=tl: e.tensor_copy(
                            xqf[:, 4:8, tl * 128:(tl + 1) * 128],
                            psB[:, 512:1024].rearrange("p (j c) -> p j c",
                                                       c=128)),
                            waits=[(ST, tv)], inc=SV)
                        xq_ready[tl] = p.now(SV)
                    D1_V = p.now(SV)
                    if dlim < 2:
                        continue
                    if "xqf" in taps and s == 0:
                        p.emit("Y", lambda e, hh=hh: e.dma_start(
                            out=tap_d["xqf"][hh], in_=xqf[:]),
                            waits=[(SV, D1_V)], inc=DO, dma=True)

                    for oc in range(8):
                        par = oc % 2
                        guard = guards["wgu"][par]
                        p.emit("Y", lambda e, oc=oc, par=par, wgb=wgb:
                               e.dma_start(
                            out=wgb[:, par],
                            in_=wgT_d[oc].rearrange("i p c -> p i c")),
                            waits=[(ST, guard)], inc=f"wg{par}", dma=True)
                        p.emit("Y", lambda e, oc=oc, par=par, wub=wub:
                               e.dma_start(
                            out=wub[:, par],
                            in_=wuT_d[oc].rearrange("i p c -> p i c")),
                            inc=f"wu{par}", dma=True)
                        dg = [(f"wg{par}", p.now(f"wg{par}")),
                              (f"wu{par}", p.now(f"wu{par}"))]
                        for tl in range(8):
                            tpar = tl % 2
                            gps = psA[:, tpar * 512:tpar * 512 + 512]
                            ups = psA[:, 1024 + tpar * 512:
                                      1024 + tpar * 512 + 512]
                            eg = p.now(SV)
                            for i in range(NC):
                                wl = []
                                if i == 0:
                                    wl = dg + [(SV, xq_ready[tl]), (SV, eg),
                                               (DO, TAP_G3_DO)]
                                t_mm(gps, xqf[:, i, tl * 128:(tl + 1) * 128],
                                     wgb[:, par, i, :],
                                     start=(i == 0), stop=(i == NC - 1),
                                     waits=wl)
                                t_mm(ups, xqf[:, i, tl * 128:(tl + 1) * 128],
                                     wub[:, par, i, :],
                                     start=(i == 0), stop=(i == NC - 1),
                                     inc=ST if i == NC - 1 else None)
                            tv = p.now(ST)
                            p.emit("S", lambda e, gps=gps: e.activation(
                                sil[:], gps, AF.Silu,
                                bias=f32(0.0), scale=wsc[:, 3:4]),
                                waits=[(ST, tv), (SV, p.now(SV))], inc=SS)
                            sv = p.now(SS)
                            p.emit("V", lambda e, ups=ups, tl=tl, oc=oc:
                                   e.scalar_tensor_tensor(
                                gu[:, tl, oc * 512:(oc + 1) * 512], ups,
                                wsc[:, 4:5], sil[:], OP.mult, OP.mult),
                                waits=[(SS, sv)], inc=SV)
                        guards["wgu"][par] = p.now(ST)
                    D2_V = p.now(SV)
                    if dlim < 3:
                        continue
                    if "gu" in taps and s == 0:
                        p.emit("Y", lambda e, hh=hh: e.dma_start(
                            out=tap_d["gu"][hh], in_=gu[:]),
                            waits=[(SV, D2_V)], inc=DO, dma=True)
                    TAP_GU_DO = p.now(DO)

                    aw = [[0] * 8 for _ in range(6)]
                    for tl in range(8):
                        p.emit("V", lambda e, tl=tl: e.tensor_reduce(
                            st1[:, 0, tl:tl + 1], gu[:, tl, :], AX, OP.max,
                            apply_absolute_value=True),
                            waits=[(SV, D2_V), (DO, TAP_GU_DO)]
                            if tl == 0 else (), inc=SV)
                        aw[0][tl] = p.now(SV)
                    for tl in range(8):
                        p.emit("V", lambda e, tl=tl: e.tensor_scalar(
                            st1[:, 1, tl:tl + 1], st1[:, 0, tl:tl + 1],
                            1e-5, None, OP.max),
                            waits=[(SV, aw[0][tl])], inc=SV)
                        aw[1][tl] = p.now(SV)
                    for tl in range(8):
                        p.emit("V", lambda e, tl=tl: e.reciprocal(
                            st1[:, 2, tl:tl + 1], st1[:, 1, tl:tl + 1]),
                            waits=[(SV, aw[1][tl])], inc=SV)
                        aw[2][tl] = p.now(SV)
                    for tl in range(8):
                        p.emit("V", lambda e, tl=tl: e.tensor_scalar(
                            st1[:, 3, tl:tl + 1], st1[:, 2, tl:tl + 1],
                            127.0, None, OP.mult),
                            waits=[(SV, aw[2][tl])], inc=SV)
                        aw[3][tl] = p.now(SV)
                    for tl in range(8):
                        p.emit("V", lambda e, tl=tl: e.tensor_scalar(
                            st1[:, 4, tl:tl + 1], st1[:, 1, tl:tl + 1],
                            1.0 / 127.0, None, OP.mult),
                            waits=[(SV, aw[1][tl])], inc=SV)
                        aw[4][tl] = p.now(SV)
                    for tl in range(8):
                        p.emit("V", lambda e, tl=tl: e.tensor_tensor(
                            st1[:, 5, tl:tl + 1], st1[:, 4, tl:tl + 1],
                            wsc[:, 5:6], OP.mult),
                            waits=[(SV, aw[4][tl])], inc=SV)
                        aw[5][tl] = p.now(SV)
                    tvp3 = [0, 0]
                    for tl in range(8):
                        par = tl % 2
                        for cc in range(8):
                            src2 = gu[:, tl, cc * 512:(cc + 1) * 512]
                            round_chain(scr, src2, st1[:, 3, tl:tl + 1],
                                        src2, 512,
                                        waits_first=[(SV, aw[3][tl])]
                                        if cc == 0 else (), slot=cc % 2,
                                        off=(cc % 2) * 512)
                        p.emit("V", lambda e, tl=tl, par=par: e.tensor_scalar(
                            diag2[:, par], ident[:], st1[:, 5, tl:tl + 1],
                            None, OP.mult),
                            waits=[(SV, aw[5][tl]), (ST, tvp3[par])], inc=SV)
                        vq = p.now(SV)
                        for j in range(NI):
                            ps_t = psA if j < 16 else psB
                            jo = j % 16
                            t_mm(ps_t[:, jo * 128:(jo + 1) * 128],
                                 gu[:, tl, j * 128:(j + 1) * 128],
                                 diag2[:, par],
                                 True, True,
                                 waits=[(SV, vq), (SS, p.now(SS))]
                                 if j == 0 else (),
                                 inc=ST if j == NI - 1 else None)
                        tv = p.now(ST)
                        tvp3[par] = tv
                        for bkt in range(8):
                            ps_t = psA if bkt < 4 else psB
                            bo = bkt % 4
                            p.emit("V", lambda e, tl=tl, bkt=bkt, ps_t=ps_t,
                                   bo=bo: e.tensor_copy(
                                gu[:, tl, bkt * 512:(bkt + 1) * 512],
                                ps_t[:, bo * 512:(bo + 1) * 512]),
                                waits=[(ST, tv)] if bkt == 0 else (),
                                inc=SV if bkt == 7 else None)
                    D3_V = p.now(SV)
                    if dlim < 4:
                        continue
                    if "gud3" in taps and s == 0:
                        p.emit("Y", lambda e, hh=hh: e.dma_start(
                            out=tap_d["gud3"][hh], in_=gu[:]),
                            waits=[(SV, D3_V)], inc=DO, dma=True)
                    TAP_G3_DO = p.now(DO)

                    for ch2 in range(2):
                        for i in range(NI):
                            par = i % 2
                            guard = guards["wdb"][par]
                            p.emit("Y", lambda e, ch2=ch2, i=i, par=par,
                                   wdb=wdb:
                                   e.dma_start(out=wdb[:, par],
                                               in_=wdT_d[ch2, i]),
                                   waits=[(ST, guard)], inc=f"wd{par}",
                                   dma=True)
                            dg = (f"wd{par}", p.now(f"wd{par}"))
                            for tl in range(8):
                                ps = (psA if tl < 4 else psB)[
                                    :, (tl % 4) * 512:(tl % 4) * 512 + 512]
                                wl = []
                                if i == 0 and tl == 0:
                                    wl = [dg, (SV, D3_V),
                                          (SV, p.now(SV)), (DO, TAP_RES1_DO)]
                                elif tl == 0:
                                    wl = [dg]
                                t_mm(ps, gu[:, tl, i * 128:(i + 1) * 128],
                                     wdb[:, par, :],
                                     start=(i == 0), stop=(i == NI - 1),
                                     waits=wl, inc=ST if tl == 7 else None)
                            guards["wdb"][par] = p.now(ST)
                        tv = p.now(ST)
                        for tl in range(8):
                            t = hh * 8 + tl
                            ps = (psA if tl < 4 else psB)[
                                :, (tl % 4) * 512:(tl % 4) * 512 + 512]
                            p.emit("V", lambda e, t=t, ps=ps, ch2=ch2:
                                   e.tensor_tensor(
                                x_sb[:, t, ch2 * 512:(ch2 + 1) * 512],
                                x_sb[:, t, ch2 * 512:(ch2 + 1) * 512],
                                ps, OP.add),
                                waits=[(ST, tv)] if tl == 0 else (),
                                inc=SV if tl == 7 else None)
                    D4_V = p.now(SV)
                    for tl in range(8):
                        t = hh * 8 + tl
                        if skip_store:
                            continue
                        p.emit("Y", lambda e, t=t, s=s: e.dma_start(
                            out=y_d.rearrange("s (n p) c -> s n p c",
                                              p=128)[s, t],
                            in_=x_sb[:, t, :]),
                            waits=[(SV, D4_V)], inc=DO, dma=True)

        for _i in range(pad_nops):
            p.emit("T", lambda e: e.nop())
        for _i in range(pad_dmas):
            p.emit("Y", lambda e: e.dma_start(out=small[:, 0:1],
                                              in_=small[:, 1:2]),
                   inc=DY, dma=True)
        if "wqrb2" in taps:
            p.emit("Y", lambda e: e.dma_start(out=tap_d["wqrb2"], in_=wgT_d[0]),
                   inc=DO, dma=True)
        if "mark" in taps:
            p.emit("Y", lambda e: e.dma_start(out=tap_d["mark"],
                                              in_=wgT_d[7, 7, :, 0:128]),
                   inc=DO, dma=True)
        total_out = p.now(DO)
        p.emit("Y", lambda e: e.nop(), waits=[(DO, total_out)])

        with nc.Block() as blk:
            p.build(blk)

    return nc


def kernel(L0_attn_x, L0_attn_w_qkv, L0_attn_b_qkv, L0_attn_w_proj,
           L0_attn_b_proj, L0_ffn_w_gate, L0_ffn_w_up, L0_ffn_w_down):
    x = np.ascontiguousarray(np.asarray(L0_attn_x, dtype=np.float32))
    wqkv = np.ascontiguousarray(np.asarray(L0_attn_w_qkv, dtype=np.float32))
    bqkv = np.ascontiguousarray(np.asarray(L0_attn_b_qkv, dtype=np.float32))
    wproj = np.ascontiguousarray(np.asarray(L0_attn_w_proj, dtype=np.float32))
    bproj = np.ascontiguousarray(np.asarray(L0_attn_b_proj, dtype=np.float32))
    wg = np.ascontiguousarray(np.asarray(L0_ffn_w_gate, dtype=np.float32))
    wu = np.ascontiguousarray(np.asarray(L0_ffn_w_up, dtype=np.float32))
    wd = np.ascontiguousarray(np.asarray(L0_ffn_w_down, dtype=np.float32))

    nc = build_program()
    in_maps = []
    for c in range(NCORE):
        in_maps.append({
            "x": x[c * SEQ_PER_CORE:(c + 1) * SEQ_PER_CORE],
            "w_qkv": wqkv, "b_qkv": bqkv, "w_proj": wproj, "b_proj": bproj,
            "w_gate": wg, "w_up": wu, "w_down": wd,
        })
    res = run_bass_kernel_spmd(nc, in_maps, core_ids=list(range(NCORE)))
    out = np.concatenate([r["y"] for r in res.results], axis=0)
    return out.astype(np.float32)


# version 12
# speedup vs baseline: 1.0211x; 1.0211x over previous
import contextlib
from collections import defaultdict

import numpy as np
import ml_dtypes

import concourse.bass as bass
import concourse.mybir as mybir
from concourse.bass_utils import run_bass_kernel_spmd

F32 = mybir.dt.float32
BF16 = mybir.dt.bfloat16
I32 = mybir.dt.int32
AX = mybir.AxisListType.X
OP = mybir.AluOpType
AF = mybir.ActivationFunctionType

f32 = np.float32

B, T, C = 16, 2048, 1024
NCORE = 8
SEQ_PER_CORE = B // NCORE
H, D = 16, 64
INNER = 4096
NT = T // 128
NC = C // 128
NI = INNER // 128
EXP_SHIFT = -4.0


class Prog:

    def __init__(self, nc):
        self.nc = nc
        self.ops = []
        self.cnt = defaultdict(int)
        self._sem_handles = {}

    def sem(self, name):
        if name not in self._sem_handles:
            self._sem_handles[name] = self.nc.alloc_semaphore(name=name)
        return self._sem_handles[name]

    def now(self, sem):
        return self.cnt[sem]

    def emit(self, eng, fn, waits=(), inc=None, dma=False):
        incs = []
        if inc is not None:
            n = 16 if dma else 1
            self.cnt[inc] += n
            incs.append((inc, n))
        self.ops.append((eng, fn, tuple((s, v) for s, v in waits if v > 0),
                         tuple(incs)))

    def build(self, block):
        def make_runner(tag):
            def run(eng):
                watermark = defaultdict(int)
                for e, fn, waits, incs in self.ops:
                    if e != tag:
                        continue
                    for s, v in waits:
                        if v > watermark[s]:
                            eng.wait_ge(self.sem(s), v)
                            watermark[s] = v
                    ins = fn(eng)
                    for s, n in incs:
                        ins = ins.then_inc(self.sem(s), n)
            return run

        block.tensor(make_runner("T"))
        block.scalar(make_runner("S"))
        block.vector(make_runner("V"))
        block.gpsimd(make_runner("G"))
        block.sync(make_runner("Y"))


def ceil_div(a, b):
    return (a + b - 1) // b


def build_program(taps=(), debug_wq=False, preproc_only=False, drc=True, pad_nops=0, pad_dmas=0, nseq=SEQ_PER_CORE, nheads=H, do_ffn=True, skip_store=False, dlim=4):
    taps = set(taps)
    nc = bass.Bass("TRN2", target_bir_lowering=False, detect_race_conditions=drc)

    x_d = nc.dram_tensor("x", [SEQ_PER_CORE, T, C], F32, kind="ExternalInput").ap()
    y_d = nc.dram_tensor("y", [SEQ_PER_CORE, T, C], F32, kind="ExternalOutput").ap()
    wqkv_d = nc.dram_tensor("w_qkv", [C, 3 * C], F32, kind="ExternalInput").ap()
    bqkv_d = nc.dram_tensor("b_qkv", [3 * C], F32, kind="ExternalInput").ap()
    wproj_d = nc.dram_tensor("w_proj", [C, C], F32, kind="ExternalInput").ap()
    bproj_d = nc.dram_tensor("b_proj", [C], F32, kind="ExternalInput").ap()
    wg_d = nc.dram_tensor("w_gate", [INNER, C], F32, kind="ExternalInput").ap()
    wu_d = nc.dram_tensor("w_up", [INNER, C], F32, kind="ExternalInput").ap()
    wd_d = nc.dram_tensor("w_down", [C, INNER], F32, kind="ExternalInput").ap()

    wk = "ExternalInput" if debug_wq else (
        "ExternalOutput" if "wq" in taps else "Internal")
    wgT_d = nc.dram_tensor("wgT", [8, NC, 128, 512], BF16, kind=wk).ap()
    wuT_d = nc.dram_tensor("wuT", [8, NC, 128, 512], BF16, kind=wk).ap()
    wdT_d = nc.dram_tensor("wdT", [2, NI, 128, 512], BF16, kind=wk).ap()

    tap_d = {}
    def decl_tap(name, shape, dt):
        if name in taps:
            tap_d[name] = nc.dram_tensor("tap_" + name, shape, dt,
                                         kind="ExternalOutput").ap()
    decl_tap("x0", [128, NT, C], F32)
    decl_tap("xn", [128, NC, T], BF16)
    decl_tap("qf", [128, NC, T], BF16)
    decl_tap("kf", [128, NC, T], BF16)
    decl_tap("vtm", [128, NT, 16 * 65], BF16)
    decl_tap("of", [128, NC, T], BF16)
    decl_tap("res1", [128, NT, C], F32)
    decl_tap("xqf", [2, 128, NC, 1024], BF16)
    decl_tap("gu", [2, 128, 8, INNER], BF16)
    decl_tap("gud3", [2, 128, 8, INNER], BF16)
    decl_tap("wsc", [128, 8], F32)
    decl_tap("wgb0", [128, NC, 512], BF16)
    decl_tap("wqrb", [NC, 128, 512], BF16)
    decl_tap("wqrb2", [NC, 128, 512], BF16)
    decl_tap("mark", [128, 128], BF16)
    decl_tap("stg", [128, 1024], BF16)
    decl_tap("sca", [128, 1024], F32)
    decl_tap("scv", [128, 1024], F32)
    decl_tap("colsum", [128, 40], F32)

    ident_np = np.eye(128, dtype=ml_dtypes.bfloat16)
    mask_np = np.tril(np.full((128, 128), -1e9, dtype=np.float32), -1)
    ident_d = nc.inline_tensor(ident_np, name="identc").ap()
    mask_d = nc.inline_tensor(mask_np, name="maskc").ap()

    ctx = contextlib.ExitStack()
    E = ctx.enter_context

    with ctx:
        x_sb = E(nc.sbuf_tensor("x_sb", [128, NT, C], F32))
        ident = E(nc.sbuf_tensor("ident", [128, 128], BF16))
        mask = E(nc.sbuf_tensor("mask", [128, 128], F32))
        ones_row = E(nc.sbuf_tensor("ones_row", [1, 128], F32))
        ones_col = E(nc.sbuf_tensor("ones_col", [128, 1], F32))
        bqk = E(nc.sbuf_tensor("bqk", [128, 24], F32))
        diag = E(nc.sbuf_tensor("diag", [128, 128], BF16))
        small = E(nc.sbuf_tensor("small", [128, 16], F32))
        invsx = E(nc.sbuf_tensor("invsx", [128, 8], F32))
        rsumc = E(nc.sbuf_tensor("rsumc", [128, 4], F32))
        wsc = E(nc.sbuf_tensor("wsc", [128, 8], F32))
        sc11 = E(nc.sbuf_tensor("sc11", [1, 8], F32))

        psA = E(nc.psum_tensor("psA", [128, 2048], F32))
        psB = E(nc.psum_tensor("psB", [128, 2048], F32))
        psA_bf = psA[:].bitcast(BF16)
        psB_bf = psB[:].bitcast(BF16)

        p = Prog(nc)
        ST, SS, SV, SG = "sT", "sS", "sV", "sG"
        DY, DG, DO = "dY", "dG", "dO"
        DC = "dC"

        guards = {"rcclip": 0, "wstr": [0, 0], "wstrb": [0, 0], "wgu": [0, 0],
                  "wdb": [0, 0], "wps": [0, 0]}

        def t_mm(out_ap, lhsT_ap, rhs_ap, start, stop, waits=(), inc=None,
                 tile_position=None):
            p.emit("T", lambda e, o=out_ap, l=lhsT_ap, r=rhs_ap, st=start,
                   sp=stop, tp=tile_position:
                   e.matmul(o, l, r, start=st, stop=sp, tile_position=tp),
                   waits=waits, inc=inc)

        rcg = {0: 0, 1: 0}

        def round_chain(scr, src_ap, scale_ap, dst_ap, n, waits_first=(),
                        slot=None, off=0):
            scrV, scrI, scrF, scrR = scr
            guard = p.now(SV) if slot is None else rcg[slot]
            wl = list(waits_first) + [("sV", guard)]
            o1 = off + n
            p.emit("S", lambda e, off=off, o1=o1: e.activation(
                scrV[:, off:o1], src_ap, AF.Identity,
                scale=scale_ap, bias=f32(0.0)), waits=wl)
            p.emit("S", lambda e, off=off, o1=o1: e.copy(scrI[:, off:o1],
                                                         scrV[:, off:o1]))
            p.emit("S", lambda e, off=off, o1=o1: e.copy(scrR[:, off:o1],
                                                         scrI[:, off:o1]),
                   inc=SS)
            sv = p.now(SS)
            p.emit("V", lambda e, off=off, o1=o1: e.tensor_tensor(
                scrV[:, off:o1], scrV[:, off:o1], scrR[:, off:o1],
                OP.subtract), waits=[(SS, sv)])
            p.emit("V", lambda e, off=off, o1=o1: e.tensor_scalar(
                scrF[:, off:o1], scrV[:, off:o1], -1.0, 0.5,
                OP.mult, OP.is_ge))
            p.emit("V", lambda e, off=off, o1=o1: e.scalar_tensor_tensor(
                scrR[:, off:o1], scrV[:, off:o1], 0.5, scrR[:, off:o1],
                OP.is_ge, OP.add))
            p.emit("V", lambda e, d=dst_ap, off=off, o1=o1: e.tensor_tensor(
                d, scrR[:, off:o1], scrF[:, off:o1], OP.subtract), inc=SV)
            if slot is not None:
                rcg[slot] = p.now(SV)

        p.emit("Y", lambda e: e.dma_start(out=ident[:], in_=ident_d), inc=DC, dma=True)
        p.emit("Y", lambda e: e.dma_start(out=mask[:], in_=mask_d), inc=DC, dma=True)
        p.emit("Y", lambda e: e.dma_start(
            out=bqk[:], in_=bqkv_d.rearrange("(f p) -> p f", p=128),
            allow_slow_non_contiguous=True), inc=DC, dma=True)
        CONST_DY = p.now(DC)
        if "mark" in taps:
            p.emit("V", lambda e: e.memset(diag[:], 7.0), inc=SV)
            mv = p.now(SV)
            p.emit("Y", lambda e: e.dma_start(out=wgT_d[7, 7, :, 0:128],
                                              in_=diag[:]),
                   waits=[(SV, mv)], inc=DC, dma=True)
        p.emit("V", lambda e: e.memset(ones_row[:], 1.0))
        p.emit("V", lambda e: e.memset(ones_col[:], 1.0), inc=SV)
        ONES_V = p.now(SV)

        with nc.sbuf_tensor("wbuf", [128, 2, 4096], F32) as wbuf, \
             nc.sbuf_tensor("mbuf", [128, 4, 1024], F32) as mbuf, \
             nc.sbuf_tensor("colsum", [128, 100], F32) as colsum, \
             nc.sbuf_tensor("scrV0", [128, 4096], F32) as scrV, \
             nc.sbuf_tensor("scrI0", [128, 4096], I32) as scrI, \
             nc.sbuf_tensor("scrR0", [128, 4096], F32) as scrR, \
             nc.sbuf_tensor("scrA0", [128, 4096], F32) as scrA, \
             nc.sbuf_tensor("stag", [128, 4096], BF16) as stag, \
             nc.sbuf_tensor("stag2", [128, 4096], BF16) as stag2:
            scrF = scrI[:].bitcast(F32)
            scr = (scrV, scrI, scrF, scrR)

            mats = [(wg_d, INNER, C, 0, wgT_d),
                    (wu_d, INNER, C, 1, wuT_d),
                    (wd_d, C, INNER, 2, wdT_d)]
            last_use = {0: 0, 1: 0}
            last_use_sem = {0: SV, 1: SV}
            WSC_S = 0

            def load_chunk(md, R, Ccols, it, par):
                eng = "Y" if par == 0 else "S"
                sem = f"wb{par}"
                if Ccols == C:
                    srcap = md.rearrange("(k p) c -> p k c", p=128)[
                        :, it * 4:(it + 1) * 4, :]
                else:
                    srcap = md[it * 128:(it + 1) * 128, :]
                p.emit(eng, lambda e, srcap=srcap, par=par, wbuf=wbuf:
                       e.dma_start(out=wbuf[:, par].rearrange(
                           "p (k c) -> p k c", c=1024) if False else
                           wbuf[:, par, :], in_=srcap),
                       waits=[(last_use_sem[par], last_use[par])],
                       inc=sem, dma=True)
                return (sem, p.now(sem))

            for it in range(8):
                par = it % 2
                dv = load_chunk(wg_d, INNER, C, it, par)
                p.emit("V", lambda e, it=it, par=par, wbuf=wbuf,
                       colsum=colsum:
                       e.tensor_reduce(colsum[:, it:it + 1],
                                       wbuf[:, par, :], AX, OP.add,
                                       apply_absolute_value=True),
                       waits=[dv], inc=SV)
                last_use[par] = p.now(SV)
                last_use_sem[par] = SV

            mitems = []
            for md2, base in ((wu_d, 32), (wd_d, 64)):
                for k in range(32):
                    mitems.append((md2, base, k))
            mb_next = [0]
            mred_v = {}
            mload_v = {}

            def emit_mean_batch():
                b = mb_next[0]
                mb_next[0] += 1
                if b > 0:
                    for k in range(4 * (b - 1), 4 * b):
                        if k >= len(mitems):
                            break
                        _, base, kk = mitems[k]
                        slot = k % 4
                        col = base + kk
                        p.emit("V", lambda e, slot=slot, col=col, mbuf=mbuf,
                               colsum=colsum:
                               e.tensor_reduce(colsum[:, col:col + 1],
                                               mbuf[:, slot], AX, OP.add,
                                               apply_absolute_value=True),
                               waits=[("mg", mload_v[k])], inc=SV)
                        mred_v[k] = p.now(SV)
                for k in range(4 * b, 4 * b + 4):
                    if k >= len(mitems):
                        break
                    md2, base, kk = mitems[k]
                    slot = k % 4
                    if base == 32:
                        src = md2[kk * 128:(kk + 1) * 128, :]
                    else:
                        rb, cb = divmod(kk, 4)
                        src = md2[rb * 128:(rb + 1) * 128,
                                  cb * 1024:(cb + 1) * 1024]
                    wl = [(SV, mred_v[k - 4])] if k >= 4 else []
                    p.emit("G", lambda e, src=src, slot=slot, mbuf=mbuf:
                           e.dma_start(out=mbuf[:, slot], in_=src),
                           waits=wl, inc="mg", dma=True)
                    mload_v[k] = p.now("mg")

            wsc_v = [0, 0, 0]

            def combine(mi, lo, nslots):
                sci = mats[mi][3]
                p.emit("V", lambda e, lo=lo, nslots=nslots, mi=mi:
                       e.tensor_reduce(
                    colsum[:, 96 + mi:97 + mi],
                    colsum[:, lo:lo + nslots], AX, OP.add),
                    waits=[(SV, p.now(SV))], inc=SV)
                gv = p.now(SV)
                t_mm(psB[0:1, 8 + mi:9 + mi], ones_col[:],
                     colsum[:, 96 + mi:97 + mi], True, True,
                     waits=[(SV, gv), (SV, ONES_V), (SS, p.now(SS))], inc=ST)
                tv = p.now(ST)
                p.emit("S", lambda e, mi=mi: e.copy(sc11[0:1, mi:mi + 1],
                                                    psB[0:1, 8 + mi:9 + mi]),
                       waits=[(ST, tv)], inc=SS)
                sv = p.now(SS)
                t_mm(psB[:, 12 + mi:13 + mi], ones_row[:],
                     sc11[0:1, mi:mi + 1], True, True,
                     waits=[(SS, sv)], inc=ST)
                tv = p.now(ST)
                p.emit("V", lambda e, sci=sci: e.tensor_scalar(
                    wsc[:, 3 + sci:4 + sci], psB[:, 12 + sci:13 + sci],
                    1.0 / (INNER * C), 1e-5, OP.mult, OP.max),
                    waits=[(ST, tv)], inc=SV)
                p.emit("V", lambda e, sci=sci: e.reciprocal(
                    wsc[:, sci:sci + 1], wsc[:, 3 + sci:4 + sci]),
                    waits=[(SV, p.now(SV))], inc=SV)
                wsc_v[mi] = p.now(SV)

            combine(0, 0, 8)

            for mi, (md, R, Ccols, sci, dstT) in enumerate(mats):
                nit = (R * Ccols) // (128 * 4096)
                qsc = wsc[:, sci:sci + 1]
                if debug_wq:
                    continue
                for it in range(nit):
                    if mi < 2 or it == 0:
                        emit_mean_batch()
                    if mi == 1 and it == 0:
                        combine(1, 32, 32)
                    if mi == 2 and it == 0:
                        combine(2, 64, 32)
                    par = it % 2
                    dv = load_chunk(md, R, Ccols, it, par)
                    for half in range(2):
                        round_chain(scr, wbuf[:, par, half * 2048:
                                                (half + 1) * 2048],
                                    qsc, scrA[:, half * 2048:(half + 1) * 2048],
                                    2048,
                                    waits_first=[dv, (SV, wsc_v[mi])],
                                    slot=half, off=half * 2048)
                    p.emit("V", lambda e, scrA=scrA, stag=stag:
                           e.tensor_scalar(stag[:], scrA[:], 1.0, -1.0,
                                           OP.min, OP.max),
                           waits=[(ST, p.now(ST))], inc=SV)
                    guards["rcclip"] = p.now(SV)
                    last_use[par] = p.now(SS)
                    last_use_sem[par] = SS
                    vq = p.now(SV)
                    for t32 in range(32):
                        ps_t = psA if t32 < 16 else psB
                        jo = t32 % 16
                        if Ccols == C:
                            ic, rr = divmod(t32, 4)
                            src_ap = stag[:, rr * 1024 + ic * 128:
                                          rr * 1024 + ic * 128 + 128]
                        else:
                            src_ap = stag[:, t32 * 128:(t32 + 1) * 128]
                        t_mm(ps_t[:, jo * 128:(jo + 1) * 128],
                             src_ap, ident[:],
                             True, True,
                             waits=[(SV, vq), (DC, CONST_DY),
                                    (SS, p.now(SS))] if t32 == 0 else (),
                             inc=ST if t32 == 31 else None)
                    tv = p.now(ST)
                    for bkt in range(8):
                        ps_t = psA if bkt < 4 else psB
                        bo = bkt % 4
                        p.emit("S", lambda e, bkt=bkt, ps_t=ps_t, bo=bo,
                               stag2=stag2: e.copy(
                            stag2[:, bkt * 512:(bkt + 1) * 512],
                            ps_t[:, bo * 512:(bo + 1) * 512]),
                            waits=[(ST, tv), (DG, p.now(DG))] if bkt == 0
                            else (), inc=SS if bkt == 7 else None)
                    sv = p.now(SS)
                    if Ccols == C:
                        dst = dstT[it].rearrange("i p c -> p i c")
                        srcv = stag2[:].rearrange("p (i c) -> p i c", c=512)
                        p.emit("Y", lambda e, dst=dst, srcv=srcv:
                               e.dma_start(out=dst, in_=srcv),
                               waits=[(SS, sv)], inc=DG, dma=True)
                    else:
                        dst = dstT[it // 4, :, :,
                                   (it % 4) * 128:(it % 4) * 128 + 128]
                        p.emit("Y", lambda e, dst=dst, stag2=stag2:
                               e.dma_start(
                            out=dst.rearrange("i p c -> p i c"),
                            in_=stag2[:].rearrange("p (i q) -> p i q", q=128)),
                            waits=[(SS, sv)], inc=DG, dma=True)
        PRE_G = p.now(DG)
        pre_counts = [(ST, p.now(ST)), (SS, p.now(SS)), (SV, p.now(SV)),
                      (SG, p.now(SG)), (DG, p.now(DG)), (DC, p.now(DC)),
                      ("wb0", p.now("wb0")), ("wb1", p.now("wb1")),
                      ("mg", p.now("mg"))]
        for _eng in ("T", "S", "V", "G"):
            p.emit(_eng, lambda e: e.nop(), waits=pre_counts)
        if "wqrb" in taps:
            p.emit("Y", lambda e: e.dma_start(out=tap_d["wqrb"], in_=wgT_d[0]),
                   waits=[(DG, PRE_G)], inc=DO, dma=True)

        x_reader_ss = [0] * NT
        x_issue = [0] * NT
        for s in range(0 if preproc_only else nseq):
            sqAC = contextlib.ExitStack()
            with sqAC:
                xn_f = sqAC.enter_context(
                    nc.sbuf_tensor(f"xn_f{s}", [128, NC, T], BF16))
                sqAB = contextlib.ExitStack()
                with sqAB:
                    EAB = sqAB.enter_context
                    q_f = EAB(nc.sbuf_tensor(f"q_f{s}", [128, NC, T], BF16))
                    k_f = EAB(nc.sbuf_tensor(f"k_f{s}", [128, NC, T], BF16))
                    v_tm = EAB(nc.sbuf_tensor(f"v_tm{s}",
                                               [128, NT, 16 * 65], BF16))

                    sqA = contextlib.ExitStack()
                    with sqA:
                        EA = sqA.enter_context
                        xn_tm = EA(nc.sbuf_tensor(f"xn_tm{s}", [128, 1024], BF16))
                        v_fb = EA(nc.sbuf_tensor(f"v_fb{s}", [128, 1024], BF16))
                        wstr = EA(nc.sbuf_tensor(f"wstr{s}", [128, 2, 4, 128], F32))
                        wstrb = EA(nc.sbuf_tensor(f"wstrb{s}",
                                                  [128, 2, NC, 128], BF16))

                        prev_out = p.now(DO)
                        for t in range(NT):
                            xs = f"xd{t % 4}"
                            wl = [(DO, prev_out)]
                            if s == 0 and t >= 4:
                                wl.append((SS, x_reader_ss[t - 4]))
                            elif s == 1:
                                wl.append((SS, x_reader_ss[(t - 4) % NT]))
                            p.emit("Y", lambda e, t=t, s=s: e.dma_start(
                                out=x_sb[:, t, :],
                                in_=x_d.rearrange("s (n p) c -> s n p c",
                                                  p=128)[s, t]),
                                waits=wl, inc=xs, dma=True)
                            x_issue[t] = p.now(xs)
                        if "x0" in taps and s == 0:
                            p.emit("Y", lambda e: e.dma_start(
                                out=tap_d["x0"], in_=x_sb[:]),
                                waits=[(f"xd{tt2}", x_issue[tt2])
                                       for tt2 in range(NT)],
                                inc=DO, dma=True)

                        def emit_ln(t, out_ap, waits0):
                            p.emit("V", lambda e, t=t: e.reduce_sum(
                                small[:, 0:1], x_sb[:, t, :], AX), waits=waits0,
                                inc=SV)
                            p.emit("V", lambda e: e.tensor_scalar(
                                small[:, 1:2], small[:, 0:1], -1.0 / C, None,
                                OP.mult), waits=[(SV, p.now(SV))], inc=SV)
                            vv = p.now(SV)
                            p.emit("S", lambda e, t=t: e.activation(
                                psA[:, 0:1024], x_sb[:, t, :], AF.Square,
                                bias=small[:, 1:2], scale=f32(1.0)),
                                waits=[(SV, vv)], inc=SS)
                            sv = p.now(SS)
                            p.emit("V", lambda e: e.reduce_sum(
                                small[:, 2:3], psA[:, 0:1024], AX),
                                waits=[(SS, sv)], inc=SV)
                            p.emit("V", lambda e: e.tensor_scalar(
                                small[:, 3:4], small[:, 2:3], 1.0 / C, 1e-5,
                                OP.mult, OP.add), waits=[(SV, p.now(SV))],
                                inc=SV)
                            vv = p.now(SV)
                            p.emit("S", lambda e: e.activation(
                                small[:, 4:5], small[:, 3:4], AF.Sqrt),
                                waits=[(SV, vv)], inc=SS)
                            sv = p.now(SS)
                            p.emit("V", lambda e: e.reciprocal(
                                small[:, 5:6], small[:, 4:5]), waits=[(SS, sv)],
                                inc=SV)
                            p.emit("V", lambda e: e.tensor_tensor(
                                small[:, 6:7], small[:, 1:2], small[:, 5:6],
                                OP.mult), waits=[(SV, p.now(SV))], inc=SV)
                            vv = p.now(SV)
                            p.emit("S", lambda e, t=t, o=out_ap: e.activation(
                                o, x_sb[:, t, :], AF.Identity,
                                bias=small[:, 6:7], scale=small[:, 5:6]),
                                waits=[(SV, vv)], inc=SS)
                            return p.now(SS)

                        for t in range(NT):
                            sv = emit_ln(t, xn_tm[:],
                                         [(f"xd{t % 4}", x_issue[t])])
                            x_reader_ss[t] = sv
                            for j in range(8):
                                bank = j // 4
                                t_mm(psB[:, bank * 512 + (j % 4) * 128:
                                         bank * 512 + (j % 4) * 128 + 128],
                                     xn_tm[:, j * 128:(j + 1) * 128], ident[:],
                                     True, True,
                                     waits=[(SS, sv), (SV, p.now(SV)),
                                            (DC, CONST_DY)] if j == 0 else (),
                                     inc=ST if j in (3, 7) else None)
                            tv = p.now(ST)
                            p.emit("V", lambda e, t=t: e.tensor_copy(
                                xn_f[:, 0:4, t * 128:(t + 1) * 128],
                                psB[:, 0:512].rearrange("p (j c) -> p j c",
                                                        c=128)),
                                waits=[(ST, tv - 1)])
                            p.emit("V", lambda e, t=t: e.tensor_copy(
                                xn_f[:, 4:8, t * 128:(t + 1) * 128],
                                psB[:, 512:1024].rearrange("p (j c) -> p j c",
                                                           c=128)),
                                waits=[(ST, tv)], inc=SV)
                        XN_V = p.now(SV)
                        if "xn" in taps and s == 0:
                            p.emit("Y", lambda e: e.dma_start(
                                out=tap_d["xn"], in_=xn_f[:]),
                                waits=[(SV, XN_V)], inc=DO, dma=True)

                        for fo in range(24):
                            par = fo % 2
                            guard_t = guards["wstrb"][par]
                            for hf in range(2):
                                guard_g = guards["wstr"][hf]
                                wssem = f"ws{hf}"
                                p.emit("Y", lambda e, fo=fo, hf=hf, wstr=wstr:
                                       e.dma_start(
                                    out=wstr[:, hf],
                                    in_=wqkv_d.rearrange(
                                        "(c p) (f j) -> f p c j",
                                        p=128, j=128)[fo][:, hf * 4:hf * 4 + 4]),
                                    waits=[(SG, guard_g)], inc=wssem, dma=True)
                                dv = p.now(wssem)
                                p.emit("G", lambda e, par=par, hf=hf,
                                       wstrb=wstrb, wstr=wstr: e.tensor_copy(
                                    wstrb[:, par, hf * 4:hf * 4 + 4, :],
                                    wstr[:, hf]),
                                    waits=[(wssem, dv), (DG, PRE_G),
                                           (ST, guard_t)],
                                    inc=SG)
                                guards["wstr"][hf] = p.now(SG)
                            gv = p.now(SG)
                            ps = psA[:, par * 1024:par * 1024 + 1024]
                            for th in range(2):
                                sv_guard = p.now(SS)
                                for c in range(NC):
                                    for tq in range(2):
                                        wl = []
                                        if c == 0 and tq == 0:
                                            wl = [(SG, gv), (SV, XN_V),
                                                  (SS, sv_guard)]
                                        t_mm(ps[:, tq * 512:(tq + 1) * 512],
                                             wstrb[:, par, c, :],
                                             xn_f[:, c, th * 1024 + tq * 512:
                                                  th * 1024 + (tq + 1) * 512],
                                             start=(c == 0), stop=(c == NC - 1),
                                             waits=wl,
                                             inc=ST if (c == NC - 1 and tq == 1)
                                             else None)
                                tv = p.now(ST)
                                if fo < 8:
                                    dst = q_f[:, fo, th * 1024:(th + 1) * 1024]
                                elif fo < 16:
                                    dst = k_f[:, fo - 8,
                                              th * 1024:(th + 1) * 1024]
                                else:
                                    dst = v_fb[:, 0:1024]
                                p.emit("S", lambda e, dst=dst, ps=ps, fo=fo:
                                       e.activation(
                                    dst, ps[:, 0:1024], AF.Identity,
                                    bias=bqk[:, fo:fo + 1], scale=f32(1.0)),
                                    waits=[(ST, tv)] + (
                                        [(ST, p.now(ST))] if fo >= 16 else []),
                                    inc=SS)
                                if fo >= 16:
                                    sv = p.now(SS)
                                    fv = fo - 16
                                    for tb in range(2):
                                        bank = tb % 2
                                        for j in range(4):
                                            wl = []
                                            if j == 0:
                                                wl = [(SS, sv), (SV, p.now(SV))]
                                            t_mm(psB[:, bank * 512 + j * 128:
                                                     bank * 512 + j * 128 + 128],
                                                 v_fb[:, (tb * 4 + j) * 128:
                                                      (tb * 4 + j + 1) * 128],
                                                 ident[:], True, True, waits=wl,
                                                 inc=ST if j == 3 else None)
                                        tv = p.now(ST)
                                        psv = psB[:, bank * 512:
                                                   bank * 512 + 512].rearrange(
                                            "p (j c) -> p j c", c=128)
                                        for hh in range(2):
                                            p.emit("V", lambda e, th=th, tb=tb,
                                                   fv=fv, hh=hh, psv=psv:
                                                   e.tensor_copy(
                                                v_tm[:, th * 8 + tb * 4:
                                                     th * 8 + tb * 4 + 4,
                                                     (2 * fv + hh) * 65:
                                                     (2 * fv + hh) * 65 + 64],
                                                psv[:, :, hh * 64:
                                                    hh * 64 + 64]),
                                                waits=[(ST, tv)]
                                                if hh == 0 else (),
                                                inc=SV if hh == 1 else None)
                            guards["wstrb"][par] = p.now(ST)
                        QKV_S, QKV_V = p.now(SS), p.now(SV)
                        if "qf" in taps and s == 0:
                            for nm, buf in [("qf", q_f), ("kf", k_f),
                                            ("vtm", v_tm)]:
                                if nm in tap_d:
                                    p.emit("Y", lambda e, nm=nm, buf=buf:
                                           e.dma_start(out=tap_d[nm],
                                                       in_=buf[:]),
                                           waits=[(SS, QKV_S), (SV, QKV_V)],
                                           inc=DO, dma=True)

                    O_f = xn_f
                    with nc.sbuf_tensor(f"eSt{s}", [128, 2, 2, 512], BF16) \
                            as expSt, \
                         nc.sbuf_tensor(f"rsr{s}", [1, 1024], F32) as rs_row, \
                         nc.sbuf_tensor(f"rsT{s}", [64, 1024], F32) as rsT:
                        ones_dst = v_tm[:].rearrange("p k (h c) -> p k h c",
                                                     c=65)
                        for half in range(2):
                            p.emit("S", lambda e, half=half,
                                   ones_dst=ones_dst: e.activation(
                                ones_dst[:, half * 8:half * 8 + 8, :, 64:65],
                                ident[:].rearrange("p (a b c) -> p a b c",
                                                   a=8, b=16, c=1),
                                AF.Identity, bias=f32(1.0), scale=f32(0.0)),
                                waits=[(DC, CONST_DY), (SV, p.now(SV)),
                                       (ST, p.now(ST)), (SG, p.now(SG))]
                                if half == 0 else (),
                                inc=SS if half == 1 else None)

                        def stb(b):
                            t = psA if b < 2 else psB
                            return t[:, (b % 2) * 512:(b % 2) * 512 + 512]

                        rsTile = [psB[0:64, 512:1024],
                                  psB[64:128, 512:1024]]

                        def ob_ap(b, p0, p1):
                            t = psA if b < 2 else psB
                            c0 = 1024 + (b % 2) * 512
                            return t[p0:p1, c0:c0 + 512]

                        st_ss = [0, 0, 0]
                        exp_free = [[0, 0], [0, 0]]
                        o_free = [(0, 0)] * 4
                        rep_st = [0, 0]
                        recip_sv = [0, 0]
                        kbctr = 0
                        items = [(qt, pr) for qt in range(4) for pr in range(8)]
                        epi = None
                        for idx in range(len(items) + 1):
                            cur = None
                            if idx < len(items):
                                qt, pr = items[idx]
                                kbmax = 4 * qt + 3
                                obs = (2 * (idx % 2), 2 * (idx % 2) + 1)
                                heads = (2 * pr, 2 * pr + 1)
                                def emit_attv(t2, h, pkb, ess):
                                    n0 = max(0, pkb - 4 * qt) * 128
                                    awl = [(SS, ess)]
                                    if pkb == 0:
                                        awl += [(SV, o_free[obs[t2]][0]),
                                                (SS, o_free[obs[t2]][1])]
                                    t_mm(ob_ap(obs[t2], 0, 65)[:, n0:512],
                                         v_tm[:, pkb, 65 * h:65 * h + 65],
                                         expSt[:, t2, pkb % 2, n0:512],
                                         start=(pkb == 0),
                                         stop=(pkb == kbmax),
                                         waits=awl, inc=ST)
                                    exp_free[t2][pkb % 2] = p.now(ST)

                                pend = None
                                for kb in range(kbmax + 1):
                                    j = kb - 4 * qt
                                    n0 = max(0, j) * 128
                                    this_kb = []
                                    for t2, h in enumerate(heads):
                                        hc, hb = h // 2, (h % 2) * 64
                                        sb_ = kbctr % 3
                                        kbctr += 1
                                        wl = [(SS, st_ss[sb_])]
                                        if idx == 0 and kb == 0 and t2 == 0:
                                            wl += [(SS, QKV_S), (SV, QKV_V)]
                                        t_mm(stb(sb_)[:, n0:512],
                                             k_f[hb:hb + 64, hc,
                                                 kb * 128:(kb + 1) * 128],
                                             q_f[hb:hb + 64, hc,
                                                 qt * 512 + n0:(qt + 1) * 512],
                                             True, True, waits=wl, inc=ST)
                                        stv = p.now(ST)
                                        ewl = [(ST, stv),
                                               (ST, exp_free[t2][kb % 2])]
                                        if j >= 0:
                                            p.emit("V", lambda e, sb_=sb_,
                                                   n0=n0: e.tensor_tensor(
                                                stb(sb_)[:, n0:n0 + 128],
                                                stb(sb_)[:, n0:n0 + 128],
                                                mask[:], OP.add),
                                                waits=[(ST, stv),
                                                       (DC, CONST_DY)],
                                                inc=SV)
                                            ewl.append((SV, p.now(SV)))
                                        p.emit("S", lambda e, sb_=sb_, n0=n0,
                                               kb=kb, t2=t2,
                                               expSt=expSt: e.activation(
                                            expSt[:, t2, kb % 2, n0:512],
                                            stb(sb_)[:, n0:512],
                                            AF.Exp, bias=f32(EXP_SHIFT),
                                            scale=f32(0.125)),
                                            waits=ewl, inc=SS)
                                        st_ss[sb_] = p.now(SS)
                                        this_kb.append((t2, h, kb,
                                                        p.now(SS)))
                                    if pend:
                                        for args in pend:
                                            emit_attv(*args)
                                    pend = this_kb
                                for args in pend:
                                    emit_attv(*args)
                                cur = (qt, heads, obs, p.now(ST))
                            if epi is not None:
                                eqt, eheads, eobs, etv = epi
                                rs_ss2 = [0, 0]
                                for t2 in range(2):
                                    p.emit("S", lambda e, eb=eobs[t2], t2=t2,
                                           rs_row=rs_row: e.copy(
                                        rs_row[0:1, t2 * 512:t2 * 512 + 512],
                                        ob_ap(eb, 64, 65)),
                                        waits=[(ST, etv), (ST, rep_st[t2])],
                                        inc=SS)
                                    rs_ss2[t2] = p.now(SS)
                                for t2 in range(2):
                                    t_mm(rsTile[t2], ones_row[0:1, 0:64],
                                         rs_row[0:1, t2 * 512:t2 * 512 + 512],
                                         True, True,
                                         waits=[(SS, rs_ss2[t2]),
                                                (SV, recip_sv[t2])],
                                         inc=ST,
                                         tile_position=(0, 64 * t2))
                                    rep_st[t2] = p.now(ST)
                                for t2 in range(2):
                                    p.emit("V", lambda e, t2=t2, rsT=rsT:
                                           e.reciprocal(
                                        rsT[0:64, t2 * 512:t2 * 512 + 512],
                                        rsTile[t2]),
                                        waits=[(ST, rep_st[t2])], inc=SV)
                                    recip_sv[t2] = p.now(SV)
                                for t2, eh in enumerate(eheads):
                                    ehc, ehb = eh // 2, (eh % 2) * 64
                                    eb = eobs[t2]
                                    p.emit("V", lambda e, ehb=ehb, ehc=ehc,
                                           eqt=eqt, eb=eb, t2=t2, rsT=rsT:
                                           e.tensor_tensor(
                                        O_f[ehb:ehb + 64, ehc,
                                            eqt * 512:(eqt + 1) * 512],
                                        ob_ap(eb, 0, 64),
                                        rsT[0:64, t2 * 512:t2 * 512 + 512],
                                        OP.mult),
                                        waits=[(SV, recip_sv[t2])], inc=SV)
                                    o_free[eb] = (p.now(SV), rs_ss2[t2])
                            epi = cur
                    ATT_S = p.now(SS)
                    if "of" in taps and s == 0:
                        p.emit("Y", lambda e: e.dma_start(
                            out=tap_d["of"], in_=O_f[:]),
                            waits=[(SS, ATT_S), (SV, p.now(SV))],
                            inc=DO, dma=True)

                with nc.sbuf_tensor(f"wpj{s}", [128, NC, C], BF16) as wpj, \
                     nc.sbuf_tensor(f"wps{s}", [128, 2, C], F32) as wps, \
                     nc.sbuf_tensor(f"bpr{s}", [1, C], F32) as bproj_row:
                    p.emit("Y", lambda e, bproj_row=bproj_row: e.dma_start(
                        out=bproj_row[:],
                        in_=bproj_d.unsqueeze(0)),
                        waits=[(SS, ATT_S)], inc="bp", dma=True)
                    BPR_DY = p.now("bp")
                    for c in range(NC):
                        par = c % 2
                        guard = guards["wps"][par]
                        wpsem = f"wp{par}"
                        p.emit("Y", lambda e, c=c, par=par, wps=wps: e.dma_start(
                            out=wps[:, par, :],
                            in_=wproj_d[c * 128:(c + 1) * 128, :]),
                            waits=[(SG, guard), (SS, ATT_S)], inc=wpsem,
                            dma=True)
                        dv = p.now(wpsem)
                        p.emit("G", lambda e, c=c, par=par, wpj=wpj, wps=wps:
                               e.tensor_copy(
                            wpj[:, c, :], wps[:, par, :]),
                            waits=[(wpsem, dv)], inc=SG)
                        guards["wps"][par] = p.now(SG)
                    WPJ_G = p.now(SG)
                    for t in range(NT):
                        par = t % 2
                        ps = psA[:, par * 1024:par * 1024 + 1024]
                        vres = p.now(SV)
                        for c in range(NC):
                            for half in range(2):
                                wl = []
                                if c == 0 and half == 0:
                                    wl = [(SG, WPJ_G), (SS, ATT_S), (SV, vres),
                                          ("bp", BPR_DY)]
                                t_mm(ps[:, half * 512:(half + 1) * 512],
                                     O_f[:, c, t * 128:(t + 1) * 128],
                                     wpj[:, c, half * 512:(half + 1) * 512],
                                     start=(c == 0), stop=False, waits=wl)
                        for half in range(2):
                            t_mm(ps[:, half * 512:(half + 1) * 512],
                                 ones_row[:, 0:128],
                                 bproj_row[0:1, half * 512:(half + 1) * 512],
                                 start=False, stop=True,
                                 inc=ST if half == 1 else None)
                        tv = p.now(ST)
                        p.emit("V", lambda e, t=t, ps=ps: e.tensor_tensor(
                            x_sb[:, t, :], x_sb[:, t, :], ps[:, 0:1024], OP.add),
                            waits=[(ST, tv)], inc=SV)
                PROJ_V = p.now(SV)
                if "res1" in taps and s == 0:
                    p.emit("Y", lambda e: e.dma_start(
                        out=tap_d["res1"], in_=x_sb[:]),
                        waits=[(SV, PROJ_V)], inc=DO, dma=True)
            TAP_RES1_DO = p.now(DO)

            if not do_ffn:
                continue
            sqD = contextlib.ExitStack()
            with sqD:
                ED = sqD.enter_context
                scrV = ED(nc.sbuf_tensor(f"scrVd{s}", [128, 1024], F32))
                scrI = ED(nc.sbuf_tensor(f"scrId{s}", [128, 1024], I32))
                scrR = ED(nc.sbuf_tensor(f"scrRd{s}", [128, 1024], F32))
                scrA = ED(nc.sbuf_tensor(f"scrAd{s}", [128, 1024], F32))
                sil = ED(nc.sbuf_tensor(f"sild{s}", [128, 512], F32))
                xn_tm = ED(nc.sbuf_tensor(f"xntmd{s}", [128, 2, 1024], BF16))
                st1 = ED(nc.sbuf_tensor(f"st1d{s}", [128, 8, 8], F32))
                st2 = ED(nc.sbuf_tensor(f"st2d{s}", [128, 8, 8], F32))
                diag2 = ED(nc.sbuf_tensor(f"dg2d{s}", [128, 2, 128], BF16))
                xqf = ED(nc.sbuf_tensor(f"xqf{s}", [128, NC, 1024], BF16))
                gu = ED(nc.sbuf_tensor(f"gu{s}", [128, 8, INNER], BF16))
                wgb = ED(nc.sbuf_tensor(f"wgb{s}", [128, 2, NC, 512], BF16))
                wub = ED(nc.sbuf_tensor(f"wub{s}", [128, 2, NC, 512], BF16))
                wdb = ED(nc.sbuf_tensor(f"wdb{s}", [128, 2, 512], BF16))
                scrF = scrI[:].bitcast(F32)
                scr = (scrV, scrI, scrF, scrR)

                TAP_G3_DO = p.now(DO)

                def emit_ln2(t, out_ap, waits0):
                    p.emit("V", lambda e, t=t: e.reduce_sum(
                        small[:, 0:1], x_sb[:, t, :], AX), waits=waits0,
                        inc=SV)
                    p.emit("V", lambda e: e.tensor_scalar(
                        small[:, 1:2], small[:, 0:1], -1.0 / C, None, OP.mult),
                        waits=[(SV, p.now(SV))], inc=SV)
                    vv = p.now(SV)
                    p.emit("S", lambda e, t=t: e.activation(
                        psA[:, 0:1024], x_sb[:, t, :], AF.Square,
                        bias=small[:, 1:2], scale=f32(1.0)),
                        waits=[(SV, vv)], inc=SS)
                    sv = p.now(SS)
                    p.emit("V", lambda e: e.reduce_sum(
                        small[:, 2:3], psA[:, 0:1024], AX), waits=[(SS, sv)],
                        inc=SV)
                    p.emit("V", lambda e: e.tensor_scalar(
                        small[:, 3:4], small[:, 2:3], 1.0 / C, 1e-5, OP.mult,
                        OP.add), waits=[(SV, p.now(SV))], inc=SV)
                    vv = p.now(SV)
                    p.emit("S", lambda e: e.activation(
                        small[:, 4:5], small[:, 3:4], AF.Sqrt),
                        waits=[(SV, vv)], inc=SS)
                    sv = p.now(SS)
                    p.emit("V", lambda e: e.reciprocal(small[:, 5:6],
                                                       small[:, 4:5]),
                           waits=[(SS, sv)], inc=SV)
                    p.emit("V", lambda e: e.tensor_tensor(
                        small[:, 6:7], small[:, 1:2], small[:, 5:6], OP.mult),
                        waits=[(SV, p.now(SV))], inc=SV)
                    vv = p.now(SV)
                    p.emit("S", lambda e, t=t, o=out_ap: e.activation(
                        o, x_sb[:, t, :], AF.Identity,
                        bias=small[:, 6:7], scale=small[:, 5:6]),
                        waits=[(SV, vv)], inc=SS)
                    return p.now(SS)

                xq_ready = [0] * 8
                for hh in range(2):
                    if dlim < 1:
                        continue
                    wv = [[0] * 8 for _ in range(14)]
                    for tl in range(8):
                        t = hh * 8 + tl
                        w0 = [(SV, PROJ_V)] if (tl == 0 and hh == 0) else ()
                        p.emit("V", lambda e, t=t, tl=tl: e.reduce_sum(
                            st1[:, 0, tl:tl + 1], x_sb[:, t, :], AX),
                            waits=w0, inc=SV)
                        p.emit("V", lambda e, t=t, tl=tl: e.tensor_reduce(
                            st1[:, 7, tl:tl + 1], x_sb[:, t, :], AX, OP.max),
                            inc=SV)
                        p.emit("V", lambda e, t=t, tl=tl: e.tensor_reduce(
                            st2[:, 0, tl:tl + 1], x_sb[:, t, :], AX, OP.min),
                            inc=SV)
                        wv[0][tl] = p.now(SV)
                    for tl in range(8):
                        p.emit("V", lambda e, tl=tl: e.tensor_scalar(
                            st1[:, 1, tl:tl + 1], st1[:, 0, tl:tl + 1],
                            -1.0 / C, None, OP.mult),
                            waits=[(SV, wv[0][tl])], inc=SV)
                        wv[1][tl] = p.now(SV)
                    for tl in range(8):
                        t = hh * 8 + tl
                        p.emit("S", lambda e, t=t, tl=tl: e.activation(
                            scrV[:], x_sb[:, t, :], AF.Square,
                            bias=st1[:, 1, tl:tl + 1], scale=f32(1.0),
                            accum_out=st1[:, 2, tl:tl + 1]),
                            waits=[(SV, wv[1][tl])], inc=SS)
                        wv[2][tl] = p.now(SS)
                    for tl in range(8):
                        p.emit("V", lambda e, tl=tl: e.tensor_scalar(
                            st1[:, 3, tl:tl + 1], st1[:, 2, tl:tl + 1],
                            1.0 / C, 1e-5, OP.mult, OP.add),
                            waits=[(SS, wv[2][tl])], inc=SV)
                        wv[3][tl] = p.now(SV)
                    for tl in range(8):
                        p.emit("S", lambda e, tl=tl: e.activation(
                            st1[:, 4, tl:tl + 1], st1[:, 3, tl:tl + 1],
                            AF.Sqrt), waits=[(SV, wv[3][tl])], inc=SS)
                        wv[4][tl] = p.now(SS)
                    for tl in range(8):
                        p.emit("V", lambda e, tl=tl: e.reciprocal(
                            st1[:, 5, tl:tl + 1], st1[:, 4, tl:tl + 1]),
                            waits=[(SS, wv[4][tl])], inc=SV)
                        wv[5][tl] = p.now(SV)
                    for tl in range(8):
                        p.emit("V", lambda e, tl=tl: e.tensor_tensor(
                            st1[:, 6, tl:tl + 1], st1[:, 1, tl:tl + 1],
                            st1[:, 5, tl:tl + 1], OP.mult),
                            waits=[(SV, wv[5][tl])], inc=SV)
                        wv[6][tl] = p.now(SV)
                    for tl in range(8):
                        p.emit("V", lambda e, tl=tl: e.tensor_tensor(
                            st2[:, 1, tl:tl + 1], st1[:, 7, tl:tl + 1],
                            st1[:, 1, tl:tl + 1], OP.add),
                            waits=[(SV, wv[1][tl])])
                        p.emit("V", lambda e, tl=tl: e.tensor_tensor(
                            st2[:, 2, tl:tl + 1], st2[:, 0, tl:tl + 1],
                            st1[:, 1, tl:tl + 1], OP.add), inc=SV)
                        wv[7][tl] = p.now(SV)
                    for tl in range(8):
                        p.emit("V", lambda e, tl=tl: e.tensor_scalar(
                            st2[:, 3, tl:tl + 1], st2[:, 2, tl:tl + 1],
                            -1.0, None, OP.mult),
                            waits=[(SV, wv[7][tl])], inc=SV)
                        wv[8][tl] = p.now(SV)
                    for tl in range(8):
                        p.emit("V", lambda e, tl=tl: e.tensor_tensor(
                            st2[:, 4, tl:tl + 1], st2[:, 1, tl:tl + 1],
                            st2[:, 3, tl:tl + 1], OP.max),
                            waits=[(SV, wv[8][tl])], inc=SV)
                        wv[9][tl] = p.now(SV)
                    for tl in range(8):
                        p.emit("V", lambda e, tl=tl: e.tensor_scalar(
                            st2[:, 5, tl:tl + 1], st2[:, 4, tl:tl + 1],
                            st1[:, 5, tl:tl + 1], 1e-5, OP.mult, OP.max),
                            waits=[(SV, wv[9][tl]), (SV, wv[5][tl])], inc=SV)
                        wv[10][tl] = p.now(SV)
                    for tl in range(8):
                        p.emit("V", lambda e, tl=tl: e.reciprocal(
                            st2[:, 6, tl:tl + 1], st2[:, 5, tl:tl + 1]),
                            waits=[(SV, wv[10][tl])], inc=SV)
                        wv[11][tl] = p.now(SV)
                    for tl in range(8):
                        p.emit("V", lambda e, tl=tl: e.tensor_scalar(
                            st2[:, 7, tl:tl + 1], st2[:, 6, tl:tl + 1],
                            127.0, None, OP.mult),
                            waits=[(SV, wv[11][tl])], inc=SV)
                        wv[12][tl] = p.now(SV)
                    for tl in range(8):
                        p.emit("V", lambda e, tl=tl: e.tensor_scalar(
                            invsx[:, tl:tl + 1], st2[:, 5, tl:tl + 1],
                            1.0 / 127.0, None, OP.mult),
                            waits=[(SV, wv[10][tl])], inc=SV)
                        wv[13][tl] = p.now(SV)
                    tvp = [0, 0]
                    for tl in range(8):
                        t = hh * 8 + tl
                        par = tl % 2
                        p.emit("S", lambda e, t=t, tl=tl: e.activation(
                            scrA[:], x_sb[:, t, :], AF.Identity,
                            bias=st1[:, 6, tl:tl + 1],
                            scale=st1[:, 5, tl:tl + 1]),
                            waits=[(SV, wv[6][tl])], inc=SS)
                        for half in range(2):
                            round_chain(scr, scrA[:, half * 512:
                                                 (half + 1) * 512],
                                        st2[:, 7, tl:tl + 1],
                                        xn_tm[:, par, half * 512:
                                              (half + 1) * 512],
                                        512,
                                        waits_first=[(SV, wv[12][tl]),
                                                     (ST, tvp[par])],
                                        slot=half, off=half * 512)
                        p.emit("V", lambda e, tl=tl, par=par: e.tensor_scalar(
                            diag2[:, par], ident[:], invsx[:, tl:tl + 1],
                            None, OP.mult),
                            waits=[(SV, wv[13][tl]), (ST, tvp[par])], inc=SV)
                        vq = p.now(SV)
                        for j in range(8):
                            bank = j // 4
                            t_mm(psB[:, bank * 512 + (j % 4) * 128:
                                     bank * 512 + (j % 4) * 128 + 128],
                                 xn_tm[:, par, j * 128:(j + 1) * 128],
                                 diag2[:, par],
                                 True, True,
                                 waits=[(SV, vq)] if j == 0 else (),
                                 inc=ST if j in (3, 7) else None)
                        tv = p.now(ST)
                        tvp[par] = tv
                        p.emit("V", lambda e, tl=tl: e.tensor_copy(
                            xqf[:, 0:4, tl * 128:(tl + 1) * 128],
                            psB[:, 0:512].rearrange("p (j c) -> p j c",
                                                    c=128)),
                            waits=[(ST, tv - 1)])
                        p.emit("V", lambda e, tl=tl: e.tensor_copy(
                            xqf[:, 4:8, tl * 128:(tl + 1) * 128],
                            psB[:, 512:1024].rearrange("p (j c) -> p j c",
                                                       c=128)),
                            waits=[(ST, tv)], inc=SV)
                        xq_ready[tl] = p.now(SV)
                    D1_V = p.now(SV)
                    if dlim < 2:
                        continue
                    if "xqf" in taps and s == 0:
                        p.emit("Y", lambda e, hh=hh: e.dma_start(
                            out=tap_d["xqf"][hh], in_=xqf[:]),
                            waits=[(SV, D1_V)], inc=DO, dma=True)

                    for oc in range(8):
                        par = oc % 2
                        guard = guards["wgu"][par]
                        p.emit("Y", lambda e, oc=oc, par=par, wgb=wgb:
                               e.dma_start(
                            out=wgb[:, par],
                            in_=wgT_d[oc].rearrange("i p c -> p i c")),
                            waits=[(ST, guard)], inc=f"wg{par}", dma=True)
                        p.emit("Y", lambda e, oc=oc, par=par, wub=wub:
                               e.dma_start(
                            out=wub[:, par],
                            in_=wuT_d[oc].rearrange("i p c -> p i c")),
                            inc=f"wu{par}", dma=True)
                        dg = [(f"wg{par}", p.now(f"wg{par}")),
                              (f"wu{par}", p.now(f"wu{par}"))]
                        for tl in range(8):
                            tpar = tl % 2
                            gps = psA[:, tpar * 512:tpar * 512 + 512]
                            ups = psA[:, 1024 + tpar * 512:
                                      1024 + tpar * 512 + 512]
                            eg = p.now(SV)
                            for i in range(NC):
                                wl = []
                                if i == 0:
                                    wl = dg + [(SV, xq_ready[tl]), (SV, eg),
                                               (DO, TAP_G3_DO)]
                                t_mm(gps, xqf[:, i, tl * 128:(tl + 1) * 128],
                                     wgb[:, par, i, :],
                                     start=(i == 0), stop=(i == NC - 1),
                                     waits=wl)
                                t_mm(ups, xqf[:, i, tl * 128:(tl + 1) * 128],
                                     wub[:, par, i, :],
                                     start=(i == 0), stop=(i == NC - 1),
                                     inc=ST if i == NC - 1 else None)
                            tv = p.now(ST)
                            p.emit("S", lambda e, gps=gps: e.activation(
                                sil[:], gps, AF.Silu,
                                bias=f32(0.0), scale=wsc[:, 3:4]),
                                waits=[(ST, tv), (SV, p.now(SV))], inc=SS)
                            sv = p.now(SS)
                            p.emit("V", lambda e, ups=ups, tl=tl, oc=oc:
                                   e.scalar_tensor_tensor(
                                gu[:, tl, oc * 512:(oc + 1) * 512], ups,
                                wsc[:, 4:5], sil[:], OP.mult, OP.mult),
                                waits=[(SS, sv)], inc=SV)
                        guards["wgu"][par] = p.now(ST)
                    D2_V = p.now(SV)
                    if dlim < 3:
                        continue
                    if "gu" in taps and s == 0:
                        p.emit("Y", lambda e, hh=hh: e.dma_start(
                            out=tap_d["gu"][hh], in_=gu[:]),
                            waits=[(SV, D2_V)], inc=DO, dma=True)
                    TAP_GU_DO = p.now(DO)

                    aw = [[0] * 8 for _ in range(6)]
                    for tl in range(8):
                        p.emit("V", lambda e, tl=tl: e.tensor_reduce(
                            st1[:, 0, tl:tl + 1], gu[:, tl, :], AX, OP.max,
                            apply_absolute_value=True),
                            waits=[(SV, D2_V), (DO, TAP_GU_DO)]
                            if tl == 0 else (), inc=SV)
                        aw[0][tl] = p.now(SV)
                    for tl in range(8):
                        p.emit("V", lambda e, tl=tl: e.tensor_scalar(
                            st1[:, 1, tl:tl + 1], st1[:, 0, tl:tl + 1],
                            1e-5, None, OP.max),
                            waits=[(SV, aw[0][tl])], inc=SV)
                        aw[1][tl] = p.now(SV)
                    for tl in range(8):
                        p.emit("V", lambda e, tl=tl: e.reciprocal(
                            st1[:, 2, tl:tl + 1], st1[:, 1, tl:tl + 1]),
                            waits=[(SV, aw[1][tl])], inc=SV)
                        aw[2][tl] = p.now(SV)
                    for tl in range(8):
                        p.emit("V", lambda e, tl=tl: e.tensor_scalar(
                            st1[:, 3, tl:tl + 1], st1[:, 2, tl:tl + 1],
                            127.0, None, OP.mult),
                            waits=[(SV, aw[2][tl])], inc=SV)
                        aw[3][tl] = p.now(SV)
                    for tl in range(8):
                        p.emit("V", lambda e, tl=tl: e.tensor_scalar(
                            st1[:, 4, tl:tl + 1], st1[:, 1, tl:tl + 1],
                            1.0 / 127.0, None, OP.mult),
                            waits=[(SV, aw[1][tl])], inc=SV)
                        aw[4][tl] = p.now(SV)
                    for tl in range(8):
                        p.emit("V", lambda e, tl=tl: e.tensor_tensor(
                            st1[:, 5, tl:tl + 1], st1[:, 4, tl:tl + 1],
                            wsc[:, 5:6], OP.mult),
                            waits=[(SV, aw[4][tl])], inc=SV)
                        aw[5][tl] = p.now(SV)
                    tvp3 = [0, 0]
                    for tl in range(8):
                        par = tl % 2
                        for cc in range(8):
                            src2 = gu[:, tl, cc * 512:(cc + 1) * 512]
                            round_chain(scr, src2, st1[:, 3, tl:tl + 1],
                                        src2, 512,
                                        waits_first=[(SV, aw[3][tl])]
                                        if cc == 0 else (), slot=cc % 2,
                                        off=(cc % 2) * 512)
                        p.emit("V", lambda e, tl=tl, par=par: e.tensor_scalar(
                            diag2[:, par], ident[:], st1[:, 5, tl:tl + 1],
                            None, OP.mult),
                            waits=[(SV, aw[5][tl]), (ST, tvp3[par])], inc=SV)
                        vq = p.now(SV)
                        for j in range(NI):
                            ps_t = psA if j < 16 else psB
                            jo = j % 16
                            t_mm(ps_t[:, jo * 128:(jo + 1) * 128],
                                 gu[:, tl, j * 128:(j + 1) * 128],
                                 diag2[:, par],
                                 True, True,
                                 waits=[(SV, vq), (SS, p.now(SS))]
                                 if j == 0 else (),
                                 inc=ST if j == NI - 1 else None)
                        tv = p.now(ST)
                        tvp3[par] = tv
                        for bkt in range(8):
                            ps_t = psA if bkt < 4 else psB
                            bo = bkt % 4
                            p.emit("V", lambda e, tl=tl, bkt=bkt, ps_t=ps_t,
                                   bo=bo: e.tensor_copy(
                                gu[:, tl, bkt * 512:(bkt + 1) * 512],
                                ps_t[:, bo * 512:(bo + 1) * 512]),
                                waits=[(ST, tv)] if bkt == 0 else (),
                                inc=SV if bkt == 7 else None)
                    D3_V = p.now(SV)
                    if dlim < 4:
                        continue
                    if "gud3" in taps and s == 0:
                        p.emit("Y", lambda e, hh=hh: e.dma_start(
                            out=tap_d["gud3"][hh], in_=gu[:]),
                            waits=[(SV, D3_V)], inc=DO, dma=True)
                    TAP_G3_DO = p.now(DO)

                    for ch2 in range(2):
                        for i in range(NI):
                            par = i % 2
                            guard = guards["wdb"][par]
                            p.emit("Y", lambda e, ch2=ch2, i=i, par=par,
                                   wdb=wdb:
                                   e.dma_start(out=wdb[:, par],
                                               in_=wdT_d[ch2, i]),
                                   waits=[(ST, guard)], inc=f"wd{par}",
                                   dma=True)
                            dg = (f"wd{par}", p.now(f"wd{par}"))
                            for tl in range(8):
                                ps = (psA if tl < 4 else psB)[
                                    :, (tl % 4) * 512:(tl % 4) * 512 + 512]
                                wl = []
                                if i == 0 and tl == 0:
                                    wl = [dg, (SV, D3_V),
                                          (SV, p.now(SV)), (DO, TAP_RES1_DO)]
                                elif tl == 0:
                                    wl = [dg]
                                t_mm(ps, gu[:, tl, i * 128:(i + 1) * 128],
                                     wdb[:, par, :],
                                     start=(i == 0), stop=(i == NI - 1),
                                     waits=wl, inc=ST if tl == 7 else None)
                            guards["wdb"][par] = p.now(ST)
                        tv = p.now(ST)
                        for tl in range(8):
                            t = hh * 8 + tl
                            ps = (psA if tl < 4 else psB)[
                                :, (tl % 4) * 512:(tl % 4) * 512 + 512]
                            p.emit("V", lambda e, t=t, ps=ps, ch2=ch2:
                                   e.tensor_tensor(
                                x_sb[:, t, ch2 * 512:(ch2 + 1) * 512],
                                x_sb[:, t, ch2 * 512:(ch2 + 1) * 512],
                                ps, OP.add),
                                waits=[(ST, tv)] if tl == 0 else (),
                                inc=SV if tl == 7 else None)
                    D4_V = p.now(SV)
                    for tl in range(8):
                        t = hh * 8 + tl
                        if skip_store:
                            continue
                        p.emit("Y", lambda e, t=t, s=s: e.dma_start(
                            out=y_d.rearrange("s (n p) c -> s n p c",
                                              p=128)[s, t],
                            in_=x_sb[:, t, :]),
                            waits=[(SV, D4_V)], inc=DO, dma=True)

        for _i in range(pad_nops):
            p.emit("T", lambda e: e.nop())
        for _i in range(pad_dmas):
            p.emit("Y", lambda e: e.dma_start(out=small[:, 0:1],
                                              in_=small[:, 1:2]),
                   inc=DY, dma=True)
        if "wqrb2" in taps:
            p.emit("Y", lambda e: e.dma_start(out=tap_d["wqrb2"], in_=wgT_d[0]),
                   inc=DO, dma=True)
        if "mark" in taps:
            p.emit("Y", lambda e: e.dma_start(out=tap_d["mark"],
                                              in_=wgT_d[7, 7, :, 0:128]),
                   inc=DO, dma=True)
        total_out = p.now(DO)
        p.emit("Y", lambda e: e.nop(), waits=[(DO, total_out)])

        with nc.Block() as blk:
            p.build(blk)

    return nc


def kernel(L0_attn_x, L0_attn_w_qkv, L0_attn_b_qkv, L0_attn_w_proj,
           L0_attn_b_proj, L0_ffn_w_gate, L0_ffn_w_up, L0_ffn_w_down):
    x = np.ascontiguousarray(np.asarray(L0_attn_x, dtype=np.float32))
    wqkv = np.ascontiguousarray(np.asarray(L0_attn_w_qkv, dtype=np.float32))
    bqkv = np.ascontiguousarray(np.asarray(L0_attn_b_qkv, dtype=np.float32))
    wproj = np.ascontiguousarray(np.asarray(L0_attn_w_proj, dtype=np.float32))
    bproj = np.ascontiguousarray(np.asarray(L0_attn_b_proj, dtype=np.float32))
    wg = np.ascontiguousarray(np.asarray(L0_ffn_w_gate, dtype=np.float32))
    wu = np.ascontiguousarray(np.asarray(L0_ffn_w_up, dtype=np.float32))
    wd = np.ascontiguousarray(np.asarray(L0_ffn_w_down, dtype=np.float32))

    nc = build_program()
    in_maps = []
    for c in range(NCORE):
        in_maps.append({
            "x": x[c * SEQ_PER_CORE:(c + 1) * SEQ_PER_CORE],
            "w_qkv": wqkv, "b_qkv": bqkv, "w_proj": wproj, "b_proj": bproj,
            "w_gate": wg, "w_up": wu, "w_down": wd,
        })
    res = run_bass_kernel_spmd(nc, in_maps, core_ids=list(range(NCORE)))
    out = np.concatenate([r["y"] for r in res.results], axis=0)
    return out.astype(np.float32)


# version 13 (final)
# speedup vs baseline: 1.0715x; 1.0493x over previous
import contextlib
from collections import defaultdict

import numpy as np
import ml_dtypes

import concourse.bass as bass
import concourse.mybir as mybir
from concourse.bass_utils import run_bass_kernel_spmd

F32 = mybir.dt.float32
BF16 = mybir.dt.bfloat16
I32 = mybir.dt.int32
AX = mybir.AxisListType.X
OP = mybir.AluOpType
AF = mybir.ActivationFunctionType

f32 = np.float32

B, T, C = 16, 2048, 1024
NCORE = 8
SEQ_PER_CORE = B // NCORE
H, D = 16, 64
INNER = 4096
NT = T // 128
NC = C // 128
NI = INNER // 128
EXP_SHIFT = -4.0


class Prog:

    def __init__(self, nc):
        self.nc = nc
        self.ops = []
        self.cnt = defaultdict(int)
        self._sem_handles = {}

    def sem(self, name):
        if name not in self._sem_handles:
            self._sem_handles[name] = self.nc.alloc_semaphore(name=name)
        return self._sem_handles[name]

    def now(self, sem):
        return self.cnt[sem]

    def emit(self, eng, fn, waits=(), inc=None, dma=False):
        incs = []
        if inc is not None:
            n = 16 if dma else 1
            self.cnt[inc] += n
            incs.append((inc, n))
        self.ops.append((eng, fn, tuple((s, v) for s, v in waits if v > 0),
                         tuple(incs)))

    def build(self, block):
        def make_runner(tag):
            def run(eng):
                watermark = defaultdict(int)
                for e, fn, waits, incs in self.ops:
                    if e != tag:
                        continue
                    for s, v in waits:
                        if v > watermark[s]:
                            eng.wait_ge(self.sem(s), v)
                            watermark[s] = v
                    ins = fn(eng)
                    for s, n in incs:
                        ins = ins.then_inc(self.sem(s), n)
            return run

        block.tensor(make_runner("T"))
        block.scalar(make_runner("S"))
        block.vector(make_runner("V"))
        block.gpsimd(make_runner("G"))
        block.sync(make_runner("Y"))


def ceil_div(a, b):
    return (a + b - 1) // b


def build_program(taps=(), debug_wq=False, preproc_only=False, drc=True, pad_nops=0, pad_dmas=0, nseq=SEQ_PER_CORE, nheads=H, do_ffn=True, skip_store=False, dlim=4):
    taps = set(taps)
    nc = bass.Bass("TRN2", target_bir_lowering=False, detect_race_conditions=drc)

    x_d = nc.dram_tensor("x", [SEQ_PER_CORE, T, C], F32, kind="ExternalInput").ap()
    y_d = nc.dram_tensor("y", [SEQ_PER_CORE, T, C], F32, kind="ExternalOutput").ap()
    wqkv_d = nc.dram_tensor("w_qkv", [C, 3 * C], F32, kind="ExternalInput").ap()
    bqkv_d = nc.dram_tensor("b_qkv", [3 * C], F32, kind="ExternalInput").ap()
    wproj_d = nc.dram_tensor("w_proj", [C, C], F32, kind="ExternalInput").ap()
    bproj_d = nc.dram_tensor("b_proj", [C], F32, kind="ExternalInput").ap()
    wg_d = nc.dram_tensor("w_gate", [INNER, C], F32, kind="ExternalInput").ap()
    wu_d = nc.dram_tensor("w_up", [INNER, C], F32, kind="ExternalInput").ap()
    wd_d = nc.dram_tensor("w_down", [C, INNER], F32, kind="ExternalInput").ap()

    wk = "ExternalInput" if debug_wq else (
        "ExternalOutput" if "wq" in taps else "Internal")
    wgT_d = nc.dram_tensor("wgT", [8, NC, 128, 512], BF16, kind=wk).ap()
    wuT_d = nc.dram_tensor("wuT", [8, NC, 128, 512], BF16, kind=wk).ap()
    wdT_d = nc.dram_tensor("wdT", [2, NI, 128, 512], BF16, kind=wk).ap()

    tap_d = {}
    def decl_tap(name, shape, dt):
        if name in taps:
            tap_d[name] = nc.dram_tensor("tap_" + name, shape, dt,
                                         kind="ExternalOutput").ap()
    decl_tap("x0", [128, NT, C], F32)
    decl_tap("xn", [128, NC, T], BF16)
    decl_tap("qf", [128, NC, T], BF16)
    decl_tap("kf", [128, NC, T], BF16)
    decl_tap("vtm", [128, NT, 16 * 65], BF16)
    decl_tap("of", [128, NC, T], BF16)
    decl_tap("res1", [128, NT, C], F32)
    decl_tap("xqf", [2, 128, NC, 1024], BF16)
    decl_tap("gu", [2, 128, 8, INNER], BF16)
    decl_tap("gud3", [2, 128, 8, INNER], BF16)
    decl_tap("wsc", [128, 8], F32)
    decl_tap("wgb0", [128, NC, 512], BF16)
    decl_tap("wqrb", [NC, 128, 512], BF16)
    decl_tap("wqrb2", [NC, 128, 512], BF16)
    decl_tap("mark", [128, 128], BF16)
    decl_tap("stg", [128, 1024], BF16)
    decl_tap("sca", [128, 1024], F32)
    decl_tap("scv", [128, 1024], F32)
    decl_tap("colsum", [128, 40], F32)

    ident_np = np.eye(128, dtype=ml_dtypes.bfloat16)
    mask_np = np.tril(np.full((128, 128), -1e9, dtype=np.float32), -1)
    ident_d = nc.inline_tensor(ident_np, name="identc").ap()
    mask_d = nc.inline_tensor(mask_np, name="maskc").ap()

    ctx = contextlib.ExitStack()
    E = ctx.enter_context

    with ctx:
        x_sb = E(nc.sbuf_tensor("x_sb", [128, NT, C], F32))
        ident = E(nc.sbuf_tensor("ident", [128, 128], BF16))
        mask = E(nc.sbuf_tensor("mask", [128, 128], F32))
        ones_row = E(nc.sbuf_tensor("ones_row", [1, 128], F32))
        ones_col = E(nc.sbuf_tensor("ones_col", [128, 1], F32))
        bqk = E(nc.sbuf_tensor("bqk", [128, 24], F32))
        diag = E(nc.sbuf_tensor("diag", [128, 128], BF16))
        small = E(nc.sbuf_tensor("small", [128, 16], F32))
        invsx = E(nc.sbuf_tensor("invsx", [128, 8], F32))
        rsumc = E(nc.sbuf_tensor("rsumc", [128, 4], F32))
        wsc = E(nc.sbuf_tensor("wsc", [128, 8], F32))
        sc11 = E(nc.sbuf_tensor("sc11", [1, 8], F32))

        psA = E(nc.psum_tensor("psA", [128, 2048], F32))
        psB = E(nc.psum_tensor("psB", [128, 2048], F32))
        psA_bf = psA[:].bitcast(BF16)
        psB_bf = psB[:].bitcast(BF16)

        p = Prog(nc)
        ST, SS, SV, SG = "sT", "sS", "sV", "sG"
        DY, DG, DO = "dY", "dG", "dO"
        DC = "dC"

        guards = {"rcclip": 0, "wstr": [0, 0], "wstrb": [0, 0], "wgu": [0, 0],
                  "wdb": [0, 0], "wps": [0, 0]}

        def t_mm(out_ap, lhsT_ap, rhs_ap, start, stop, waits=(), inc=None,
                 tile_position=None):
            p.emit("T", lambda e, o=out_ap, l=lhsT_ap, r=rhs_ap, st=start,
                   sp=stop, tp=tile_position:
                   e.matmul(o, l, r, start=st, stop=sp, tile_position=tp),
                   waits=waits, inc=inc)

        rcg = {0: 0, 1: 0}

        def round_chain(scr, src_ap, scale_ap, dst_ap, n, waits_first=(),
                        slot=None, off=0):
            scrV, scrI, scrF, scrR = scr
            guard = p.now(SV) if slot is None else rcg[slot]
            wl = list(waits_first) + [("sV", guard)]
            o1 = off + n
            p.emit("S", lambda e, off=off, o1=o1: e.activation(
                scrV[:, off:o1], src_ap, AF.Identity,
                scale=scale_ap, bias=f32(0.0)), waits=wl)
            p.emit("S", lambda e, off=off, o1=o1: e.copy(scrI[:, off:o1],
                                                         scrV[:, off:o1]))
            p.emit("S", lambda e, off=off, o1=o1: e.copy(scrR[:, off:o1],
                                                         scrI[:, off:o1]),
                   inc=SS)
            sv = p.now(SS)
            p.emit("V", lambda e, off=off, o1=o1: e.tensor_tensor(
                scrV[:, off:o1], scrV[:, off:o1], scrR[:, off:o1],
                OP.subtract), waits=[(SS, sv)])
            p.emit("V", lambda e, off=off, o1=o1: e.tensor_scalar(
                scrF[:, off:o1], scrV[:, off:o1], -1.0, 0.5,
                OP.mult, OP.is_ge))
            p.emit("V", lambda e, off=off, o1=o1: e.scalar_tensor_tensor(
                scrR[:, off:o1], scrV[:, off:o1], 0.5, scrR[:, off:o1],
                OP.is_ge, OP.add))
            p.emit("V", lambda e, d=dst_ap, off=off, o1=o1: e.tensor_tensor(
                d, scrR[:, off:o1], scrF[:, off:o1], OP.subtract), inc=SV)
            if slot is not None:
                rcg[slot] = p.now(SV)

        p.emit("Y", lambda e: e.dma_start(out=ident[:], in_=ident_d), inc=DC, dma=True)
        p.emit("Y", lambda e: e.dma_start(out=mask[:], in_=mask_d), inc=DC, dma=True)
        p.emit("Y", lambda e: e.dma_start(
            out=bqk[:], in_=bqkv_d.rearrange("(f p) -> p f", p=128),
            allow_slow_non_contiguous=True), inc=DC, dma=True)
        CONST_DY = p.now(DC)
        if "mark" in taps:
            p.emit("V", lambda e: e.memset(diag[:], 7.0), inc=SV)
            mv = p.now(SV)
            p.emit("Y", lambda e: e.dma_start(out=wgT_d[7, 7, :, 0:128],
                                              in_=diag[:]),
                   waits=[(SV, mv)], inc=DC, dma=True)
        p.emit("V", lambda e: e.memset(ones_row[:], 1.0))
        p.emit("V", lambda e: e.memset(ones_col[:], 1.0), inc=SV)
        ONES_V = p.now(SV)

        with nc.sbuf_tensor("wbuf", [128, 2, 4096], F32) as wbuf, \
             nc.sbuf_tensor("mbuf", [128, 4, 1024], F32) as mbuf, \
             nc.sbuf_tensor("colsum", [128, 100], F32) as colsum, \
             nc.sbuf_tensor("scrV0", [128, 4096], F32) as scrV, \
             nc.sbuf_tensor("scrI0", [128, 4096], I32) as scrI, \
             nc.sbuf_tensor("scrR0", [128, 4096], F32) as scrR, \
             nc.sbuf_tensor("scrA0", [128, 4096], F32) as scrA, \
             nc.sbuf_tensor("stag", [128, 4096], BF16) as stag, \
             nc.sbuf_tensor("stag2", [128, 4096], BF16) as stag2:
            scrF = scrI[:].bitcast(F32)
            scr = (scrV, scrI, scrF, scrR)

            mats = [(wg_d, INNER, C, 0, wgT_d),
                    (wu_d, INNER, C, 1, wuT_d),
                    (wd_d, C, INNER, 2, wdT_d)]
            last_use = {0: 0, 1: 0}
            last_use_sem = {0: SV, 1: SV}
            WSC_S = 0

            def load_chunk(md, R, Ccols, it, par):
                eng = "Y" if par == 0 else "S"
                sem = f"wb{par}"
                if Ccols == C:
                    srcap = md.rearrange("(k p) c -> p k c", p=128)[
                        :, it * 4:(it + 1) * 4, :]
                else:
                    srcap = md[it * 128:(it + 1) * 128, :]
                p.emit(eng, lambda e, srcap=srcap, par=par, wbuf=wbuf:
                       e.dma_start(out=wbuf[:, par].rearrange(
                           "p (k c) -> p k c", c=1024) if False else
                           wbuf[:, par, :], in_=srcap),
                       waits=[(last_use_sem[par], last_use[par])],
                       inc=sem, dma=True)
                return (sem, p.now(sem))

            for it in range(8):
                par = it % 2
                dv = load_chunk(wg_d, INNER, C, it, par)
                p.emit("V", lambda e, it=it, par=par, wbuf=wbuf,
                       colsum=colsum:
                       e.tensor_reduce(colsum[:, it:it + 1],
                                       wbuf[:, par, :], AX, OP.add,
                                       apply_absolute_value=True),
                       waits=[dv], inc=SV)
                last_use[par] = p.now(SV)
                last_use_sem[par] = SV

            mitems = []
            for md2, base in ((wu_d, 32), (wd_d, 64)):
                for k in range(32):
                    mitems.append((md2, base, k))
            mb_next = [0]
            mred_v = {}
            mload_v = {}

            def emit_mean_batch():
                b = mb_next[0]
                mb_next[0] += 1
                if b > 0:
                    for k in range(4 * (b - 1), 4 * b):
                        if k >= len(mitems):
                            break
                        _, base, kk = mitems[k]
                        slot = k % 4
                        col = base + kk
                        p.emit("V", lambda e, slot=slot, col=col, mbuf=mbuf,
                               colsum=colsum:
                               e.tensor_reduce(colsum[:, col:col + 1],
                                               mbuf[:, slot], AX, OP.add,
                                               apply_absolute_value=True),
                               waits=[("mg", mload_v[k])], inc=SV)
                        mred_v[k] = p.now(SV)
                for k in range(4 * b, 4 * b + 4):
                    if k >= len(mitems):
                        break
                    md2, base, kk = mitems[k]
                    slot = k % 4
                    if base == 32:
                        src = md2[kk * 128:(kk + 1) * 128, :]
                    else:
                        rb, cb = divmod(kk, 4)
                        src = md2[rb * 128:(rb + 1) * 128,
                                  cb * 1024:(cb + 1) * 1024]
                    wl = [(SV, mred_v[k - 4])] if k >= 4 else []
                    p.emit("G", lambda e, src=src, slot=slot, mbuf=mbuf:
                           e.dma_start(out=mbuf[:, slot], in_=src),
                           waits=wl, inc="mg", dma=True)
                    mload_v[k] = p.now("mg")

            wsc_v = [0, 0, 0]

            def combine(mi, lo, nslots):
                sci = mats[mi][3]
                p.emit("V", lambda e, lo=lo, nslots=nslots, mi=mi:
                       e.tensor_reduce(
                    colsum[:, 96 + mi:97 + mi],
                    colsum[:, lo:lo + nslots], AX, OP.add),
                    waits=[(SV, p.now(SV))], inc=SV)
                gv = p.now(SV)
                t_mm(psB[0:1, 8 + mi:9 + mi], ones_col[:],
                     colsum[:, 96 + mi:97 + mi], True, True,
                     waits=[(SV, gv), (SV, ONES_V), (SS, p.now(SS))], inc=ST)
                tv = p.now(ST)
                p.emit("S", lambda e, mi=mi: e.copy(sc11[0:1, mi:mi + 1],
                                                    psB[0:1, 8 + mi:9 + mi]),
                       waits=[(ST, tv)], inc=SS)
                sv = p.now(SS)
                t_mm(psB[:, 12 + mi:13 + mi], ones_row[:],
                     sc11[0:1, mi:mi + 1], True, True,
                     waits=[(SS, sv)], inc=ST)
                tv = p.now(ST)
                p.emit("V", lambda e, sci=sci: e.tensor_scalar(
                    wsc[:, 3 + sci:4 + sci], psB[:, 12 + sci:13 + sci],
                    1.0 / (INNER * C), 1e-5, OP.mult, OP.max),
                    waits=[(ST, tv)], inc=SV)
                p.emit("V", lambda e, sci=sci: e.reciprocal(
                    wsc[:, sci:sci + 1], wsc[:, 3 + sci:4 + sci]),
                    waits=[(SV, p.now(SV))], inc=SV)
                wsc_v[mi] = p.now(SV)

            combine(0, 0, 8)
            tern_g = [0, 0]

            for mi, (md, R, Ccols, sci, dstT) in enumerate(mats):
                nit = (R * Ccols) // (128 * 4096)
                qsc = wsc[:, sci:sci + 1]
                if debug_wq:
                    continue
                for it in range(nit):
                    if mi < 2 or it == 0:
                        emit_mean_batch()
                    if mi == 1 and it == 0:
                        combine(1, 32, 32)
                    if mi == 2 and it == 0:
                        combine(2, 64, 32)
                    par = it % 2
                    dv = load_chunk(md, R, Ccols, it, par)
                    for half in range(2):
                        o0, o1 = half * 2048, (half + 1) * 2048
                        p.emit("S", lambda e, par=par, o0=o0, o1=o1, qsc=qsc,
                               sV=scrV, wbuf=wbuf:
                               e.activation(sV[:, o0:o1],
                                            wbuf[:, par, o0:o1],
                                            AF.Identity, scale=qsc,
                                            bias=f32(0.0)),
                               waits=[dv, (SV, wsc_v[mi]),
                                      (SV, tern_g[half])], inc=SS)
                        sv = p.now(SS)
                        p.emit("V", lambda e, o0=o0, o1=o1, sV=scrV,
                               sR=scrR: e.tensor_scalar(
                            sR[:, o0:o1], sV[:, o0:o1], -0.5, None,
                            OP.is_lt), waits=[(SS, sv)])
                        p.emit("V", lambda e, o0=o0, o1=o1, sV=scrV,
                               sR=scrR, stag=stag:
                               e.scalar_tensor_tensor(
                            stag[:, o0:o1], sV[:, o0:o1], 0.5,
                            sR[:, o0:o1], OP.is_gt, OP.subtract),
                            waits=[(ST, p.now(ST))], inc=SV)
                        tern_g[half] = p.now(SV)
                    last_use[par] = p.now(SS)
                    last_use_sem[par] = SS
                    vq = p.now(SV)
                    for t32 in range(32):
                        ps_t = psA if t32 < 16 else psB
                        jo = t32 % 16
                        if Ccols == C:
                            ic, rr = divmod(t32, 4)
                            src_ap = stag[:, rr * 1024 + ic * 128:
                                          rr * 1024 + ic * 128 + 128]
                        else:
                            src_ap = stag[:, t32 * 128:(t32 + 1) * 128]
                        t_mm(ps_t[:, jo * 128:(jo + 1) * 128],
                             src_ap, ident[:],
                             True, True,
                             waits=[(SV, vq), (DC, CONST_DY),
                                    (SS, p.now(SS))] if t32 == 0 else (),
                             inc=ST if t32 == 31 else None)
                    tv = p.now(ST)
                    for bkt in range(8):
                        ps_t = psA if bkt < 4 else psB
                        bo = bkt % 4
                        p.emit("S", lambda e, bkt=bkt, ps_t=ps_t, bo=bo,
                               stag2=stag2: e.copy(
                            stag2[:, bkt * 512:(bkt + 1) * 512],
                            ps_t[:, bo * 512:(bo + 1) * 512]),
                            waits=[(ST, tv), (DG, p.now(DG))] if bkt == 0
                            else (), inc=SS if bkt == 7 else None)
                    sv = p.now(SS)
                    if Ccols == C:
                        dst = dstT[it].rearrange("i p c -> p i c")
                        srcv = stag2[:].rearrange("p (i c) -> p i c", c=512)
                        p.emit("Y", lambda e, dst=dst, srcv=srcv:
                               e.dma_start(out=dst, in_=srcv),
                               waits=[(SS, sv)], inc=DG, dma=True)
                    else:
                        dst = dstT[it // 4, :, :,
                                   (it % 4) * 128:(it % 4) * 128 + 128]
                        p.emit("Y", lambda e, dst=dst, stag2=stag2:
                               e.dma_start(
                            out=dst.rearrange("i p c -> p i c"),
                            in_=stag2[:].rearrange("p (i q) -> p i q", q=128)),
                            waits=[(SS, sv)], inc=DG, dma=True)
        PRE_G = p.now(DG)
        pre_counts = [(ST, p.now(ST)), (SS, p.now(SS)), (SV, p.now(SV)),
                      (SG, p.now(SG)), (DG, p.now(DG)), (DC, p.now(DC)),
                      ("wb0", p.now("wb0")), ("wb1", p.now("wb1")),
                      ("mg", p.now("mg"))]
        for _eng in ("T", "S", "V", "G"):
            p.emit(_eng, lambda e: e.nop(), waits=pre_counts)
        if "wqrb" in taps:
            p.emit("Y", lambda e: e.dma_start(out=tap_d["wqrb"], in_=wgT_d[0]),
                   waits=[(DG, PRE_G)], inc=DO, dma=True)

        x_reader_ss = [0] * NT
        x_issue = [0] * NT
        for s in range(0 if preproc_only else nseq):
            sqAC = contextlib.ExitStack()
            with sqAC:
                xn_f = sqAC.enter_context(
                    nc.sbuf_tensor(f"xn_f{s}", [128, NC, T], BF16))
                sqAB = contextlib.ExitStack()
                with sqAB:
                    EAB = sqAB.enter_context
                    q_f = EAB(nc.sbuf_tensor(f"q_f{s}", [128, NC, T], BF16))
                    k_f = EAB(nc.sbuf_tensor(f"k_f{s}", [128, NC, T], BF16))
                    v_tm = EAB(nc.sbuf_tensor(f"v_tm{s}",
                                               [128, NT, 16 * 65], BF16))

                    sqA = contextlib.ExitStack()
                    with sqA:
                        EA = sqA.enter_context
                        xn_tm = EA(nc.sbuf_tensor(f"xn_tm{s}", [128, 1024], BF16))
                        v_fb = EA(nc.sbuf_tensor(f"v_fb{s}", [128, 1024], BF16))
                        wstr = EA(nc.sbuf_tensor(f"wstr{s}", [128, 2, 4, 128], F32))
                        wstrb = EA(nc.sbuf_tensor(f"wstrb{s}",
                                                  [128, 2, NC, 128], BF16))

                        prev_out = p.now(DO)
                        for t in range(NT):
                            xs = f"xd{t % 4}"
                            wl = [(DO, prev_out)]
                            if s == 0 and t >= 4:
                                wl.append((SS, x_reader_ss[t - 4]))
                            elif s == 1:
                                wl.append((SS, x_reader_ss[(t - 4) % NT]))
                            p.emit("Y", lambda e, t=t, s=s: e.dma_start(
                                out=x_sb[:, t, :],
                                in_=x_d.rearrange("s (n p) c -> s n p c",
                                                  p=128)[s, t]),
                                waits=wl, inc=xs, dma=True)
                            x_issue[t] = p.now(xs)
                        if "x0" in taps and s == 0:
                            p.emit("Y", lambda e: e.dma_start(
                                out=tap_d["x0"], in_=x_sb[:]),
                                waits=[(f"xd{tt2}", x_issue[tt2])
                                       for tt2 in range(NT)],
                                inc=DO, dma=True)

                        def emit_ln(t, out_ap, waits0):
                            p.emit("V", lambda e, t=t: e.reduce_sum(
                                small[:, 0:1], x_sb[:, t, :], AX), waits=waits0,
                                inc=SV)
                            p.emit("V", lambda e: e.tensor_scalar(
                                small[:, 1:2], small[:, 0:1], -1.0 / C, None,
                                OP.mult), waits=[(SV, p.now(SV))], inc=SV)
                            vv = p.now(SV)
                            p.emit("S", lambda e, t=t: e.activation(
                                psA[:, 0:1024], x_sb[:, t, :], AF.Square,
                                bias=small[:, 1:2], scale=f32(1.0)),
                                waits=[(SV, vv)], inc=SS)
                            sv = p.now(SS)
                            p.emit("V", lambda e: e.reduce_sum(
                                small[:, 2:3], psA[:, 0:1024], AX),
                                waits=[(SS, sv)], inc=SV)
                            p.emit("V", lambda e: e.tensor_scalar(
                                small[:, 3:4], small[:, 2:3], 1.0 / C, 1e-5,
                                OP.mult, OP.add), waits=[(SV, p.now(SV))],
                                inc=SV)
                            vv = p.now(SV)
                            p.emit("S", lambda e: e.activation(
                                small[:, 4:5], small[:, 3:4], AF.Sqrt),
                                waits=[(SV, vv)], inc=SS)
                            sv = p.now(SS)
                            p.emit("V", lambda e: e.reciprocal(
                                small[:, 5:6], small[:, 4:5]), waits=[(SS, sv)],
                                inc=SV)
                            p.emit("V", lambda e: e.tensor_tensor(
                                small[:, 6:7], small[:, 1:2], small[:, 5:6],
                                OP.mult), waits=[(SV, p.now(SV))], inc=SV)
                            vv = p.now(SV)
                            p.emit("S", lambda e, t=t, o=out_ap: e.activation(
                                o, x_sb[:, t, :], AF.Identity,
                                bias=small[:, 6:7], scale=small[:, 5:6]),
                                waits=[(SV, vv)], inc=SS)
                            return p.now(SS)

                        for t in range(NT):
                            sv = emit_ln(t, xn_tm[:],
                                         [(f"xd{t % 4}", x_issue[t])])
                            x_reader_ss[t] = sv
                            for j in range(8):
                                bank = j // 4
                                t_mm(psB[:, bank * 512 + (j % 4) * 128:
                                         bank * 512 + (j % 4) * 128 + 128],
                                     xn_tm[:, j * 128:(j + 1) * 128], ident[:],
                                     True, True,
                                     waits=[(SS, sv), (SV, p.now(SV)),
                                            (DC, CONST_DY)] if j == 0 else (),
                                     inc=ST if j in (3, 7) else None)
                            tv = p.now(ST)
                            p.emit("V", lambda e, t=t: e.tensor_copy(
                                xn_f[:, 0:4, t * 128:(t + 1) * 128],
                                psB[:, 0:512].rearrange("p (j c) -> p j c",
                                                        c=128)),
                                waits=[(ST, tv - 1)])
                            p.emit("V", lambda e, t=t: e.tensor_copy(
                                xn_f[:, 4:8, t * 128:(t + 1) * 128],
                                psB[:, 512:1024].rearrange("p (j c) -> p j c",
                                                           c=128)),
                                waits=[(ST, tv)], inc=SV)
                        XN_V = p.now(SV)
                        if "xn" in taps and s == 0:
                            p.emit("Y", lambda e: e.dma_start(
                                out=tap_d["xn"], in_=xn_f[:]),
                                waits=[(SV, XN_V)], inc=DO, dma=True)

                        for fo in range(24):
                            par = fo % 2
                            guard_t = guards["wstrb"][par]
                            for hf in range(2):
                                guard_g = guards["wstr"][hf]
                                wssem = f"ws{hf}"
                                p.emit("Y", lambda e, fo=fo, hf=hf, wstr=wstr:
                                       e.dma_start(
                                    out=wstr[:, hf],
                                    in_=wqkv_d.rearrange(
                                        "(c p) (f j) -> f p c j",
                                        p=128, j=128)[fo][:, hf * 4:hf * 4 + 4]),
                                    waits=[(SG, guard_g)], inc=wssem, dma=True)
                                dv = p.now(wssem)
                                p.emit("G", lambda e, par=par, hf=hf,
                                       wstrb=wstrb, wstr=wstr: e.tensor_copy(
                                    wstrb[:, par, hf * 4:hf * 4 + 4, :],
                                    wstr[:, hf]),
                                    waits=[(wssem, dv), (DG, PRE_G),
                                           (ST, guard_t)],
                                    inc=SG)
                                guards["wstr"][hf] = p.now(SG)
                            gv = p.now(SG)
                            ps = psA[:, par * 1024:par * 1024 + 1024]
                            for th in range(2):
                                sv_guard = p.now(SS)
                                for c in range(NC):
                                    for tq in range(2):
                                        wl = []
                                        if c == 0 and tq == 0:
                                            wl = [(SG, gv), (SV, XN_V),
                                                  (SS, sv_guard)]
                                        t_mm(ps[:, tq * 512:(tq + 1) * 512],
                                             wstrb[:, par, c, :],
                                             xn_f[:, c, th * 1024 + tq * 512:
                                                  th * 1024 + (tq + 1) * 512],
                                             start=(c == 0), stop=(c == NC - 1),
                                             waits=wl,
                                             inc=ST if (c == NC - 1 and tq == 1)
                                             else None)
                                tv = p.now(ST)
                                if fo < 8:
                                    dst = q_f[:, fo, th * 1024:(th + 1) * 1024]
                                elif fo < 16:
                                    dst = k_f[:, fo - 8,
                                              th * 1024:(th + 1) * 1024]
                                else:
                                    dst = v_fb[:, 0:1024]
                                p.emit("S", lambda e, dst=dst, ps=ps, fo=fo:
                                       e.activation(
                                    dst, ps[:, 0:1024], AF.Identity,
                                    bias=bqk[:, fo:fo + 1], scale=f32(1.0)),
                                    waits=[(ST, tv)] + (
                                        [(ST, p.now(ST))] if fo >= 16 else []),
                                    inc=SS)
                                if fo >= 16:
                                    sv = p.now(SS)
                                    fv = fo - 16
                                    for tb in range(2):
                                        bank = tb % 2
                                        for j in range(4):
                                            wl = []
                                            if j == 0:
                                                wl = [(SS, sv), (SV, p.now(SV))]
                                            t_mm(psB[:, bank * 512 + j * 128:
                                                     bank * 512 + j * 128 + 128],
                                                 v_fb[:, (tb * 4 + j) * 128:
                                                      (tb * 4 + j + 1) * 128],
                                                 ident[:], True, True, waits=wl,
                                                 inc=ST if j == 3 else None)
                                        tv = p.now(ST)
                                        psv = psB[:, bank * 512:
                                                   bank * 512 + 512].rearrange(
                                            "p (j c) -> p j c", c=128)
                                        for hh in range(2):
                                            p.emit("V", lambda e, th=th, tb=tb,
                                                   fv=fv, hh=hh, psv=psv:
                                                   e.tensor_copy(
                                                v_tm[:, th * 8 + tb * 4:
                                                     th * 8 + tb * 4 + 4,
                                                     (2 * fv + hh) * 65:
                                                     (2 * fv + hh) * 65 + 64],
                                                psv[:, :, hh * 64:
                                                    hh * 64 + 64]),
                                                waits=[(ST, tv)]
                                                if hh == 0 else (),
                                                inc=SV if hh == 1 else None)
                            guards["wstrb"][par] = p.now(ST)
                        QKV_S, QKV_V = p.now(SS), p.now(SV)
                        if "qf" in taps and s == 0:
                            for nm, buf in [("qf", q_f), ("kf", k_f),
                                            ("vtm", v_tm)]:
                                if nm in tap_d:
                                    p.emit("Y", lambda e, nm=nm, buf=buf:
                                           e.dma_start(out=tap_d[nm],
                                                       in_=buf[:]),
                                           waits=[(SS, QKV_S), (SV, QKV_V)],
                                           inc=DO, dma=True)

                    O_f = xn_f
                    with nc.sbuf_tensor(f"eSt{s}", [128, 2, 2, 512], BF16) \
                            as expSt, \
                         nc.sbuf_tensor(f"rsr{s}", [1, 1024], F32) as rs_row, \
                         nc.sbuf_tensor(f"rsT{s}", [64, 1024], F32) as rsT:
                        ones_dst = v_tm[:].rearrange("p k (h c) -> p k h c",
                                                     c=65)
                        for half in range(2):
                            p.emit("S", lambda e, half=half,
                                   ones_dst=ones_dst: e.activation(
                                ones_dst[:, half * 8:half * 8 + 8, :, 64:65],
                                ident[:].rearrange("p (a b c) -> p a b c",
                                                   a=8, b=16, c=1),
                                AF.Identity, bias=f32(1.0), scale=f32(0.0)),
                                waits=[(DC, CONST_DY), (SV, p.now(SV)),
                                       (ST, p.now(ST)), (SG, p.now(SG))]
                                if half == 0 else (),
                                inc=SS if half == 1 else None)

                        def stb(b):
                            t = psA if b < 2 else psB
                            return t[:, (b % 2) * 512:(b % 2) * 512 + 512]

                        rsTile = [psB[0:64, 512:1024],
                                  psB[64:128, 512:1024]]

                        def ob_ap(b, p0, p1):
                            t = psA if b < 2 else psB
                            c0 = 1024 + (b % 2) * 512
                            return t[p0:p1, c0:c0 + 512]

                        st_ss = [0, 0, 0]
                        exp_free = [[0, 0], [0, 0]]
                        o_free = [(0, 0)] * 4
                        rep_st = [0, 0]
                        recip_sv = [0, 0]
                        kbctr = 0
                        items = [(qt, pr) for qt in range(4) for pr in range(8)]
                        epi = None
                        for idx in range(len(items) + 1):
                            cur = None
                            if idx < len(items):
                                qt, pr = items[idx]
                                kbmax = 4 * qt + 3
                                obs = (2 * (idx % 2), 2 * (idx % 2) + 1)
                                heads = (2 * pr, 2 * pr + 1)
                                def emit_attv(t2, h, pkb, ess):
                                    n0 = max(0, pkb - 4 * qt) * 128
                                    awl = [(SS, ess)]
                                    if pkb == 0:
                                        awl += [(SV, o_free[obs[t2]][0]),
                                                (SS, o_free[obs[t2]][1])]
                                    t_mm(ob_ap(obs[t2], 0, 65)[:, n0:512],
                                         v_tm[:, pkb, 65 * h:65 * h + 65],
                                         expSt[:, t2, pkb % 2, n0:512],
                                         start=(pkb == 0),
                                         stop=(pkb == kbmax),
                                         waits=awl, inc=ST)
                                    exp_free[t2][pkb % 2] = p.now(ST)

                                pend = None
                                for kb in range(kbmax + 1):
                                    j = kb - 4 * qt
                                    n0 = max(0, j) * 128
                                    this_kb = []
                                    for t2, h in enumerate(heads):
                                        hc, hb = h // 2, (h % 2) * 64
                                        sb_ = kbctr % 3
                                        kbctr += 1
                                        wl = [(SS, st_ss[sb_])]
                                        if idx == 0 and kb == 0 and t2 == 0:
                                            wl += [(SS, QKV_S), (SV, QKV_V)]
                                        t_mm(stb(sb_)[:, n0:512],
                                             k_f[hb:hb + 64, hc,
                                                 kb * 128:(kb + 1) * 128],
                                             q_f[hb:hb + 64, hc,
                                                 qt * 512 + n0:(qt + 1) * 512],
                                             True, True, waits=wl, inc=ST)
                                        stv = p.now(ST)
                                        ewl = [(ST, stv),
                                               (ST, exp_free[t2][kb % 2])]
                                        if j >= 0:
                                            p.emit("V", lambda e, sb_=sb_,
                                                   n0=n0: e.tensor_tensor(
                                                stb(sb_)[:, n0:n0 + 128],
                                                stb(sb_)[:, n0:n0 + 128],
                                                mask[:], OP.add),
                                                waits=[(ST, stv),
                                                       (DC, CONST_DY)],
                                                inc=SV)
                                            ewl.append((SV, p.now(SV)))
                                        p.emit("S", lambda e, sb_=sb_, n0=n0,
                                               kb=kb, t2=t2,
                                               expSt=expSt: e.activation(
                                            expSt[:, t2, kb % 2, n0:512],
                                            stb(sb_)[:, n0:512],
                                            AF.Exp, bias=f32(EXP_SHIFT),
                                            scale=f32(0.125)),
                                            waits=ewl, inc=SS)
                                        st_ss[sb_] = p.now(SS)
                                        this_kb.append((t2, h, kb,
                                                        p.now(SS)))
                                    if pend:
                                        for args in pend:
                                            emit_attv(*args)
                                    pend = this_kb
                                for args in pend:
                                    emit_attv(*args)
                                cur = (qt, heads, obs, p.now(ST))
                            if epi is not None:
                                eqt, eheads, eobs, etv = epi
                                rs_ss2 = [0, 0]
                                for t2 in range(2):
                                    p.emit("S", lambda e, eb=eobs[t2], t2=t2,
                                           rs_row=rs_row: e.copy(
                                        rs_row[0:1, t2 * 512:t2 * 512 + 512],
                                        ob_ap(eb, 64, 65)),
                                        waits=[(ST, etv), (ST, rep_st[t2])],
                                        inc=SS)
                                    rs_ss2[t2] = p.now(SS)
                                for t2 in range(2):
                                    t_mm(rsTile[t2], ones_row[0:1, 0:64],
                                         rs_row[0:1, t2 * 512:t2 * 512 + 512],
                                         True, True,
                                         waits=[(SS, rs_ss2[t2]),
                                                (SV, recip_sv[t2])],
                                         inc=ST,
                                         tile_position=(0, 64 * t2))
                                    rep_st[t2] = p.now(ST)
                                for t2 in range(2):
                                    p.emit("V", lambda e, t2=t2, rsT=rsT:
                                           e.reciprocal(
                                        rsT[0:64, t2 * 512:t2 * 512 + 512],
                                        rsTile[t2]),
                                        waits=[(ST, rep_st[t2])], inc=SV)
                                    recip_sv[t2] = p.now(SV)
                                for t2, eh in enumerate(eheads):
                                    ehc, ehb = eh // 2, (eh % 2) * 64
                                    eb = eobs[t2]
                                    p.emit("V", lambda e, ehb=ehb, ehc=ehc,
                                           eqt=eqt, eb=eb, t2=t2, rsT=rsT:
                                           e.tensor_tensor(
                                        O_f[ehb:ehb + 64, ehc,
                                            eqt * 512:(eqt + 1) * 512],
                                        ob_ap(eb, 0, 64),
                                        rsT[0:64, t2 * 512:t2 * 512 + 512],
                                        OP.mult),
                                        waits=[(SV, recip_sv[t2])], inc=SV)
                                    o_free[eb] = (p.now(SV), rs_ss2[t2])
                            epi = cur
                    ATT_S = p.now(SS)
                    if "of" in taps and s == 0:
                        p.emit("Y", lambda e: e.dma_start(
                            out=tap_d["of"], in_=O_f[:]),
                            waits=[(SS, ATT_S), (SV, p.now(SV))],
                            inc=DO, dma=True)

                with nc.sbuf_tensor(f"wpj{s}", [128, NC, C], BF16) as wpj, \
                     nc.sbuf_tensor(f"wps{s}", [128, 2, C], F32) as wps, \
                     nc.sbuf_tensor(f"bpr{s}", [1, C], F32) as bproj_row:
                    p.emit("Y", lambda e, bproj_row=bproj_row: e.dma_start(
                        out=bproj_row[:],
                        in_=bproj_d.unsqueeze(0)),
                        waits=[(SS, ATT_S)], inc="bp", dma=True)
                    BPR_DY = p.now("bp")
                    for c in range(NC):
                        par = c % 2
                        guard = guards["wps"][par]
                        wpsem = f"wp{par}"
                        p.emit("Y", lambda e, c=c, par=par, wps=wps: e.dma_start(
                            out=wps[:, par, :],
                            in_=wproj_d[c * 128:(c + 1) * 128, :]),
                            waits=[(SG, guard), (SS, ATT_S)], inc=wpsem,
                            dma=True)
                        dv = p.now(wpsem)
                        p.emit("G", lambda e, c=c, par=par, wpj=wpj, wps=wps:
                               e.tensor_copy(
                            wpj[:, c, :], wps[:, par, :]),
                            waits=[(wpsem, dv)], inc=SG)
                        guards["wps"][par] = p.now(SG)
                    WPJ_G = p.now(SG)
                    for t in range(NT):
                        par = t % 2
                        ps = psA[:, par * 1024:par * 1024 + 1024]
                        vres = p.now(SV)
                        for c in range(NC):
                            for half in range(2):
                                wl = []
                                if c == 0 and half == 0:
                                    wl = [(SG, WPJ_G), (SS, ATT_S), (SV, vres),
                                          ("bp", BPR_DY)]
                                t_mm(ps[:, half * 512:(half + 1) * 512],
                                     O_f[:, c, t * 128:(t + 1) * 128],
                                     wpj[:, c, half * 512:(half + 1) * 512],
                                     start=(c == 0), stop=False, waits=wl)
                        for half in range(2):
                            t_mm(ps[:, half * 512:(half + 1) * 512],
                                 ones_row[:, 0:128],
                                 bproj_row[0:1, half * 512:(half + 1) * 512],
                                 start=False, stop=True,
                                 inc=ST if half == 1 else None)
                        tv = p.now(ST)
                        p.emit("V", lambda e, t=t, ps=ps: e.tensor_tensor(
                            x_sb[:, t, :], x_sb[:, t, :], ps[:, 0:1024], OP.add),
                            waits=[(ST, tv)], inc=SV)
                PROJ_V = p.now(SV)
                if "res1" in taps and s == 0:
                    p.emit("Y", lambda e: e.dma_start(
                        out=tap_d["res1"], in_=x_sb[:]),
                        waits=[(SV, PROJ_V)], inc=DO, dma=True)
            TAP_RES1_DO = p.now(DO)

            if not do_ffn:
                continue
            sqD = contextlib.ExitStack()
            with sqD:
                ED = sqD.enter_context
                scrV = ED(nc.sbuf_tensor(f"scrVd{s}", [128, 1024], F32))
                scrI = ED(nc.sbuf_tensor(f"scrId{s}", [128, 1024], I32))
                scrR = ED(nc.sbuf_tensor(f"scrRd{s}", [128, 1024], F32))
                scrA = ED(nc.sbuf_tensor(f"scrAd{s}", [128, 1024], F32))
                sil = ED(nc.sbuf_tensor(f"sild{s}", [128, 512], F32))
                xn_tm = ED(nc.sbuf_tensor(f"xntmd{s}", [128, 2, 1024], BF16))
                st1 = ED(nc.sbuf_tensor(f"st1d{s}", [128, 8, 8], F32))
                st2 = ED(nc.sbuf_tensor(f"st2d{s}", [128, 8, 8], F32))
                diag2 = ED(nc.sbuf_tensor(f"dg2d{s}", [128, 2, 128], BF16))
                xqf = ED(nc.sbuf_tensor(f"xqf{s}", [128, NC, 1024], BF16))
                gu = ED(nc.sbuf_tensor(f"gu{s}", [128, 8, INNER], BF16))
                wgb = ED(nc.sbuf_tensor(f"wgb{s}", [128, 2, NC, 512], BF16))
                wub = ED(nc.sbuf_tensor(f"wub{s}", [128, 2, NC, 512], BF16))
                wdb = ED(nc.sbuf_tensor(f"wdb{s}", [128, 2, 512], BF16))
                scrF = scrI[:].bitcast(F32)
                scr = (scrV, scrI, scrF, scrR)

                TAP_G3_DO = p.now(DO)

                def emit_ln2(t, out_ap, waits0):
                    p.emit("V", lambda e, t=t: e.reduce_sum(
                        small[:, 0:1], x_sb[:, t, :], AX), waits=waits0,
                        inc=SV)
                    p.emit("V", lambda e: e.tensor_scalar(
                        small[:, 1:2], small[:, 0:1], -1.0 / C, None, OP.mult),
                        waits=[(SV, p.now(SV))], inc=SV)
                    vv = p.now(SV)
                    p.emit("S", lambda e, t=t: e.activation(
                        psA[:, 0:1024], x_sb[:, t, :], AF.Square,
                        bias=small[:, 1:2], scale=f32(1.0)),
                        waits=[(SV, vv)], inc=SS)
                    sv = p.now(SS)
                    p.emit("V", lambda e: e.reduce_sum(
                        small[:, 2:3], psA[:, 0:1024], AX), waits=[(SS, sv)],
                        inc=SV)
                    p.emit("V", lambda e: e.tensor_scalar(
                        small[:, 3:4], small[:, 2:3], 1.0 / C, 1e-5, OP.mult,
                        OP.add), waits=[(SV, p.now(SV))], inc=SV)
                    vv = p.now(SV)
                    p.emit("S", lambda e: e.activation(
                        small[:, 4:5], small[:, 3:4], AF.Sqrt),
                        waits=[(SV, vv)], inc=SS)
                    sv = p.now(SS)
                    p.emit("V", lambda e: e.reciprocal(small[:, 5:6],
                                                       small[:, 4:5]),
                           waits=[(SS, sv)], inc=SV)
                    p.emit("V", lambda e: e.tensor_tensor(
                        small[:, 6:7], small[:, 1:2], small[:, 5:6], OP.mult),
                        waits=[(SV, p.now(SV))], inc=SV)
                    vv = p.now(SV)
                    p.emit("S", lambda e, t=t, o=out_ap: e.activation(
                        o, x_sb[:, t, :], AF.Identity,
                        bias=small[:, 6:7], scale=small[:, 5:6]),
                        waits=[(SV, vv)], inc=SS)
                    return p.now(SS)

                xq_ready = [0] * 8
                for hh in range(2):
                    if dlim < 1:
                        continue
                    wv = [[0] * 8 for _ in range(14)]
                    for tl in range(8):
                        t = hh * 8 + tl
                        w0 = [(SV, PROJ_V)] if (tl == 0 and hh == 0) else ()
                        p.emit("V", lambda e, t=t, tl=tl: e.reduce_sum(
                            st1[:, 0, tl:tl + 1], x_sb[:, t, :], AX),
                            waits=w0, inc=SV)
                        p.emit("V", lambda e, t=t, tl=tl: e.tensor_reduce(
                            st1[:, 7, tl:tl + 1], x_sb[:, t, :], AX, OP.max),
                            inc=SV)
                        p.emit("V", lambda e, t=t, tl=tl: e.tensor_reduce(
                            st2[:, 0, tl:tl + 1], x_sb[:, t, :], AX, OP.min),
                            inc=SV)
                        wv[0][tl] = p.now(SV)
                    for tl in range(8):
                        p.emit("V", lambda e, tl=tl: e.tensor_scalar(
                            st1[:, 1, tl:tl + 1], st1[:, 0, tl:tl + 1],
                            -1.0 / C, None, OP.mult),
                            waits=[(SV, wv[0][tl])], inc=SV)
                        wv[1][tl] = p.now(SV)
                    for tl in range(8):
                        t = hh * 8 + tl
                        p.emit("S", lambda e, t=t, tl=tl: e.activation(
                            scrV[:], x_sb[:, t, :], AF.Square,
                            bias=st1[:, 1, tl:tl + 1], scale=f32(1.0),
                            accum_out=st1[:, 2, tl:tl + 1]),
                            waits=[(SV, wv[1][tl])], inc=SS)
                        wv[2][tl] = p.now(SS)
                    for tl in range(8):
                        p.emit("V", lambda e, tl=tl: e.tensor_scalar(
                            st1[:, 3, tl:tl + 1], st1[:, 2, tl:tl + 1],
                            1.0 / C, 1e-5, OP.mult, OP.add),
                            waits=[(SS, wv[2][tl])], inc=SV)
                        wv[3][tl] = p.now(SV)
                    for tl in range(8):
                        p.emit("S", lambda e, tl=tl: e.activation(
                            st1[:, 4, tl:tl + 1], st1[:, 3, tl:tl + 1],
                            AF.Sqrt), waits=[(SV, wv[3][tl])], inc=SS)
                        wv[4][tl] = p.now(SS)
                    for tl in range(8):
                        p.emit("V", lambda e, tl=tl: e.reciprocal(
                            st1[:, 5, tl:tl + 1], st1[:, 4, tl:tl + 1]),
                            waits=[(SS, wv[4][tl])], inc=SV)
                        wv[5][tl] = p.now(SV)
                    for tl in range(8):
                        p.emit("V", lambda e, tl=tl: e.tensor_tensor(
                            st1[:, 6, tl:tl + 1], st1[:, 1, tl:tl + 1],
                            st1[:, 5, tl:tl + 1], OP.mult),
                            waits=[(SV, wv[5][tl])], inc=SV)
                        wv[6][tl] = p.now(SV)
                    for tl in range(8):
                        p.emit("V", lambda e, tl=tl: e.tensor_tensor(
                            st2[:, 1, tl:tl + 1], st1[:, 7, tl:tl + 1],
                            st1[:, 1, tl:tl + 1], OP.add),
                            waits=[(SV, wv[1][tl])])
                        p.emit("V", lambda e, tl=tl: e.tensor_tensor(
                            st2[:, 2, tl:tl + 1], st2[:, 0, tl:tl + 1],
                            st1[:, 1, tl:tl + 1], OP.add), inc=SV)
                        wv[7][tl] = p.now(SV)
                    for tl in range(8):
                        p.emit("V", lambda e, tl=tl: e.tensor_scalar(
                            st2[:, 3, tl:tl + 1], st2[:, 2, tl:tl + 1],
                            -1.0, None, OP.mult),
                            waits=[(SV, wv[7][tl])], inc=SV)
                        wv[8][tl] = p.now(SV)
                    for tl in range(8):
                        p.emit("V", lambda e, tl=tl: e.tensor_tensor(
                            st2[:, 4, tl:tl + 1], st2[:, 1, tl:tl + 1],
                            st2[:, 3, tl:tl + 1], OP.max),
                            waits=[(SV, wv[8][tl])], inc=SV)
                        wv[9][tl] = p.now(SV)
                    for tl in range(8):
                        p.emit("V", lambda e, tl=tl: e.tensor_scalar(
                            st2[:, 5, tl:tl + 1], st2[:, 4, tl:tl + 1],
                            st1[:, 5, tl:tl + 1], 1e-5, OP.mult, OP.max),
                            waits=[(SV, wv[9][tl]), (SV, wv[5][tl])], inc=SV)
                        wv[10][tl] = p.now(SV)
                    for tl in range(8):
                        p.emit("V", lambda e, tl=tl: e.reciprocal(
                            st2[:, 6, tl:tl + 1], st2[:, 5, tl:tl + 1]),
                            waits=[(SV, wv[10][tl])], inc=SV)
                        wv[11][tl] = p.now(SV)
                    for tl in range(8):
                        p.emit("V", lambda e, tl=tl: e.tensor_scalar(
                            st2[:, 7, tl:tl + 1], st2[:, 6, tl:tl + 1],
                            127.0, None, OP.mult),
                            waits=[(SV, wv[11][tl])], inc=SV)
                        wv[12][tl] = p.now(SV)
                    for tl in range(8):
                        p.emit("V", lambda e, tl=tl: e.tensor_scalar(
                            invsx[:, tl:tl + 1], st2[:, 5, tl:tl + 1],
                            1.0 / 127.0, None, OP.mult),
                            waits=[(SV, wv[10][tl])], inc=SV)
                        wv[13][tl] = p.now(SV)
                    tvp = [0, 0]
                    for tl in range(8):
                        t = hh * 8 + tl
                        par = tl % 2
                        p.emit("S", lambda e, t=t, tl=tl: e.activation(
                            scrA[:], x_sb[:, t, :], AF.Identity,
                            bias=st1[:, 6, tl:tl + 1],
                            scale=st1[:, 5, tl:tl + 1]),
                            waits=[(SV, wv[6][tl])], inc=SS)
                        for half in range(2):
                            round_chain(scr, scrA[:, half * 512:
                                                 (half + 1) * 512],
                                        st2[:, 7, tl:tl + 1],
                                        xn_tm[:, par, half * 512:
                                              (half + 1) * 512],
                                        512,
                                        waits_first=[(SV, wv[12][tl]),
                                                     (ST, tvp[par])],
                                        slot=half, off=half * 512)
                        p.emit("V", lambda e, tl=tl, par=par: e.tensor_scalar(
                            diag2[:, par], ident[:], invsx[:, tl:tl + 1],
                            None, OP.mult),
                            waits=[(SV, wv[13][tl]), (ST, tvp[par])], inc=SV)
                        vq = p.now(SV)
                        for j in range(8):
                            bank = j // 4
                            t_mm(psB[:, bank * 512 + (j % 4) * 128:
                                     bank * 512 + (j % 4) * 128 + 128],
                                 xn_tm[:, par, j * 128:(j + 1) * 128],
                                 diag2[:, par],
                                 True, True,
                                 waits=[(SV, vq)] if j == 0 else (),
                                 inc=ST if j in (3, 7) else None)
                        tv = p.now(ST)
                        tvp[par] = tv
                        p.emit("V", lambda e, tl=tl: e.tensor_copy(
                            xqf[:, 0:4, tl * 128:(tl + 1) * 128],
                            psB[:, 0:512].rearrange("p (j c) -> p j c",
                                                    c=128)),
                            waits=[(ST, tv - 1)])
                        p.emit("V", lambda e, tl=tl: e.tensor_copy(
                            xqf[:, 4:8, tl * 128:(tl + 1) * 128],
                            psB[:, 512:1024].rearrange("p (j c) -> p j c",
                                                       c=128)),
                            waits=[(ST, tv)], inc=SV)
                        xq_ready[tl] = p.now(SV)
                    D1_V = p.now(SV)
                    if dlim < 2:
                        continue
                    if "xqf" in taps and s == 0:
                        p.emit("Y", lambda e, hh=hh: e.dma_start(
                            out=tap_d["xqf"][hh], in_=xqf[:]),
                            waits=[(SV, D1_V)], inc=DO, dma=True)

                    for oc in range(8):
                        par = oc % 2
                        guard = guards["wgu"][par]
                        p.emit("Y", lambda e, oc=oc, par=par, wgb=wgb:
                               e.dma_start(
                            out=wgb[:, par],
                            in_=wgT_d[oc].rearrange("i p c -> p i c")),
                            waits=[(ST, guard)], inc=f"wg{par}", dma=True)
                        p.emit("Y", lambda e, oc=oc, par=par, wub=wub:
                               e.dma_start(
                            out=wub[:, par],
                            in_=wuT_d[oc].rearrange("i p c -> p i c")),
                            inc=f"wu{par}", dma=True)
                        dg = [(f"wg{par}", p.now(f"wg{par}")),
                              (f"wu{par}", p.now(f"wu{par}"))]
                        for tl in range(8):
                            tpar = tl % 2
                            gps = psA[:, tpar * 512:tpar * 512 + 512]
                            ups = psA[:, 1024 + tpar * 512:
                                      1024 + tpar * 512 + 512]
                            eg = p.now(SV)
                            for i in range(NC):
                                wl = []
                                if i == 0:
                                    wl = dg + [(SV, xq_ready[tl]), (SV, eg),
                                               (DO, TAP_G3_DO)]
                                t_mm(gps, xqf[:, i, tl * 128:(tl + 1) * 128],
                                     wgb[:, par, i, :],
                                     start=(i == 0), stop=(i == NC - 1),
                                     waits=wl)
                                t_mm(ups, xqf[:, i, tl * 128:(tl + 1) * 128],
                                     wub[:, par, i, :],
                                     start=(i == 0), stop=(i == NC - 1),
                                     inc=ST if i == NC - 1 else None)
                            tv = p.now(ST)
                            p.emit("S", lambda e, gps=gps: e.activation(
                                sil[:], gps, AF.Silu,
                                bias=f32(0.0), scale=wsc[:, 3:4]),
                                waits=[(ST, tv), (SV, p.now(SV))], inc=SS)
                            sv = p.now(SS)
                            p.emit("V", lambda e, ups=ups, tl=tl, oc=oc:
                                   e.scalar_tensor_tensor(
                                gu[:, tl, oc * 512:(oc + 1) * 512], ups,
                                wsc[:, 4:5], sil[:], OP.mult, OP.mult),
                                waits=[(SS, sv)], inc=SV)
                        guards["wgu"][par] = p.now(ST)
                    D2_V = p.now(SV)
                    if dlim < 3:
                        continue
                    if "gu" in taps and s == 0:
                        p.emit("Y", lambda e, hh=hh: e.dma_start(
                            out=tap_d["gu"][hh], in_=gu[:]),
                            waits=[(SV, D2_V)], inc=DO, dma=True)
                    TAP_GU_DO = p.now(DO)

                    aw = [[0] * 8 for _ in range(6)]
                    for tl in range(8):
                        p.emit("V", lambda e, tl=tl: e.tensor_reduce(
                            st1[:, 0, tl:tl + 1], gu[:, tl, :], AX, OP.max,
                            apply_absolute_value=True),
                            waits=[(SV, D2_V), (DO, TAP_GU_DO)]
                            if tl == 0 else (), inc=SV)
                        aw[0][tl] = p.now(SV)
                    for tl in range(8):
                        p.emit("V", lambda e, tl=tl: e.tensor_scalar(
                            st1[:, 1, tl:tl + 1], st1[:, 0, tl:tl + 1],
                            1e-5, None, OP.max),
                            waits=[(SV, aw[0][tl])], inc=SV)
                        aw[1][tl] = p.now(SV)
                    for tl in range(8):
                        p.emit("V", lambda e, tl=tl: e.reciprocal(
                            st1[:, 2, tl:tl + 1], st1[:, 1, tl:tl + 1]),
                            waits=[(SV, aw[1][tl])], inc=SV)
                        aw[2][tl] = p.now(SV)
                    for tl in range(8):
                        p.emit("V", lambda e, tl=tl: e.tensor_scalar(
                            st1[:, 3, tl:tl + 1], st1[:, 2, tl:tl + 1],
                            127.0, None, OP.mult),
                            waits=[(SV, aw[2][tl])], inc=SV)
                        aw[3][tl] = p.now(SV)
                    for tl in range(8):
                        p.emit("V", lambda e, tl=tl: e.tensor_scalar(
                            st1[:, 4, tl:tl + 1], st1[:, 1, tl:tl + 1],
                            1.0 / 127.0, None, OP.mult),
                            waits=[(SV, aw[1][tl])], inc=SV)
                        aw[4][tl] = p.now(SV)
                    for tl in range(8):
                        p.emit("V", lambda e, tl=tl: e.tensor_tensor(
                            st1[:, 5, tl:tl + 1], st1[:, 4, tl:tl + 1],
                            wsc[:, 5:6], OP.mult),
                            waits=[(SV, aw[4][tl])], inc=SV)
                        aw[5][tl] = p.now(SV)
                    tvp3 = [0, 0]
                    for tl in range(8):
                        par = tl % 2
                        for cc in range(8):
                            src2 = gu[:, tl, cc * 512:(cc + 1) * 512]
                            round_chain(scr, src2, st1[:, 3, tl:tl + 1],
                                        src2, 512,
                                        waits_first=[(SV, aw[3][tl])]
                                        if cc == 0 else (), slot=cc % 2,
                                        off=(cc % 2) * 512)
                        p.emit("V", lambda e, tl=tl, par=par: e.tensor_scalar(
                            diag2[:, par], ident[:], st1[:, 5, tl:tl + 1],
                            None, OP.mult),
                            waits=[(SV, aw[5][tl]), (ST, tvp3[par])], inc=SV)
                        vq = p.now(SV)
                        for j in range(NI):
                            ps_t = psA if j < 16 else psB
                            jo = j % 16
                            t_mm(ps_t[:, jo * 128:(jo + 1) * 128],
                                 gu[:, tl, j * 128:(j + 1) * 128],
                                 diag2[:, par],
                                 True, True,
                                 waits=[(SV, vq), (SS, p.now(SS))]
                                 if j == 0 else (),
                                 inc=ST if j == NI - 1 else None)
                        tv = p.now(ST)
                        tvp3[par] = tv
                        for bkt in range(8):
                            ps_t = psA if bkt < 4 else psB
                            bo = bkt % 4
                            p.emit("V", lambda e, tl=tl, bkt=bkt, ps_t=ps_t,
                                   bo=bo: e.tensor_copy(
                                gu[:, tl, bkt * 512:(bkt + 1) * 512],
                                ps_t[:, bo * 512:(bo + 1) * 512]),
                                waits=[(ST, tv)] if bkt == 0 else (),
                                inc=SV if bkt == 7 else None)
                    D3_V = p.now(SV)
                    if dlim < 4:
                        continue
                    if "gud3" in taps and s == 0:
                        p.emit("Y", lambda e, hh=hh: e.dma_start(
                            out=tap_d["gud3"][hh], in_=gu[:]),
                            waits=[(SV, D3_V)], inc=DO, dma=True)
                    TAP_G3_DO = p.now(DO)

                    for ch2 in range(2):
                        for i in range(NI):
                            par = i % 2
                            guard = guards["wdb"][par]
                            p.emit("Y", lambda e, ch2=ch2, i=i, par=par,
                                   wdb=wdb:
                                   e.dma_start(out=wdb[:, par],
                                               in_=wdT_d[ch2, i]),
                                   waits=[(ST, guard)], inc=f"wd{par}",
                                   dma=True)
                            dg = (f"wd{par}", p.now(f"wd{par}"))
                            for tl in range(8):
                                ps = (psA if tl < 4 else psB)[
                                    :, (tl % 4) * 512:(tl % 4) * 512 + 512]
                                wl = []
                                if i == 0 and tl == 0:
                                    wl = [dg, (SV, D3_V),
                                          (SV, p.now(SV)), (DO, TAP_RES1_DO)]
                                elif tl == 0:
                                    wl = [dg]
                                t_mm(ps, gu[:, tl, i * 128:(i + 1) * 128],
                                     wdb[:, par, :],
                                     start=(i == 0), stop=(i == NI - 1),
                                     waits=wl, inc=ST if tl == 7 else None)
                            guards["wdb"][par] = p.now(ST)
                        tv = p.now(ST)
                        for tl in range(8):
                            t = hh * 8 + tl
                            ps = (psA if tl < 4 else psB)[
                                :, (tl % 4) * 512:(tl % 4) * 512 + 512]
                            p.emit("V", lambda e, t=t, ps=ps, ch2=ch2:
                                   e.tensor_tensor(
                                x_sb[:, t, ch2 * 512:(ch2 + 1) * 512],
                                x_sb[:, t, ch2 * 512:(ch2 + 1) * 512],
                                ps, OP.add),
                                waits=[(ST, tv)] if tl == 0 else (),
                                inc=SV if tl == 7 else None)
                    D4_V = p.now(SV)
                    for tl in range(8):
                        t = hh * 8 + tl
                        if skip_store:
                            continue
                        p.emit("Y", lambda e, t=t, s=s: e.dma_start(
                            out=y_d.rearrange("s (n p) c -> s n p c",
                                              p=128)[s, t],
                            in_=x_sb[:, t, :]),
                            waits=[(SV, D4_V)], inc=DO, dma=True)

        for _i in range(pad_nops):
            p.emit("T", lambda e: e.nop())
        for _i in range(pad_dmas):
            p.emit("Y", lambda e: e.dma_start(out=small[:, 0:1],
                                              in_=small[:, 1:2]),
                   inc=DY, dma=True)
        if "wqrb2" in taps:
            p.emit("Y", lambda e: e.dma_start(out=tap_d["wqrb2"], in_=wgT_d[0]),
                   inc=DO, dma=True)
        if "mark" in taps:
            p.emit("Y", lambda e: e.dma_start(out=tap_d["mark"],
                                              in_=wgT_d[7, 7, :, 0:128]),
                   inc=DO, dma=True)
        total_out = p.now(DO)
        p.emit("Y", lambda e: e.nop(), waits=[(DO, total_out)])

        with nc.Block() as blk:
            p.build(blk)

    return nc


def kernel(L0_attn_x, L0_attn_w_qkv, L0_attn_b_qkv, L0_attn_w_proj,
           L0_attn_b_proj, L0_ffn_w_gate, L0_ffn_w_up, L0_ffn_w_down):
    x = np.ascontiguousarray(np.asarray(L0_attn_x, dtype=np.float32))
    wqkv = np.ascontiguousarray(np.asarray(L0_attn_w_qkv, dtype=np.float32))
    bqkv = np.ascontiguousarray(np.asarray(L0_attn_b_qkv, dtype=np.float32))
    wproj = np.ascontiguousarray(np.asarray(L0_attn_w_proj, dtype=np.float32))
    bproj = np.ascontiguousarray(np.asarray(L0_attn_b_proj, dtype=np.float32))
    wg = np.ascontiguousarray(np.asarray(L0_ffn_w_gate, dtype=np.float32))
    wu = np.ascontiguousarray(np.asarray(L0_ffn_w_up, dtype=np.float32))
    wd = np.ascontiguousarray(np.asarray(L0_ffn_w_down, dtype=np.float32))

    nc = build_program()
    in_maps = []
    for c in range(NCORE):
        in_maps.append({
            "x": x[c * SEQ_PER_CORE:(c + 1) * SEQ_PER_CORE],
            "w_qkv": wqkv, "b_qkv": bqkv, "w_proj": wproj, "b_proj": bproj,
            "w_gate": wg, "w_up": wu, "w_down": wd,
        })
    res = run_bass_kernel_spmd(nc, in_maps, core_ids=list(range(NCORE)))
    out = np.concatenate([r["y"] for r in res.results], axis=0)
    return out.astype(np.float32)


# version 14
# speedup vs baseline: 1.0740x; 1.0023x over previous
import contextlib
from collections import defaultdict

import numpy as np
import ml_dtypes

import concourse.bass as bass
import concourse.mybir as mybir
from concourse.bass_utils import run_bass_kernel_spmd

F32 = mybir.dt.float32
BF16 = mybir.dt.bfloat16
I32 = mybir.dt.int32
AX = mybir.AxisListType.X
OP = mybir.AluOpType
AF = mybir.ActivationFunctionType

f32 = np.float32

B, T, C = 16, 2048, 1024
NCORE = 8
SEQ_PER_CORE = B // NCORE
H, D = 16, 64
INNER = 4096
NT = T // 128
NC = C // 128
NI = INNER // 128
EXP_SHIFT = -4.0


class Prog:

    def __init__(self, nc):
        self.nc = nc
        self.ops = []
        self.cnt = defaultdict(int)
        self._sem_handles = {}

    def sem(self, name):
        if name not in self._sem_handles:
            self._sem_handles[name] = self.nc.alloc_semaphore(name=name)
        return self._sem_handles[name]

    def now(self, sem):
        return self.cnt[sem]

    def emit(self, eng, fn, waits=(), inc=None, dma=False):
        incs = []
        if inc is not None:
            n = 16 if dma else 1
            self.cnt[inc] += n
            incs.append((inc, n))
        self.ops.append((eng, fn, tuple((s, v) for s, v in waits if v > 0),
                         tuple(incs)))

    def build(self, block):
        def make_runner(tag):
            def run(eng):
                watermark = defaultdict(int)
                for e, fn, waits, incs in self.ops:
                    if e != tag:
                        continue
                    for s, v in waits:
                        if v > watermark[s]:
                            eng.wait_ge(self.sem(s), v)
                            watermark[s] = v
                    ins = fn(eng)
                    for s, n in incs:
                        ins = ins.then_inc(self.sem(s), n)
            return run

        block.tensor(make_runner("T"))
        block.scalar(make_runner("S"))
        block.vector(make_runner("V"))
        block.gpsimd(make_runner("G"))
        block.sync(make_runner("Y"))


def ceil_div(a, b):
    return (a + b - 1) // b


def build_program(taps=(), debug_wq=False, preproc_only=False, drc=True, pad_nops=0, pad_dmas=0, nseq=SEQ_PER_CORE, nheads=H, do_ffn=True, skip_store=False, dlim=4):
    taps = set(taps)
    nc = bass.Bass("TRN2", target_bir_lowering=False, detect_race_conditions=drc)

    x_d = nc.dram_tensor("x", [SEQ_PER_CORE, T, C], F32, kind="ExternalInput").ap()
    y_d = nc.dram_tensor("y", [SEQ_PER_CORE, T, C], F32, kind="ExternalOutput").ap()
    wqkv_d = nc.dram_tensor("w_qkv", [C, 3 * C], F32, kind="ExternalInput").ap()
    bqkv_d = nc.dram_tensor("b_qkv", [3 * C], F32, kind="ExternalInput").ap()
    wproj_d = nc.dram_tensor("w_proj", [C, C], F32, kind="ExternalInput").ap()
    bproj_d = nc.dram_tensor("b_proj", [C], F32, kind="ExternalInput").ap()
    wg_d = nc.dram_tensor("w_gate", [INNER, C], F32, kind="ExternalInput").ap()
    wu_d = nc.dram_tensor("w_up", [INNER, C], F32, kind="ExternalInput").ap()
    wd_d = nc.dram_tensor("w_down", [C, INNER], F32, kind="ExternalInput").ap()

    wk = "ExternalInput" if debug_wq else (
        "ExternalOutput" if "wq" in taps else "Internal")
    wgT_d = nc.dram_tensor("wgT", [8, NC, 128, 512], BF16, kind=wk).ap()
    wuT_d = nc.dram_tensor("wuT", [8, NC, 128, 512], BF16, kind=wk).ap()
    wdT_d = nc.dram_tensor("wdT", [2, NI, 128, 512], BF16, kind=wk).ap()

    tap_d = {}
    def decl_tap(name, shape, dt):
        if name in taps:
            tap_d[name] = nc.dram_tensor("tap_" + name, shape, dt,
                                         kind="ExternalOutput").ap()
    decl_tap("x0", [128, NT, C], F32)
    decl_tap("xn", [128, NC, T], BF16)
    decl_tap("qf", [128, NC, T], BF16)
    decl_tap("kf", [128, NC, T], BF16)
    decl_tap("vtm", [128, NT, 16 * 65], BF16)
    decl_tap("of", [128, NC, T], BF16)
    decl_tap("res1", [128, NT, C], F32)
    decl_tap("xqf", [2, 128, NC, 1024], BF16)
    decl_tap("gu", [2, 128, 8, INNER], BF16)
    decl_tap("gud3", [2, 128, 8, INNER], BF16)
    decl_tap("wsc", [128, 8], F32)
    decl_tap("wgb0", [128, NC, 512], BF16)
    decl_tap("wqrb", [NC, 128, 512], BF16)
    decl_tap("wqrb2", [NC, 128, 512], BF16)
    decl_tap("mark", [128, 128], BF16)
    decl_tap("stg", [128, 1024], BF16)
    decl_tap("sca", [128, 1024], F32)
    decl_tap("scv", [128, 1024], F32)
    decl_tap("colsum", [128, 40], F32)

    ident_np = np.eye(128, dtype=ml_dtypes.bfloat16)
    mask_np = np.tril(np.full((128, 128), -1e9, dtype=np.float32), -1)
    ident_d = nc.inline_tensor(ident_np, name="identc").ap()
    mask_d = nc.inline_tensor(mask_np, name="maskc").ap()

    ctx = contextlib.ExitStack()
    E = ctx.enter_context

    with ctx:
        x_sb = E(nc.sbuf_tensor("x_sb", [128, NT, C], F32))
        ident = E(nc.sbuf_tensor("ident", [128, 128], BF16))
        mask = E(nc.sbuf_tensor("mask", [128, 128], F32))
        ones_row = E(nc.sbuf_tensor("ones_row", [1, 128], F32))
        ones_col = E(nc.sbuf_tensor("ones_col", [128, 1], F32))
        bqk = E(nc.sbuf_tensor("bqk", [128, 24], F32))
        diag = E(nc.sbuf_tensor("diag", [128, 128], BF16))
        small = E(nc.sbuf_tensor("small", [128, 16], F32))
        invsx = E(nc.sbuf_tensor("invsx", [128, 8], F32))
        rsumc = E(nc.sbuf_tensor("rsumc", [128, 4], F32))
        wsc = E(nc.sbuf_tensor("wsc", [128, 8], F32))
        sc11 = E(nc.sbuf_tensor("sc11", [1, 8], F32))

        psA = E(nc.psum_tensor("psA", [128, 2048], F32))
        psB = E(nc.psum_tensor("psB", [128, 2048], F32))
        psA_bf = psA[:].bitcast(BF16)
        psB_bf = psB[:].bitcast(BF16)

        p = Prog(nc)
        ST, SS, SV, SG = "sT", "sS", "sV", "sG"
        DY, DG, DO = "dY", "dG", "dO"
        DC = "dC"

        guards = {"rcclip": 0, "wstr": [0, 0], "wstrb": [0, 0], "wgu": [0, 0],
                  "wdb": [0, 0], "wps": [0, 0]}

        def t_mm(out_ap, lhsT_ap, rhs_ap, start, stop, waits=(), inc=None,
                 tile_position=None):
            p.emit("T", lambda e, o=out_ap, l=lhsT_ap, r=rhs_ap, st=start,
                   sp=stop, tp=tile_position:
                   e.matmul(o, l, r, start=st, stop=sp, tile_position=tp),
                   waits=waits, inc=inc)

        rcg = {0: 0, 1: 0}

        def round_chain(scr, src_ap, scale_ap, dst_ap, n, waits_first=(),
                        slot=None, off=0):
            scrV, scrI, scrF, scrR = scr
            guard = p.now(SV) if slot is None else rcg[slot]
            wl = list(waits_first) + [("sV", guard)]
            o1 = off + n
            p.emit("S", lambda e, off=off, o1=o1: e.activation(
                scrV[:, off:o1], src_ap, AF.Identity,
                scale=scale_ap, bias=f32(0.0)), waits=wl)
            p.emit("S", lambda e, off=off, o1=o1: e.copy(scrI[:, off:o1],
                                                         scrV[:, off:o1]))
            p.emit("S", lambda e, off=off, o1=o1: e.copy(scrR[:, off:o1],
                                                         scrI[:, off:o1]),
                   inc=SS)
            sv = p.now(SS)
            p.emit("V", lambda e, off=off, o1=o1: e.tensor_tensor(
                scrV[:, off:o1], scrV[:, off:o1], scrR[:, off:o1],
                OP.subtract), waits=[(SS, sv)])
            p.emit("V", lambda e, off=off, o1=o1: e.tensor_scalar(
                scrF[:, off:o1], scrV[:, off:o1], -1.0, 0.5,
                OP.mult, OP.is_ge))
            p.emit("V", lambda e, off=off, o1=o1: e.scalar_tensor_tensor(
                scrR[:, off:o1], scrV[:, off:o1], 0.5, scrR[:, off:o1],
                OP.is_ge, OP.add))
            p.emit("V", lambda e, d=dst_ap, off=off, o1=o1: e.tensor_tensor(
                d, scrR[:, off:o1], scrF[:, off:o1], OP.subtract), inc=SV)
            if slot is not None:
                rcg[slot] = p.now(SV)

        p.emit("Y", lambda e: e.dma_start(out=ident[:], in_=ident_d), inc=DC, dma=True)
        p.emit("Y", lambda e: e.dma_start(out=mask[:], in_=mask_d), inc=DC, dma=True)
        p.emit("Y", lambda e: e.dma_start(
            out=bqk[:], in_=bqkv_d.rearrange("(f p) -> p f", p=128),
            allow_slow_non_contiguous=True), inc=DC, dma=True)
        CONST_DY = p.now(DC)
        if "mark" in taps:
            p.emit("V", lambda e: e.memset(diag[:], 7.0), inc=SV)
            mv = p.now(SV)
            p.emit("Y", lambda e: e.dma_start(out=wgT_d[7, 7, :, 0:128],
                                              in_=diag[:]),
                   waits=[(SV, mv)], inc=DC, dma=True)
        p.emit("V", lambda e: e.memset(ones_row[:], 1.0))
        p.emit("V", lambda e: e.memset(ones_col[:], 1.0), inc=SV)
        ONES_V = p.now(SV)

        with nc.sbuf_tensor("wbuf", [128, 2, 4096], F32) as wbuf, \
             nc.sbuf_tensor("mbuf", [128, 4, 1024], F32) as mbuf, \
             nc.sbuf_tensor("colsum", [128, 100], F32) as colsum, \
             nc.sbuf_tensor("scrV0", [128, 4096], F32) as scrV, \
             nc.sbuf_tensor("scrR0", [128, 4096], F32) as scrR, \
             nc.sbuf_tensor("stag", [128, 4096], BF16) as stag, \
             nc.sbuf_tensor("stag2", [128, 2, 4096], BF16) as stag2:
            mats = [(wg_d, INNER, C, 0, wgT_d),
                    (wu_d, INNER, C, 1, wuT_d),
                    (wd_d, C, INNER, 2, wdT_d)]
            last_use = {0: 0, 1: 0}
            last_use_sem = {0: SV, 1: SV}
            WSC_S = 0

            def load_chunk(md, R, Ccols, it, par):
                eng = "Y" if par == 0 else "S"
                sem = f"wb{par}"
                if Ccols == C:
                    srcap = md.rearrange("(k p) c -> p k c", p=128)[
                        :, it * 4:(it + 1) * 4, :]
                else:
                    srcap = md[it * 128:(it + 1) * 128, :]
                p.emit(eng, lambda e, srcap=srcap, par=par, wbuf=wbuf:
                       e.dma_start(out=wbuf[:, par].rearrange(
                           "p (k c) -> p k c", c=1024) if False else
                           wbuf[:, par, :], in_=srcap),
                       waits=[(last_use_sem[par], last_use[par])],
                       inc=sem, dma=True)
                return (sem, p.now(sem))

            for it in range(8):
                par = it % 2
                dv = load_chunk(wg_d, INNER, C, it, par)
                p.emit("V", lambda e, it=it, par=par, wbuf=wbuf,
                       colsum=colsum:
                       e.tensor_reduce(colsum[:, it:it + 1],
                                       wbuf[:, par, :], AX, OP.add,
                                       apply_absolute_value=True),
                       waits=[dv], inc=SV)
                last_use[par] = p.now(SV)
                last_use_sem[par] = SV

            mitems = []
            for md2, base in ((wu_d, 32), (wd_d, 64)):
                for k in range(32):
                    mitems.append((md2, base, k))
            mb_next = [0]
            mred_v = {}
            mload_v = {}

            def emit_mean_batch():
                b = mb_next[0]
                mb_next[0] += 1
                if b > 0:
                    for k in range(4 * (b - 1), 4 * b):
                        if k >= len(mitems):
                            break
                        _, base, kk = mitems[k]
                        slot = k % 4
                        col = base + kk
                        p.emit("V", lambda e, slot=slot, col=col, mbuf=mbuf,
                               colsum=colsum:
                               e.tensor_reduce(colsum[:, col:col + 1],
                                               mbuf[:, slot], AX, OP.add,
                                               apply_absolute_value=True),
                               waits=[("mg", mload_v[k])], inc=SV)
                        mred_v[k] = p.now(SV)
                for k in range(4 * b, 4 * b + 4):
                    if k >= len(mitems):
                        break
                    md2, base, kk = mitems[k]
                    slot = k % 4
                    if base == 32:
                        src = md2[kk * 128:(kk + 1) * 128, :]
                    else:
                        rb, cb = divmod(kk, 4)
                        src = md2[rb * 128:(rb + 1) * 128,
                                  cb * 1024:(cb + 1) * 1024]
                    wl = [(SV, mred_v[k - 4])] if k >= 4 else []
                    p.emit("G", lambda e, src=src, slot=slot, mbuf=mbuf:
                           e.dma_start(out=mbuf[:, slot], in_=src),
                           waits=wl, inc="mg", dma=True)
                    mload_v[k] = p.now("mg")

            wsc_v = [0, 0, 0]

            def combine(mi, lo, nslots):
                sci = mats[mi][3]
                p.emit("V", lambda e, lo=lo, nslots=nslots, mi=mi:
                       e.tensor_reduce(
                    colsum[:, 96 + mi:97 + mi],
                    colsum[:, lo:lo + nslots], AX, OP.add),
                    waits=[(SV, p.now(SV))], inc=SV)
                gv = p.now(SV)
                t_mm(psB[0:1, 8 + mi:9 + mi], ones_col[:],
                     colsum[:, 96 + mi:97 + mi], True, True,
                     waits=[(SV, gv), (SV, ONES_V), (SS, p.now(SS))], inc=ST)
                tv = p.now(ST)
                p.emit("S", lambda e, mi=mi: e.copy(sc11[0:1, mi:mi + 1],
                                                    psB[0:1, 8 + mi:9 + mi]),
                       waits=[(ST, tv)], inc=SS)
                sv = p.now(SS)
                t_mm(psB[:, 12 + mi:13 + mi], ones_row[:],
                     sc11[0:1, mi:mi + 1], True, True,
                     waits=[(SS, sv)], inc=ST)
                tv = p.now(ST)
                p.emit("V", lambda e, sci=sci: e.tensor_scalar(
                    wsc[:, 3 + sci:4 + sci], psB[:, 12 + sci:13 + sci],
                    1.0 / (INNER * C), 1e-5, OP.mult, OP.max),
                    waits=[(ST, tv)], inc=SV)
                p.emit("V", lambda e, sci=sci: e.reciprocal(
                    wsc[:, sci:sci + 1], wsc[:, 3 + sci:4 + sci]),
                    waits=[(SV, p.now(SV))], inc=SV)
                wsc_v[mi] = p.now(SV)

            combine(0, 0, 8)
            tern_g = [0, 0]
            wr_g = [0, 0]

            for mi, (md, R, Ccols, sci, dstT) in enumerate(mats):
                nit = (R * Ccols) // (128 * 4096)
                qsc = wsc[:, sci:sci + 1]
                if debug_wq:
                    continue
                for it in range(nit):
                    if mi < 2 or it == 0:
                        emit_mean_batch()
                    if mi == 1 and it == 0:
                        combine(1, 32, 32)
                    if mi == 2 and it == 0:
                        combine(2, 64, 32)
                    par = it % 2
                    dv = load_chunk(md, R, Ccols, it, par)
                    for half in range(2):
                        o0, o1 = half * 2048, (half + 1) * 2048
                        p.emit("S", lambda e, par=par, o0=o0, o1=o1, qsc=qsc,
                               sV=scrV, wbuf=wbuf:
                               e.activation(sV[:, o0:o1],
                                            wbuf[:, par, o0:o1],
                                            AF.Identity, scale=qsc,
                                            bias=f32(0.0)),
                               waits=[dv, (SV, wsc_v[mi]),
                                      (SV, tern_g[half])], inc=SS)
                        sv = p.now(SS)
                        p.emit("V", lambda e, o0=o0, o1=o1, sV=scrV,
                               sR=scrR: e.tensor_scalar(
                            sR[:, o0:o1], sV[:, o0:o1], -0.5, None,
                            OP.is_lt), waits=[(SS, sv)])
                        p.emit("V", lambda e, o0=o0, o1=o1, sV=scrV,
                               sR=scrR, stag=stag:
                               e.scalar_tensor_tensor(
                            stag[:, o0:o1], sV[:, o0:o1], 0.5,
                            sR[:, o0:o1], OP.is_gt, OP.subtract),
                            waits=[(ST, p.now(ST))], inc=SV)
                        tern_g[half] = p.now(SV)
                    last_use[par] = p.now(SS)
                    last_use_sem[par] = SS
                    vq = p.now(SV)
                    for t32 in range(32):
                        ps_t = psA if t32 < 16 else psB
                        jo = t32 % 16
                        if Ccols == C:
                            ic, rr = divmod(t32, 4)
                            src_ap = stag[:, rr * 1024 + ic * 128:
                                          rr * 1024 + ic * 128 + 128]
                        else:
                            src_ap = stag[:, t32 * 128:(t32 + 1) * 128]
                        t_mm(ps_t[:, jo * 128:(jo + 1) * 128],
                             src_ap, ident[:],
                             True, True,
                             waits=[(SV, vq), (DC, CONST_DY),
                                    (SS, p.now(SS))] if t32 == 0 else (),
                             inc=ST if t32 == 31 else None)
                    tv = p.now(ST)
                    wpar = it % 2
                    for bkt in range(8):
                        ps_t = psA if bkt < 4 else psB
                        bo = bkt % 4
                        p.emit("S", lambda e, bkt=bkt, ps_t=ps_t, bo=bo,
                               wpar=wpar, stag2=stag2: e.copy(
                            stag2[:, wpar, bkt * 512:(bkt + 1) * 512],
                            ps_t[:, bo * 512:(bo + 1) * 512]),
                            waits=[(ST, tv), (DG, wr_g[wpar])] if bkt == 0
                            else (), inc=SS if bkt == 7 else None)
                    sv = p.now(SS)
                    if Ccols == C:
                        dst = dstT[it].rearrange("i p c -> p i c")
                        srcv = stag2[:, wpar].rearrange(
                            "p (i c) -> p i c", c=512)
                        p.emit("Y", lambda e, dst=dst, srcv=srcv:
                               e.dma_start(out=dst, in_=srcv),
                               waits=[(SS, sv)], inc=DG, dma=True)
                    else:
                        dst = dstT[it // 4, :, :,
                                   (it % 4) * 128:(it % 4) * 128 + 128]
                        srcv = stag2[:, wpar].rearrange(
                            "p (i q) -> p i q", q=128)
                        p.emit("Y", lambda e, dst=dst, srcv=srcv:
                               e.dma_start(
                            out=dst.rearrange("i p c -> p i c"), in_=srcv),
                            waits=[(SS, sv)], inc=DG, dma=True)
                    wr_g[wpar] = p.now(DG)
        PRE_G = p.now(DG)
        pre_counts = [(ST, p.now(ST)), (SS, p.now(SS)), (SV, p.now(SV)),
                      (SG, p.now(SG)), (DG, p.now(DG)), (DC, p.now(DC)),
                      ("wb0", p.now("wb0")), ("wb1", p.now("wb1")),
                      ("mg", p.now("mg"))]
        for _eng in ("T", "S", "V", "G"):
            p.emit(_eng, lambda e: e.nop(), waits=pre_counts)
        if "wqrb" in taps:
            p.emit("Y", lambda e: e.dma_start(out=tap_d["wqrb"], in_=wgT_d[0]),
                   waits=[(DG, PRE_G)], inc=DO, dma=True)

        x_reader_ss = [0] * NT
        x_issue = [0] * NT
        for s in range(0 if preproc_only else nseq):
            sqAC = contextlib.ExitStack()
            with sqAC:
                xn_f = sqAC.enter_context(
                    nc.sbuf_tensor(f"xn_f{s}", [128, NC, T], BF16))
                sqAB = contextlib.ExitStack()
                with sqAB:
                    EAB = sqAB.enter_context
                    q_f = EAB(nc.sbuf_tensor(f"q_f{s}", [128, NC, T], BF16))
                    k_f = EAB(nc.sbuf_tensor(f"k_f{s}", [128, NC, T], BF16))
                    v_tm = EAB(nc.sbuf_tensor(f"v_tm{s}",
                                               [128, NT, 16 * 65], BF16))

                    sqA = contextlib.ExitStack()
                    with sqA:
                        EA = sqA.enter_context
                        xn_tm = EA(nc.sbuf_tensor(f"xn_tm{s}", [128, 1024], BF16))
                        v_fb = EA(nc.sbuf_tensor(f"v_fb{s}", [128, 1024], BF16))
                        wstr = EA(nc.sbuf_tensor(f"wstr{s}", [128, 2, 4, 128], F32))
                        wstrb = EA(nc.sbuf_tensor(f"wstrb{s}",
                                                  [128, 2, NC, 128], BF16))

                        prev_out = p.now(DO)
                        for t in range(NT):
                            xs = f"xd{t % 4}"
                            wl = [(DO, prev_out)]
                            if s == 0 and t >= 4:
                                wl.append((SS, x_reader_ss[t - 4]))
                            elif s == 1:
                                wl.append((SS, x_reader_ss[(t - 4) % NT]))
                            p.emit("Y", lambda e, t=t, s=s: e.dma_start(
                                out=x_sb[:, t, :],
                                in_=x_d.rearrange("s (n p) c -> s n p c",
                                                  p=128)[s, t]),
                                waits=wl, inc=xs, dma=True)
                            x_issue[t] = p.now(xs)
                        if "x0" in taps and s == 0:
                            p.emit("Y", lambda e: e.dma_start(
                                out=tap_d["x0"], in_=x_sb[:]),
                                waits=[(f"xd{tt2}", x_issue[tt2])
                                       for tt2 in range(NT)],
                                inc=DO, dma=True)

                        def emit_ln(t, out_ap, waits0):
                            p.emit("V", lambda e, t=t: e.reduce_sum(
                                small[:, 0:1], x_sb[:, t, :], AX), waits=waits0,
                                inc=SV)
                            p.emit("V", lambda e: e.tensor_scalar(
                                small[:, 1:2], small[:, 0:1], -1.0 / C, None,
                                OP.mult), waits=[(SV, p.now(SV))], inc=SV)
                            vv = p.now(SV)
                            p.emit("S", lambda e, t=t: e.activation(
                                psA[:, 0:1024], x_sb[:, t, :], AF.Square,
                                bias=small[:, 1:2], scale=f32(1.0)),
                                waits=[(SV, vv)], inc=SS)
                            sv = p.now(SS)
                            p.emit("V", lambda e: e.reduce_sum(
                                small[:, 2:3], psA[:, 0:1024], AX),
                                waits=[(SS, sv)], inc=SV)
                            p.emit("V", lambda e: e.tensor_scalar(
                                small[:, 3:4], small[:, 2:3], 1.0 / C, 1e-5,
                                OP.mult, OP.add), waits=[(SV, p.now(SV))],
                                inc=SV)
                            vv = p.now(SV)
                            p.emit("S", lambda e: e.activation(
                                small[:, 4:5], small[:, 3:4], AF.Sqrt),
                                waits=[(SV, vv)], inc=SS)
                            sv = p.now(SS)
                            p.emit("V", lambda e: e.reciprocal(
                                small[:, 5:6], small[:, 4:5]), waits=[(SS, sv)],
                                inc=SV)
                            p.emit("V", lambda e: e.tensor_tensor(
                                small[:, 6:7], small[:, 1:2], small[:, 5:6],
                                OP.mult), waits=[(SV, p.now(SV))], inc=SV)
                            vv = p.now(SV)
                            p.emit("S", lambda e, t=t, o=out_ap: e.activation(
                                o, x_sb[:, t, :], AF.Identity,
                                bias=small[:, 6:7], scale=small[:, 5:6]),
                                waits=[(SV, vv)], inc=SS)
                            return p.now(SS)

                        for t in range(NT):
                            sv = emit_ln(t, xn_tm[:],
                                         [(f"xd{t % 4}", x_issue[t])])
                            x_reader_ss[t] = sv
                            for j in range(8):
                                bank = j // 4
                                t_mm(psB[:, bank * 512 + (j % 4) * 128:
                                         bank * 512 + (j % 4) * 128 + 128],
                                     xn_tm[:, j * 128:(j + 1) * 128], ident[:],
                                     True, True,
                                     waits=[(SS, sv), (SV, p.now(SV)),
                                            (DC, CONST_DY)] if j == 0 else (),
                                     inc=ST if j in (3, 7) else None)
                            tv = p.now(ST)
                            p.emit("V", lambda e, t=t: e.tensor_copy(
                                xn_f[:, 0:4, t * 128:(t + 1) * 128],
                                psB[:, 0:512].rearrange("p (j c) -> p j c",
                                                        c=128)),
                                waits=[(ST, tv - 1)])
                            p.emit("V", lambda e, t=t: e.tensor_copy(
                                xn_f[:, 4:8, t * 128:(t + 1) * 128],
                                psB[:, 512:1024].rearrange("p (j c) -> p j c",
                                                           c=128)),
                                waits=[(ST, tv)], inc=SV)
                        XN_V = p.now(SV)
                        if "xn" in taps and s == 0:
                            p.emit("Y", lambda e: e.dma_start(
                                out=tap_d["xn"], in_=xn_f[:]),
                                waits=[(SV, XN_V)], inc=DO, dma=True)

                        for fo in range(24):
                            par = fo % 2
                            guard_t = guards["wstrb"][par]
                            for hf in range(2):
                                guard_g = guards["wstr"][hf]
                                wssem = f"ws{hf}"
                                p.emit("Y", lambda e, fo=fo, hf=hf, wstr=wstr:
                                       e.dma_start(
                                    out=wstr[:, hf],
                                    in_=wqkv_d.rearrange(
                                        "(c p) (f j) -> f p c j",
                                        p=128, j=128)[fo][:, hf * 4:hf * 4 + 4]),
                                    waits=[(SG, guard_g)], inc=wssem, dma=True)
                                dv = p.now(wssem)
                                p.emit("G", lambda e, par=par, hf=hf,
                                       wstrb=wstrb, wstr=wstr: e.tensor_copy(
                                    wstrb[:, par, hf * 4:hf * 4 + 4, :],
                                    wstr[:, hf]),
                                    waits=[(wssem, dv), (DG, PRE_G),
                                           (ST, guard_t)],
                                    inc=SG)
                                guards["wstr"][hf] = p.now(SG)
                            gv = p.now(SG)
                            ps = psA[:, par * 1024:par * 1024 + 1024]
                            for th in range(2):
                                sv_guard = p.now(SS)
                                for c in range(NC):
                                    for tq in range(2):
                                        wl = []
                                        if c == 0 and tq == 0:
                                            wl = [(SG, gv), (SV, XN_V),
                                                  (SS, sv_guard)]
                                        t_mm(ps[:, tq * 512:(tq + 1) * 512],
                                             wstrb[:, par, c, :],
                                             xn_f[:, c, th * 1024 + tq * 512:
                                                  th * 1024 + (tq + 1) * 512],
                                             start=(c == 0), stop=(c == NC - 1),
                                             waits=wl,
                                             inc=ST if (c == NC - 1 and tq == 1)
                                             else None)
                                tv = p.now(ST)
                                if fo < 8:
                                    dst = q_f[:, fo, th * 1024:(th + 1) * 1024]
                                elif fo < 16:
                                    dst = k_f[:, fo - 8,
                                              th * 1024:(th + 1) * 1024]
                                else:
                                    dst = v_fb[:, 0:1024]
                                p.emit("S", lambda e, dst=dst, ps=ps, fo=fo:
                                       e.activation(
                                    dst, ps[:, 0:1024], AF.Identity,
                                    bias=bqk[:, fo:fo + 1], scale=f32(1.0)),
                                    waits=[(ST, tv)] + (
                                        [(ST, p.now(ST))] if fo >= 16 else []),
                                    inc=SS)
                                if fo >= 16:
                                    sv = p.now(SS)
                                    fv = fo - 16
                                    for tb in range(2):
                                        bank = tb % 2
                                        for j in range(4):
                                            wl = []
                                            if j == 0:
                                                wl = [(SS, sv), (SV, p.now(SV))]
                                            t_mm(psB[:, bank * 512 + j * 128:
                                                     bank * 512 + j * 128 + 128],
                                                 v_fb[:, (tb * 4 + j) * 128:
                                                      (tb * 4 + j + 1) * 128],
                                                 ident[:], True, True, waits=wl,
                                                 inc=ST if j == 3 else None)
                                        tv = p.now(ST)
                                        psv = psB[:, bank * 512:
                                                   bank * 512 + 512].rearrange(
                                            "p (j c) -> p j c", c=128)
                                        for hh in range(2):
                                            p.emit("V", lambda e, th=th, tb=tb,
                                                   fv=fv, hh=hh, psv=psv:
                                                   e.tensor_copy(
                                                v_tm[:, th * 8 + tb * 4:
                                                     th * 8 + tb * 4 + 4,
                                                     (2 * fv + hh) * 65:
                                                     (2 * fv + hh) * 65 + 64],
                                                psv[:, :, hh * 64:
                                                    hh * 64 + 64]),
                                                waits=[(ST, tv)]
                                                if hh == 0 else (),
                                                inc=SV if hh == 1 else None)
                            guards["wstrb"][par] = p.now(ST)
                        QKV_S, QKV_V = p.now(SS), p.now(SV)
                        if "qf" in taps and s == 0:
                            for nm, buf in [("qf", q_f), ("kf", k_f),
                                            ("vtm", v_tm)]:
                                if nm in tap_d:
                                    p.emit("Y", lambda e, nm=nm, buf=buf:
                                           e.dma_start(out=tap_d[nm],
                                                       in_=buf[:]),
                                           waits=[(SS, QKV_S), (SV, QKV_V)],
                                           inc=DO, dma=True)

                    O_f = xn_f
                    with nc.sbuf_tensor(f"eSt{s}", [128, 2, 2, 512], BF16) \
                            as expSt, \
                         nc.sbuf_tensor(f"rsr{s}", [1, 1024], F32) as rs_row, \
                         nc.sbuf_tensor(f"rsT{s}", [64, 1024], F32) as rsT:
                        ones_dst = v_tm[:].rearrange("p k (h c) -> p k h c",
                                                     c=65)
                        for half in range(2):
                            p.emit("S", lambda e, half=half,
                                   ones_dst=ones_dst: e.activation(
                                ones_dst[:, half * 8:half * 8 + 8, :, 64:65],
                                ident[:].rearrange("p (a b c) -> p a b c",
                                                   a=8, b=16, c=1),
                                AF.Identity, bias=f32(1.0), scale=f32(0.0)),
                                waits=[(DC, CONST_DY), (SV, p.now(SV)),
                                       (ST, p.now(ST)), (SG, p.now(SG))]
                                if half == 0 else (),
                                inc=SS if half == 1 else None)

                        def stb(b):
                            t = psA if b < 2 else psB
                            return t[:, (b % 2) * 512:(b % 2) * 512 + 512]

                        rsTile = [psB[0:64, 512:1024],
                                  psB[64:128, 512:1024]]

                        def ob_ap(b, p0, p1):
                            t = psA if b < 2 else psB
                            c0 = 1024 + (b % 2) * 512
                            return t[p0:p1, c0:c0 + 512]

                        st_ss = [0, 0, 0]
                        exp_free = [[0, 0], [0, 0]]
                        o_free = [(0, 0)] * 4
                        rep_st = [0, 0]
                        recip_sv = [0, 0]
                        kbctr = 0
                        items = [(qt, pr) for qt in range(4) for pr in range(8)]
                        epi = None
                        for idx in range(len(items) + 1):
                            cur = None
                            if idx < len(items):
                                qt, pr = items[idx]
                                kbmax = 4 * qt + 3
                                obs = (2 * (idx % 2), 2 * (idx % 2) + 1)
                                heads = (2 * pr, 2 * pr + 1)
                                def emit_attv(t2, h, pkb, ess):
                                    n0 = max(0, pkb - 4 * qt) * 128
                                    awl = [(SS, ess)]
                                    if pkb == 0:
                                        awl += [(SV, o_free[obs[t2]][0]),
                                                (SS, o_free[obs[t2]][1])]
                                    t_mm(ob_ap(obs[t2], 0, 65)[:, n0:512],
                                         v_tm[:, pkb, 65 * h:65 * h + 65],
                                         expSt[:, t2, pkb % 2, n0:512],
                                         start=(pkb == 0),
                                         stop=(pkb == kbmax),
                                         waits=awl, inc=ST)
                                    exp_free[t2][pkb % 2] = p.now(ST)

                                pend = None
                                for kb in range(kbmax + 1):
                                    j = kb - 4 * qt
                                    n0 = max(0, j) * 128
                                    this_kb = []
                                    for t2, h in enumerate(heads):
                                        hc, hb = h // 2, (h % 2) * 64
                                        sb_ = kbctr % 3
                                        kbctr += 1
                                        wl = [(SS, st_ss[sb_])]
                                        if idx == 0 and kb == 0 and t2 == 0:
                                            wl += [(SS, QKV_S), (SV, QKV_V)]
                                        t_mm(stb(sb_)[:, n0:512],
                                             k_f[hb:hb + 64, hc,
                                                 kb * 128:(kb + 1) * 128],
                                             q_f[hb:hb + 64, hc,
                                                 qt * 512 + n0:(qt + 1) * 512],
                                             True, True, waits=wl, inc=ST)
                                        stv = p.now(ST)
                                        ewl = [(ST, stv),
                                               (ST, exp_free[t2][kb % 2])]
                                        if j >= 0:
                                            p.emit("V", lambda e, sb_=sb_,
                                                   n0=n0: e.tensor_tensor(
                                                stb(sb_)[:, n0:n0 + 128],
                                                stb(sb_)[:, n0:n0 + 128],
                                                mask[:], OP.add),
                                                waits=[(ST, stv),
                                                       (DC, CONST_DY)],
                                                inc=SV)
                                            ewl.append((SV, p.now(SV)))
                                        p.emit("S", lambda e, sb_=sb_, n0=n0,
                                               kb=kb, t2=t2,
                                               expSt=expSt: e.activation(
                                            expSt[:, t2, kb % 2, n0:512],
                                            stb(sb_)[:, n0:512],
                                            AF.Exp, bias=f32(EXP_SHIFT),
                                            scale=f32(0.125)),
                                            waits=ewl, inc=SS)
                                        st_ss[sb_] = p.now(SS)
                                        this_kb.append((t2, h, kb,
                                                        p.now(SS)))
                                    if pend:
                                        for args in pend:
                                            emit_attv(*args)
                                    pend = this_kb
                                for args in pend:
                                    emit_attv(*args)
                                cur = (qt, heads, obs, p.now(ST))
                            if epi is not None:
                                eqt, eheads, eobs, etv = epi
                                rs_ss2 = [0, 0]
                                for t2 in range(2):
                                    p.emit("S", lambda e, eb=eobs[t2], t2=t2,
                                           rs_row=rs_row: e.copy(
                                        rs_row[0:1, t2 * 512:t2 * 512 + 512],
                                        ob_ap(eb, 64, 65)),
                                        waits=[(ST, etv), (ST, rep_st[t2])],
                                        inc=SS)
                                    rs_ss2[t2] = p.now(SS)
                                for t2 in range(2):
                                    t_mm(rsTile[t2], ones_row[0:1, 0:64],
                                         rs_row[0:1, t2 * 512:t2 * 512 + 512],
                                         True, True,
                                         waits=[(SS, rs_ss2[t2]),
                                                (SV, recip_sv[t2])],
                                         inc=ST,
                                         tile_position=(0, 64 * t2))
                                    rep_st[t2] = p.now(ST)
                                for t2 in range(2):
                                    p.emit("V", lambda e, t2=t2, rsT=rsT:
                                           e.reciprocal(
                                        rsT[0:64, t2 * 512:t2 * 512 + 512],
                                        rsTile[t2]),
                                        waits=[(ST, rep_st[t2])], inc=SV)
                                    recip_sv[t2] = p.now(SV)
                                for t2, eh in enumerate(eheads):
                                    ehc, ehb = eh // 2, (eh % 2) * 64
                                    eb = eobs[t2]
                                    p.emit("V", lambda e, ehb=ehb, ehc=ehc,
                                           eqt=eqt, eb=eb, t2=t2, rsT=rsT:
                                           e.tensor_tensor(
                                        O_f[ehb:ehb + 64, ehc,
                                            eqt * 512:(eqt + 1) * 512],
                                        ob_ap(eb, 0, 64),
                                        rsT[0:64, t2 * 512:t2 * 512 + 512],
                                        OP.mult),
                                        waits=[(SV, recip_sv[t2])], inc=SV)
                                    o_free[eb] = (p.now(SV), rs_ss2[t2])
                            epi = cur
                    ATT_S = p.now(SS)
                    if "of" in taps and s == 0:
                        p.emit("Y", lambda e: e.dma_start(
                            out=tap_d["of"], in_=O_f[:]),
                            waits=[(SS, ATT_S), (SV, p.now(SV))],
                            inc=DO, dma=True)

                with nc.sbuf_tensor(f"wpj{s}", [128, NC, C], BF16) as wpj, \
                     nc.sbuf_tensor(f"wps{s}", [128, 2, C], F32) as wps, \
                     nc.sbuf_tensor(f"bpr{s}", [1, C], F32) as bproj_row:
                    p.emit("Y", lambda e, bproj_row=bproj_row: e.dma_start(
                        out=bproj_row[:],
                        in_=bproj_d.unsqueeze(0)),
                        waits=[(SS, ATT_S)], inc="bp", dma=True)
                    BPR_DY = p.now("bp")
                    for c in range(NC):
                        par = c % 2
                        guard = guards["wps"][par]
                        wpsem = f"wp{par}"
                        p.emit("Y", lambda e, c=c, par=par, wps=wps: e.dma_start(
                            out=wps[:, par, :],
                            in_=wproj_d[c * 128:(c + 1) * 128, :]),
                            waits=[(SG, guard), (SS, ATT_S)], inc=wpsem,
                            dma=True)
                        dv = p.now(wpsem)
                        p.emit("G", lambda e, c=c, par=par, wpj=wpj, wps=wps:
                               e.tensor_copy(
                            wpj[:, c, :], wps[:, par, :]),
                            waits=[(wpsem, dv)], inc=SG)
                        guards["wps"][par] = p.now(SG)
                    WPJ_G = p.now(SG)
                    for t in range(NT):
                        par = t % 2
                        ps = psA[:, par * 1024:par * 1024 + 1024]
                        vres = p.now(SV)
                        for c in range(NC):
                            for half in range(2):
                                wl = []
                                if c == 0 and half == 0:
                                    wl = [(SG, WPJ_G), (SS, ATT_S), (SV, vres),
                                          ("bp", BPR_DY)]
                                t_mm(ps[:, half * 512:(half + 1) * 512],
                                     O_f[:, c, t * 128:(t + 1) * 128],
                                     wpj[:, c, half * 512:(half + 1) * 512],
                                     start=(c == 0), stop=False, waits=wl)
                        for half in range(2):
                            t_mm(ps[:, half * 512:(half + 1) * 512],
                                 ones_row[:, 0:128],
                                 bproj_row[0:1, half * 512:(half + 1) * 512],
                                 start=False, stop=True,
                                 inc=ST if half == 1 else None)
                        tv = p.now(ST)
                        p.emit("V", lambda e, t=t, ps=ps: e.tensor_tensor(
                            x_sb[:, t, :], x_sb[:, t, :], ps[:, 0:1024], OP.add),
                            waits=[(ST, tv)], inc=SV)
                PROJ_V = p.now(SV)
                if "res1" in taps and s == 0:
                    p.emit("Y", lambda e: e.dma_start(
                        out=tap_d["res1"], in_=x_sb[:]),
                        waits=[(SV, PROJ_V)], inc=DO, dma=True)
            TAP_RES1_DO = p.now(DO)

            if not do_ffn:
                continue
            sqD = contextlib.ExitStack()
            with sqD:
                ED = sqD.enter_context
                scrV = ED(nc.sbuf_tensor(f"scrVd{s}", [128, 1024], F32))
                scrI = ED(nc.sbuf_tensor(f"scrId{s}", [128, 1024], I32))
                scrR = ED(nc.sbuf_tensor(f"scrRd{s}", [128, 1024], F32))
                scrA = ED(nc.sbuf_tensor(f"scrAd{s}", [128, 1024], F32))
                sil = ED(nc.sbuf_tensor(f"sild{s}", [128, 512], F32))
                xn_tm = ED(nc.sbuf_tensor(f"xntmd{s}", [128, 2, 1024], BF16))
                st1 = ED(nc.sbuf_tensor(f"st1d{s}", [128, 8, 8], F32))
                st2 = ED(nc.sbuf_tensor(f"st2d{s}", [128, 8, 8], F32))
                diag2 = ED(nc.sbuf_tensor(f"dg2d{s}", [128, 2, 128], BF16))
                xqf = ED(nc.sbuf_tensor(f"xqf{s}", [128, NC, 1024], BF16))
                gu = ED(nc.sbuf_tensor(f"gu{s}", [128, 8, INNER], BF16))
                wgb = ED(nc.sbuf_tensor(f"wgb{s}", [128, 2, NC, 512], BF16))
                wub = ED(nc.sbuf_tensor(f"wub{s}", [128, 2, NC, 512], BF16))
                wdb = ED(nc.sbuf_tensor(f"wdb{s}", [128, 2, 512], BF16))
                scrF = scrI[:].bitcast(F32)
                scr = (scrV, scrI, scrF, scrR)

                TAP_G3_DO = p.now(DO)

                def emit_ln2(t, out_ap, waits0):
                    p.emit("V", lambda e, t=t: e.reduce_sum(
                        small[:, 0:1], x_sb[:, t, :], AX), waits=waits0,
                        inc=SV)
                    p.emit("V", lambda e: e.tensor_scalar(
                        small[:, 1:2], small[:, 0:1], -1.0 / C, None, OP.mult),
                        waits=[(SV, p.now(SV))], inc=SV)
                    vv = p.now(SV)
                    p.emit("S", lambda e, t=t: e.activation(
                        psA[:, 0:1024], x_sb[:, t, :], AF.Square,
                        bias=small[:, 1:2], scale=f32(1.0)),
                        waits=[(SV, vv)], inc=SS)
                    sv = p.now(SS)
                    p.emit("V", lambda e: e.reduce_sum(
                        small[:, 2:3], psA[:, 0:1024], AX), waits=[(SS, sv)],
                        inc=SV)
                    p.emit("V", lambda e: e.tensor_scalar(
                        small[:, 3:4], small[:, 2:3], 1.0 / C, 1e-5, OP.mult,
                        OP.add), waits=[(SV, p.now(SV))], inc=SV)
                    vv = p.now(SV)
                    p.emit("S", lambda e: e.activation(
                        small[:, 4:5], small[:, 3:4], AF.Sqrt),
                        waits=[(SV, vv)], inc=SS)
                    sv = p.now(SS)
                    p.emit("V", lambda e: e.reciprocal(small[:, 5:6],
                                                       small[:, 4:5]),
                           waits=[(SS, sv)], inc=SV)
                    p.emit("V", lambda e: e.tensor_tensor(
                        small[:, 6:7], small[:, 1:2], small[:, 5:6], OP.mult),
                        waits=[(SV, p.now(SV))], inc=SV)
                    vv = p.now(SV)
                    p.emit("S", lambda e, t=t, o=out_ap: e.activation(
                        o, x_sb[:, t, :], AF.Identity,
                        bias=small[:, 6:7], scale=small[:, 5:6]),
                        waits=[(SV, vv)], inc=SS)
                    return p.now(SS)

                xq_ready = [0] * 8
                for hh in range(2):
                    if dlim < 1:
                        continue
                    wv = [[0] * 8 for _ in range(14)]
                    for tl in range(8):
                        t = hh * 8 + tl
                        w0 = [(SV, PROJ_V)] if (tl == 0 and hh == 0) else ()
                        p.emit("V", lambda e, t=t, tl=tl: e.reduce_sum(
                            st1[:, 0, tl:tl + 1], x_sb[:, t, :], AX),
                            waits=w0, inc=SV)
                        p.emit("V", lambda e, t=t, tl=tl: e.tensor_reduce(
                            st1[:, 7, tl:tl + 1], x_sb[:, t, :], AX, OP.max),
                            inc=SV)
                        p.emit("V", lambda e, t=t, tl=tl: e.tensor_reduce(
                            st2[:, 0, tl:tl + 1], x_sb[:, t, :], AX, OP.min),
                            inc=SV)
                        wv[0][tl] = p.now(SV)
                    for tl in range(8):
                        p.emit("V", lambda e, tl=tl: e.tensor_scalar(
                            st1[:, 1, tl:tl + 1], st1[:, 0, tl:tl + 1],
                            -1.0 / C, None, OP.mult),
                            waits=[(SV, wv[0][tl])], inc=SV)
                        wv[1][tl] = p.now(SV)
                    for tl in range(8):
                        t = hh * 8 + tl
                        p.emit("S", lambda e, t=t, tl=tl: e.activation(
                            scrV[:], x_sb[:, t, :], AF.Square,
                            bias=st1[:, 1, tl:tl + 1], scale=f32(1.0),
                            accum_out=st1[:, 2, tl:tl + 1]),
                            waits=[(SV, wv[1][tl])], inc=SS)
                        wv[2][tl] = p.now(SS)
                    for tl in range(8):
                        p.emit("V", lambda e, tl=tl: e.tensor_scalar(
                            st1[:, 3, tl:tl + 1], st1[:, 2, tl:tl + 1],
                            1.0 / C, 1e-5, OP.mult, OP.add),
                            waits=[(SS, wv[2][tl])], inc=SV)
                        wv[3][tl] = p.now(SV)
                    for tl in range(8):
                        p.emit("S", lambda e, tl=tl: e.activation(
                            st1[:, 4, tl:tl + 1], st1[:, 3, tl:tl + 1],
                            AF.Sqrt), waits=[(SV, wv[3][tl])], inc=SS)
                        wv[4][tl] = p.now(SS)
                    for tl in range(8):
                        p.emit("V", lambda e, tl=tl: e.reciprocal(
                            st1[:, 5, tl:tl + 1], st1[:, 4, tl:tl + 1]),
                            waits=[(SS, wv[4][tl])], inc=SV)
                        wv[5][tl] = p.now(SV)
                    for tl in range(8):
                        p.emit("V", lambda e, tl=tl: e.tensor_tensor(
                            st1[:, 6, tl:tl + 1], st1[:, 1, tl:tl + 1],
                            st1[:, 5, tl:tl + 1], OP.mult),
                            waits=[(SV, wv[5][tl])], inc=SV)
                        wv[6][tl] = p.now(SV)
                    for tl in range(8):
                        p.emit("V", lambda e, tl=tl: e.tensor_tensor(
                            st2[:, 1, tl:tl + 1], st1[:, 7, tl:tl + 1],
                            st1[:, 1, tl:tl + 1], OP.add),
                            waits=[(SV, wv[1][tl])])
                        p.emit("V", lambda e, tl=tl: e.tensor_tensor(
                            st2[:, 2, tl:tl + 1], st2[:, 0, tl:tl + 1],
                            st1[:, 1, tl:tl + 1], OP.add), inc=SV)
                        wv[7][tl] = p.now(SV)
                    for tl in range(8):
                        p.emit("V", lambda e, tl=tl: e.tensor_scalar(
                            st2[:, 3, tl:tl + 1], st2[:, 2, tl:tl + 1],
                            -1.0, None, OP.mult),
                            waits=[(SV, wv[7][tl])], inc=SV)
                        wv[8][tl] = p.now(SV)
                    for tl in range(8):
                        p.emit("V", lambda e, tl=tl: e.tensor_tensor(
                            st2[:, 4, tl:tl + 1], st2[:, 1, tl:tl + 1],
                            st2[:, 3, tl:tl + 1], OP.max),
                            waits=[(SV, wv[8][tl])], inc=SV)
                        wv[9][tl] = p.now(SV)
                    for tl in range(8):
                        p.emit("V", lambda e, tl=tl: e.tensor_scalar(
                            st2[:, 5, tl:tl + 1], st2[:, 4, tl:tl + 1],
                            st1[:, 5, tl:tl + 1], 1e-5, OP.mult, OP.max),
                            waits=[(SV, wv[9][tl]), (SV, wv[5][tl])], inc=SV)
                        wv[10][tl] = p.now(SV)
                    for tl in range(8):
                        p.emit("V", lambda e, tl=tl: e.reciprocal(
                            st2[:, 6, tl:tl + 1], st2[:, 5, tl:tl + 1]),
                            waits=[(SV, wv[10][tl])], inc=SV)
                        wv[11][tl] = p.now(SV)
                    for tl in range(8):
                        p.emit("V", lambda e, tl=tl: e.tensor_scalar(
                            st2[:, 7, tl:tl + 1], st2[:, 6, tl:tl + 1],
                            127.0, None, OP.mult),
                            waits=[(SV, wv[11][tl])], inc=SV)
                        wv[12][tl] = p.now(SV)
                    for tl in range(8):
                        p.emit("V", lambda e, tl=tl: e.tensor_scalar(
                            invsx[:, tl:tl + 1], st2[:, 5, tl:tl + 1],
                            1.0 / 127.0, None, OP.mult),
                            waits=[(SV, wv[10][tl])], inc=SV)
                        wv[13][tl] = p.now(SV)
                    tvp = [0, 0]
                    for tl in range(8):
                        t = hh * 8 + tl
                        par = tl % 2
                        p.emit("S", lambda e, t=t, tl=tl: e.activation(
                            scrA[:], x_sb[:, t, :], AF.Identity,
                            bias=st1[:, 6, tl:tl + 1],
                            scale=st1[:, 5, tl:tl + 1]),
                            waits=[(SV, wv[6][tl])], inc=SS)
                        for half in range(2):
                            round_chain(scr, scrA[:, half * 512:
                                                 (half + 1) * 512],
                                        st2[:, 7, tl:tl + 1],
                                        xn_tm[:, par, half * 512:
                                              (half + 1) * 512],
                                        512,
                                        waits_first=[(SV, wv[12][tl]),
                                                     (ST, tvp[par])],
                                        slot=half, off=half * 512)
                        p.emit("V", lambda e, tl=tl, par=par: e.tensor_scalar(
                            diag2[:, par], ident[:], invsx[:, tl:tl + 1],
                            None, OP.mult),
                            waits=[(SV, wv[13][tl]), (ST, tvp[par])], inc=SV)
                        vq = p.now(SV)
                        for j in range(8):
                            bank = j // 4
                            t_mm(psB[:, bank * 512 + (j % 4) * 128:
                                     bank * 512 + (j % 4) * 128 + 128],
                                 xn_tm[:, par, j * 128:(j + 1) * 128],
                                 diag2[:, par],
                                 True, True,
                                 waits=[(SV, vq)] if j == 0 else (),
                                 inc=ST if j in (3, 7) else None)
                        tv = p.now(ST)
                        tvp[par] = tv
                        p.emit("V", lambda e, tl=tl: e.tensor_copy(
                            xqf[:, 0:4, tl * 128:(tl + 1) * 128],
                            psB[:, 0:512].rearrange("p (j c) -> p j c",
                                                    c=128)),
                            waits=[(ST, tv - 1)])
                        p.emit("V", lambda e, tl=tl: e.tensor_copy(
                            xqf[:, 4:8, tl * 128:(tl + 1) * 128],
                            psB[:, 512:1024].rearrange("p (j c) -> p j c",
                                                       c=128)),
                            waits=[(ST, tv)], inc=SV)
                        xq_ready[tl] = p.now(SV)
                    D1_V = p.now(SV)
                    if dlim < 2:
                        continue
                    if "xqf" in taps and s == 0:
                        p.emit("Y", lambda e, hh=hh: e.dma_start(
                            out=tap_d["xqf"][hh], in_=xqf[:]),
                            waits=[(SV, D1_V)], inc=DO, dma=True)

                    for oc in range(8):
                        par = oc % 2
                        guard = guards["wgu"][par]
                        p.emit("Y", lambda e, oc=oc, par=par, wgb=wgb:
                               e.dma_start(
                            out=wgb[:, par],
                            in_=wgT_d[oc].rearrange("i p c -> p i c")),
                            waits=[(ST, guard)], inc=f"wg{par}", dma=True)
                        p.emit("Y", lambda e, oc=oc, par=par, wub=wub:
                               e.dma_start(
                            out=wub[:, par],
                            in_=wuT_d[oc].rearrange("i p c -> p i c")),
                            inc=f"wu{par}", dma=True)
                        dg = [(f"wg{par}", p.now(f"wg{par}")),
                              (f"wu{par}", p.now(f"wu{par}"))]
                        for tl in range(8):
                            tpar = tl % 2
                            gps = psA[:, tpar * 512:tpar * 512 + 512]
                            ups = psA[:, 1024 + tpar * 512:
                                      1024 + tpar * 512 + 512]
                            eg = p.now(SV)
                            for i in range(NC):
                                wl = []
                                if i == 0:
                                    wl = dg + [(SV, xq_ready[tl]), (SV, eg),
                                               (DO, TAP_G3_DO)]
                                t_mm(gps, xqf[:, i, tl * 128:(tl + 1) * 128],
                                     wgb[:, par, i, :],
                                     start=(i == 0), stop=(i == NC - 1),
                                     waits=wl)
                                t_mm(ups, xqf[:, i, tl * 128:(tl + 1) * 128],
                                     wub[:, par, i, :],
                                     start=(i == 0), stop=(i == NC - 1),
                                     inc=ST if i == NC - 1 else None)
                            tv = p.now(ST)
                            p.emit("S", lambda e, gps=gps: e.activation(
                                sil[:], gps, AF.Silu,
                                bias=f32(0.0), scale=wsc[:, 3:4]),
                                waits=[(ST, tv), (SV, p.now(SV))], inc=SS)
                            sv = p.now(SS)
                            p.emit("V", lambda e, ups=ups, tl=tl, oc=oc:
                                   e.scalar_tensor_tensor(
                                gu[:, tl, oc * 512:(oc + 1) * 512], ups,
                                wsc[:, 4:5], sil[:], OP.mult, OP.mult),
                                waits=[(SS, sv)], inc=SV)
                        guards["wgu"][par] = p.now(ST)
                    D2_V = p.now(SV)
                    if dlim < 3:
                        continue
                    if "gu" in taps and s == 0:
                        p.emit("Y", lambda e, hh=hh: e.dma_start(
                            out=tap_d["gu"][hh], in_=gu[:]),
                            waits=[(SV, D2_V)], inc=DO, dma=True)
                    TAP_GU_DO = p.now(DO)

                    aw = [[0] * 8 for _ in range(6)]
                    for tl in range(8):
                        p.emit("V", lambda e, tl=tl: e.tensor_reduce(
                            st1[:, 0, tl:tl + 1], gu[:, tl, :], AX, OP.max,
                            apply_absolute_value=True),
                            waits=[(SV, D2_V), (DO, TAP_GU_DO)]
                            if tl == 0 else (), inc=SV)
                        aw[0][tl] = p.now(SV)
                    for tl in range(8):
                        p.emit("V", lambda e, tl=tl: e.tensor_scalar(
                            st1[:, 1, tl:tl + 1], st1[:, 0, tl:tl + 1],
                            1e-5, None, OP.max),
                            waits=[(SV, aw[0][tl])], inc=SV)
                        aw[1][tl] = p.now(SV)
                    for tl in range(8):
                        p.emit("V", lambda e, tl=tl: e.reciprocal(
                            st1[:, 2, tl:tl + 1], st1[:, 1, tl:tl + 1]),
                            waits=[(SV, aw[1][tl])], inc=SV)
                        aw[2][tl] = p.now(SV)
                    for tl in range(8):
                        p.emit("V", lambda e, tl=tl: e.tensor_scalar(
                            st1[:, 3, tl:tl + 1], st1[:, 2, tl:tl + 1],
                            127.0, None, OP.mult),
                            waits=[(SV, aw[2][tl])], inc=SV)
                        aw[3][tl] = p.now(SV)
                    for tl in range(8):
                        p.emit("V", lambda e, tl=tl: e.tensor_scalar(
                            st1[:, 4, tl:tl + 1], st1[:, 1, tl:tl + 1],
                            1.0 / 127.0, None, OP.mult),
                            waits=[(SV, aw[1][tl])], inc=SV)
                        aw[4][tl] = p.now(SV)
                    for tl in range(8):
                        p.emit("V", lambda e, tl=tl: e.tensor_tensor(
                            st1[:, 5, tl:tl + 1], st1[:, 4, tl:tl + 1],
                            wsc[:, 5:6], OP.mult),
                            waits=[(SV, aw[4][tl])], inc=SV)
                        aw[5][tl] = p.now(SV)
                    tvp3 = [0, 0]
                    for tl in range(8):
                        par = tl % 2
                        for cc in range(8):
                            src2 = gu[:, tl, cc * 512:(cc + 1) * 512]
                            round_chain(scr, src2, st1[:, 3, tl:tl + 1],
                                        src2, 512,
                                        waits_first=[(SV, aw[3][tl])]
                                        if cc == 0 else (), slot=cc % 2,
                                        off=(cc % 2) * 512)
                        p.emit("V", lambda e, tl=tl, par=par: e.tensor_scalar(
                            diag2[:, par], ident[:], st1[:, 5, tl:tl + 1],
                            None, OP.mult),
                            waits=[(SV, aw[5][tl]), (ST, tvp3[par])], inc=SV)
                        vq = p.now(SV)
                        for j in range(NI):
                            ps_t = psA if j < 16 else psB
                            jo = j % 16
                            t_mm(ps_t[:, jo * 128:(jo + 1) * 128],
                                 gu[:, tl, j * 128:(j + 1) * 128],
                                 diag2[:, par],
                                 True, True,
                                 waits=[(SV, vq), (SS, p.now(SS))]
                                 if j == 0 else (),
                                 inc=ST if j == NI - 1 else None)
                        tv = p.now(ST)
                        tvp3[par] = tv
                        for bkt in range(8):
                            ps_t = psA if bkt < 4 else psB
                            bo = bkt % 4
                            p.emit("V", lambda e, tl=tl, bkt=bkt, ps_t=ps_t,
                                   bo=bo: e.tensor_copy(
                                gu[:, tl, bkt * 512:(bkt + 1) * 512],
                                ps_t[:, bo * 512:(bo + 1) * 512]),
                                waits=[(ST, tv)] if bkt == 0 else (),
                                inc=SV if bkt == 7 else None)
                    D3_V = p.now(SV)
                    if dlim < 4:
                        continue
                    if "gud3" in taps and s == 0:
                        p.emit("Y", lambda e, hh=hh: e.dma_start(
                            out=tap_d["gud3"][hh], in_=gu[:]),
                            waits=[(SV, D3_V)], inc=DO, dma=True)
                    TAP_G3_DO = p.now(DO)

                    for ch2 in range(2):
                        for i in range(NI):
                            par = i % 2
                            guard = guards["wdb"][par]
                            p.emit("Y", lambda e, ch2=ch2, i=i, par=par,
                                   wdb=wdb:
                                   e.dma_start(out=wdb[:, par],
                                               in_=wdT_d[ch2, i]),
                                   waits=[(ST, guard)], inc=f"wd{par}",
                                   dma=True)
                            dg = (f"wd{par}", p.now(f"wd{par}"))
                            for tl in range(8):
                                ps = (psA if tl < 4 else psB)[
                                    :, (tl % 4) * 512:(tl % 4) * 512 + 512]
                                wl = []
                                if i == 0 and tl == 0:
                                    wl = [dg, (SV, D3_V),
                                          (SV, p.now(SV)), (DO, TAP_RES1_DO)]
                                elif tl == 0:
                                    wl = [dg]
                                t_mm(ps, gu[:, tl, i * 128:(i + 1) * 128],
                                     wdb[:, par, :],
                                     start=(i == 0), stop=(i == NI - 1),
                                     waits=wl, inc=ST if tl == 7 else None)
                            guards["wdb"][par] = p.now(ST)
                        tv = p.now(ST)
                        for tl in range(8):
                            t = hh * 8 + tl
                            ps = (psA if tl < 4 else psB)[
                                :, (tl % 4) * 512:(tl % 4) * 512 + 512]
                            p.emit("V", lambda e, t=t, ps=ps, ch2=ch2:
                                   e.tensor_tensor(
                                x_sb[:, t, ch2 * 512:(ch2 + 1) * 512],
                                x_sb[:, t, ch2 * 512:(ch2 + 1) * 512],
                                ps, OP.add),
                                waits=[(ST, tv)] if tl == 0 else (),
                                inc=SV if tl == 7 else None)
                    D4_V = p.now(SV)
                    for tl in range(8):
                        t = hh * 8 + tl
                        if skip_store:
                            continue
                        p.emit("Y", lambda e, t=t, s=s: e.dma_start(
                            out=y_d.rearrange("s (n p) c -> s n p c",
                                              p=128)[s, t],
                            in_=x_sb[:, t, :]),
                            waits=[(SV, D4_V)], inc=DO, dma=True)

        for _i in range(pad_nops):
            p.emit("T", lambda e: e.nop())
        for _i in range(pad_dmas):
            p.emit("Y", lambda e: e.dma_start(out=small[:, 0:1],
                                              in_=small[:, 1:2]),
                   inc=DY, dma=True)
        if "wqrb2" in taps:
            p.emit("Y", lambda e: e.dma_start(out=tap_d["wqrb2"], in_=wgT_d[0]),
                   inc=DO, dma=True)
        if "mark" in taps:
            p.emit("Y", lambda e: e.dma_start(out=tap_d["mark"],
                                              in_=wgT_d[7, 7, :, 0:128]),
                   inc=DO, dma=True)
        total_out = p.now(DO)
        p.emit("Y", lambda e: e.nop(), waits=[(DO, total_out)])

        with nc.Block() as blk:
            p.build(blk)

    return nc


def kernel(L0_attn_x, L0_attn_w_qkv, L0_attn_b_qkv, L0_attn_w_proj,
           L0_attn_b_proj, L0_ffn_w_gate, L0_ffn_w_up, L0_ffn_w_down):
    x = np.ascontiguousarray(np.asarray(L0_attn_x, dtype=np.float32))
    wqkv = np.ascontiguousarray(np.asarray(L0_attn_w_qkv, dtype=np.float32))
    bqkv = np.ascontiguousarray(np.asarray(L0_attn_b_qkv, dtype=np.float32))
    wproj = np.ascontiguousarray(np.asarray(L0_attn_w_proj, dtype=np.float32))
    bproj = np.ascontiguousarray(np.asarray(L0_attn_b_proj, dtype=np.float32))
    wg = np.ascontiguousarray(np.asarray(L0_ffn_w_gate, dtype=np.float32))
    wu = np.ascontiguousarray(np.asarray(L0_ffn_w_up, dtype=np.float32))
    wd = np.ascontiguousarray(np.asarray(L0_ffn_w_down, dtype=np.float32))

    nc = build_program()
    in_maps = []
    for c in range(NCORE):
        in_maps.append({
            "x": x[c * SEQ_PER_CORE:(c + 1) * SEQ_PER_CORE],
            "w_qkv": wqkv, "b_qkv": bqkv, "w_proj": wproj, "b_proj": bproj,
            "w_gate": wg, "w_up": wu, "w_down": wd,
        })
    res = run_bass_kernel_spmd(nc, in_maps, core_ids=list(range(NCORE)))
    out = np.concatenate([r["y"] for r in res.results], axis=0)
    return out.astype(np.float32)
